# version 04
# baseline (speedup 1.0000x reference)
import numpy as np
import concourse.bass as bass
import concourse.mybir as mybir
from concourse.bass_utils import run_bass_kernel_spmd

F32 = mybir.dt.float32
BF16 = mybir.dt.bfloat16
AF = mybir.ActivationFunctionType
ALU = mybir.AluOpType
AX = mybir.AxisListType

NCORE = 8
NE = 4
TOK = 2048
DM = 1024
NH = 16
NK = 8
NSTR = 4
SW = 512
NC = 16
SCALE = 0.125
DFF = 4096
RANK = 256
_N_SUB = 4
_DBG = False
_BARR = 0
import os as _os
_BARR = int(_os.environ.get('KBARR', '0'))
_DBG2 = int(_os.environ.get('KDBG2', '0'))
_KPIPE = int(_os.environ.get('KPIPE', '2'))
_DBG3 = int(_os.environ.get('KDBG3', '0'))
_DBG4 = int(_os.environ.get('KDBG4', '0'))

BQKV_Q, BQKV_K, BQ, BK, BL, BO, BOUT, B2, B1, BV = 0, 8, 16, 24, 32, 40, 48, 56, 64, 96


class Sem:
    def __init__(self, h):
        self.h = h
        self.v = 0

    def inc(self, n=1):
        self.v += n
        return self.v


class RotSem:
    def __init__(self, sems):
        self.sems = sems
        self.cur = sems[0]

    @property
    def h(self):
        return self.cur.h

    @property
    def v(self):
        return self.cur.v

    def inc(self, n=1):
        return self.cur.inc(n)

    def rotate(self, i):
        self.cur = self.sems[i]


class Prog:
    ENGINES = ("tensor", "vector", "scalar", "sync", "gpsimd")

    def __init__(self):
        self.st = {e: [] for e in self.ENGINES}

    def emit(self, eng, fn):
        self.st[eng].append(fn)

    def wait(self, eng, sem, val):
        if val > 0:
            self.emit(eng, lambda e, s=sem.h, v=val: e.wait_ge(s, v))


def build(n_sub=4):
    nc = bass.Bass("TRN2", target_bir_lowering=False)
    p = Prog()

    def din(name, shape):
        return nc.dram_tensor(name, list(shape), F32, kind="ExternalInput").ap()

    xT = din("xT", [NE, DM, TOK])
    WqkvT = din("WqkvT", [DM, 3 * DM])
    WoutT = din("WoutT", [DM, DM])
    Wq, Wk, Wv, Wl, Wo = (din(n, [DM, DM]) for n in ("Wq", "Wk", "Wv", "Wl", "Wo"))
    U1 = din("U1", [DM, RANK])
    V1 = din("V1", [RANK, DFF])
    U2 = din("U2", [DFF, RANK])
    V2 = din("V2", [RANK, DM])
    biasf_in = din("biasf", [128, 112])
    bvrow = din("bvrow", [1, DM])
    blrow = din("blrow", [1, DM])
    onesc_in = din("onesc", [1, 128])
    monesc_in = din("monesc", [1, 128])
    onescol_in = din("onescol", [128, 1])
    id16_in = din("id16", [16, 16])
    indn_in = din("indn", [128, 8 * 16])
    inds48_in = din("inds48", [48, NH * 64])
    sel33_in = din("sel33", [33, 128])
    amask_in = din("amask", [128, 512])
    ehc_in = din("ehc", [1, 256])
    outT = nc.dram_tensor("outT", [NE, DM, TOK], F32, kind="ExternalOutput").ap()
    dbg = {}
    if _DBG or _DBG2 or _DBG3 or _DBG4:
        for nm, shape in (("d_ln", [NK * 128, SW]), ("d_q", [NK * 128, SW]),
                          ("d_k", [NK * 128, SW]), ("d_v", [128, 4 * NH * 65]),
                          ("d_negc", [128, 16]), ("d_tmpu", [NK * 128, SW]),
                          ("d_zg", [16, SW]), ("d_out", [NK * 128, SW]),
                          ("d_x", [NK * 128, SW])):
            dbg[nm] = nc.dram_tensor(nm, shape, F32, kind="ExternalOutput").ap()

    from contextlib import ExitStack
    stack = ExitStack()
    E = stack.enter_context
    ledger = [0]

    def sbt(name, shape, dt):
        n = 1
        for s in shape[1:]:
            n *= s
        ledger[0] += n * (2 if dt == BF16 else 4)
        return E(nc.sbuf_tensor(name, list(shape), dt))

    x = sbt("x", [128, NK, TOK], F32)
    qbuf = sbt("qbuf", [128, NK, TOK], BF16)
    kbuf = sbt("kbuf", [128, NK, TOK], BF16)
    vbuf = sbt("vbuf", [128, NC, NH * 65], BF16)
    lnbuf = sbt("lnbuf", [128, NK, SW], BF16)
    t1buf = sbt("t1buf", [128, NK, SW], BF16)
    wbuf = sbt("wbuf", [128, 2048], BF16)
    wffn = sbt("wffn", [128, 1024], BF16)
    bcline = sbt("bcline", [128, DM], BF16)
    rowh = sbt("rowh", [1, DM], BF16)
    zst = sbt("zst", [1, SW], F32)
    rz = sbt("rz", [16, SW], F32)
    rzhl = sbt("rzhl", [48, SW], BF16)
    mrows = sbt("mrows", [128, SW], F32)
    hlr = sbt("hlr", [33, SW], BF16)
    hlm = sbt("hlm", [33, SW], BF16)
    tmpf = sbt("tmpf", [128, SW], F32)
    negcA = sbt("negcA", [128, 16], F32)
    negcB = sbt("negcB", [128, 16], F32)
    smax = sbt("smax", [16, 16], F32)
    lnsc = sbt("lnsc", [16, 1], F32)
    epsc = sbt("epsc", [128, 1], F32)
    biasf = sbt("biasf_s", [128, 112], F32)
    onesc = sbt("onesc_s", [1, 128], BF16)
    monesc = sbt("monesc_s", [1, 128], BF16)
    onescol = sbt("onescol_s", [128, 1], BF16)
    id16 = sbt("id16_s", [16, 16], BF16)
    indn = sbt("indn_s", [128, 8, 16], BF16)
    inds48 = sbt("inds48_s", [48, NH, 64], BF16)
    sel33 = sbt("sel33_s", [33, 128], BF16)
    amask = sbt("amask_s", [128, 512], BF16)
    ehc = sbt("ehc_s", [1, 256], BF16)
    zhi = sbt("zhi", [1, SW], BF16)
    zlo = sbt("zlo", [1, SW], BF16)
    zhiB = wffn[0:1, 0:512]
    zloB = wffn[0:1, 512:1024]
    assert ledger[0] <= 211500, f"SBUF over budget: {ledger[0]}"

    pall = E(nc.psum_tensor("pall", [128, 4096], F32))

    def bank(i, rows=128, w=512):
        return pall[0:rows, i * 512:i * 512 + w]

    S_LDX = Sem(E(nc.semaphore("ldx")))
    S_LDW = RotSem([Sem(E(nc.semaphore(f"ldw{i}"))) for i in range(NE)])
    S_LDF = Sem(E(nc.semaphore("ldf")))
    S_T = Sem(E(nc.semaphore("t")))
    S_V = Sem(E(nc.semaphore("v")))
    S_S = Sem(E(nc.semaphore("s")))
    S_ST = Sem(E(nc.semaphore("st")))

    def mm(out, lhsT, rhs, start, stop, sem=None, tp=None):
        hv = None if sem is None else (sem[0].h, sem[1])
        def f(e, o=out, l=lhsT, r=rhs, a=start, b=stop, sm=hv, t=tp):
            i = e.matmul(o, l, r, start=a, stop=b, tile_position=t)
            if sm is not None:
                i.then_inc(sm[0], sm[1])
        p.emit("tensor", f)

    def act(out, in_, func, bias=0.0, scale=1.0, sem=None):
        if func == AF.Copy and not isinstance(bias, float):
            func = AF.Identity
        hv = None if sem is None else (sem[0].h, sem[1])
        def f(e, o=out, i=in_, fn=func, b=bias, sc=scale, sm=hv):
            ins = e.activation(o, i, fn, bias=b, scale=sc)
            if sm is not None:
                ins.then_inc(sm[0], sm[1])
        p.emit("scalar", f)

    def vop(name, *args, sem=None, **kw):
        hv = None if sem is None else (sem[0].h, sem[1])
        def f(e, nm=name, a=args, sm=hv, k=kw):
            ins = getattr(e, nm)(*a, **k)
            if sm is not None:
                ins.then_inc(sm[0], sm[1])
        p.emit("vector", f)

    def dma(eng, out, in_, sem):
        hv = (sem[0].h, sem[1])
        def f(e, o=out, i=in_, sm=hv):
            e.dma_start(out=o, in_=i).then_inc(sm[0], sm[1])
        p.emit(eng, f)

    def barrier():
        p.wait("tensor", S_V, S_V.v)
        p.wait("tensor", S_S, S_S.v)
        p.wait("vector", S_T, S_T.v)
        p.wait("vector", S_S, S_S.v)
        p.wait("scalar", S_T, S_T.v)
        p.wait("scalar", S_V, S_V.v)

    for dst, src in ((biasf[:, :], biasf_in), (onesc[:, :], onesc_in),
                     (monesc[:, :], monesc_in), (onescol[:, :], onescol_in),
                     (id16[:, :], id16_in), (indn[:, :, :], indn_in),
                     (inds48[:, :, :], inds48_in), (sel33[:, :], sel33_in),
                     (amask[:, :], amask_in), (ehc[:, :], ehc_in)):
        S_LDW.inc(16)
        dma("gpsimd", dst, src, (S_LDW, 16))
    p.wait("vector", S_LDW, S_LDW.v)
    vop("memset", lnsc[:, :], float(np.log(SCALE)) + 0.03)
    vop("memset", epsc[:, :], 1e-5)
    vop("memset", hlr[:, :], 0.0)
    vop("memset", hlm[:, :], 0.0)
    vop("memset", rzhl[:, :], 0.0)
    S_V.inc()
    vop("memset", vbuf[:, :, :], 1.0, sem=(S_V, 1))
    p.wait("vector", S_V, S_V.v)

    wslot = [0, 0]
    quad_free = [(0, 0), (0, 0)]
    zmm_guard = [0]
    f1slot = [0, 0]
    f2slot = [0, 0, 0, 0]
    ra_guard = [0, 0]

    def emit_bcline(row_dram):
        S_LDW.inc(16)
        dma("gpsimd", rowh[:, :], row_dram, (S_LDW, 16))
        barrier()
        p.wait("tensor", S_LDW, S_LDW.v)
        mm(bank(0), onesc[0:1, :], rowh[0:1, 0:512], True, True)
        S_T.inc()
        mm(bank(1), onesc[0:1, :], rowh[0:1, 512:1024], True, True, sem=(S_T, 1))
        p.wait("vector", S_T, S_T.v)
        vop("tensor_copy", bcline[:, 0:512], bank(0))
        S_V.inc()
        vop("tensor_copy", bcline[:, 512:1024], bank(1), sem=(S_V, 1))
        quad_free[0] = (S_S.v, S_V.v)

    def emit_ln(s):
        sl = slice(s * SW, (s + 1) * SW)
        barrier()
        p.wait("vector", S_LDF, S_LDF.v)
        S_V.inc()
        vop("tensor_copy", lnbuf[:, :, :], x[:, :, sl], sem=(S_V, 1))
        p.wait("vector", S_V, S_V.v)
        S_V.inc()
        vop("tensor_mul", t1buf[:, :, :], lnbuf[:, :, :], lnbuf[:, :, :], sem=(S_V, 1))
        p.wait("tensor", S_V, S_V.v)
        for k in range(NK):
            sem = (S_T, 1) if k == NK - 1 else None
            if sem:
                S_T.inc()
            mm(bank(0, 1), onescol[:, 0:1], lnbuf[:, k, :], k == 0, k == NK - 1, sem=sem)
        tm = S_T.v
        for k in range(NK):
            sem = (S_T, 1) if k == NK - 1 else None
            if sem:
                S_T.inc()
            mm(bank(1, 1), onescol[:, 0:1], t1buf[:, k, :], k == 0, k == NK - 1, sem=sem)
        tsq = S_T.v
        p.wait("vector", S_T, tm)
        S_V.inc()
        vop("tensor_copy", mrows[0:1, :], bank(0, 1), sem=(S_V, 1))
        p.wait("vector", S_T, tsq)
        S_V.inc()
        vop("tensor_copy", rz[0:1, :], bank(1, 1), sem=(S_V, 1))
        p.wait("vector", S_V, S_V.v)
        S_V.inc()
        vop("tensor_scalar", mrows[0:1, :], mrows[0:1, :], 1.0 / DM, None, ALU.mult,
            sem=(S_V, 1))
        p.wait("vector", S_V, S_V.v)
        S_V.inc()
        vop("tensor_mul", zst[0:1, :], mrows[0:1, :], mrows[0:1, :], sem=(S_V, 1))
        p.wait("vector", S_V, S_V.v)
        S_V.inc()
        vop("scalar_tensor_tensor", rz[0:1, :], rz[0:1, :], 1.0 / DM,
            zst[0:1, :], ALU.mult, ALU.subtract, sem=(S_V, 1))
        p.wait("scalar", S_V, S_V.v)
        S_S.inc()
        act(rz[0:1, :], rz[0:1, :], AF.Ln, bias=epsc[0:1, 0:1], sem=(S_S, 1))
        p.wait("scalar", S_S, S_S.v)
        S_S.inc()
        act(rz[0:1, :], rz[0:1, :], AF.Exp, scale=-0.5, sem=(S_S, 1))
        p.wait("vector", S_S, S_S.v)
        S_V.inc()
        vop("tensor_mul", mrows[0:1, :], mrows[0:1, :], rz[0:1, :], sem=(S_V, 1))
        p.wait("vector", S_V, S_V.v)
        S_V.inc()
        vop("tensor_copy", hlr[0:1, :], rz[0:1, :], sem=(S_V, 1))
        p.wait("vector", S_V, S_V.v)
        S_V.inc()
        vop("tensor_sub", hlr[32:33, :], rz[0:1, :], hlr[0:1, :], sem=(S_V, 1))
        p.wait("vector", S_V, S_V.v)
        S_V.inc()
        vop("tensor_copy", hlm[0:1, :], mrows[0:1, :], sem=(S_V, 1))
        p.wait("vector", S_V, S_V.v)
        S_V.inc()
        vop("tensor_sub", hlm[32:33, :], mrows[0:1, :], hlm[0:1, :], sem=(S_V, 1))
        p.wait("tensor", S_V, S_V.v)
        mm(bank(2), sel33[:, :], hlr[:, :], True, True)
        S_T.inc()
        mm(bank(3), sel33[:, :], hlm[:, :], True, True, sem=(S_T, 1))
        p.wait("vector", S_T, S_T.v)
        for k in range(NK):
            vop("scalar_tensor_tensor", tmpf[:, :], x[:, k, sl], 1.0,
                bank(2), ALU.mult, ALU.mult)
            sem = (S_V, 1) if k == NK - 1 else None
            if sem:
                S_V.inc()
            vop("tensor_sub", lnbuf[:, k, :], tmpf[:, :], bank(3), sem=sem)
        p.wait("tensor", S_V, S_V.v)
        p.wait("scalar", S_V, S_V.v)
        quad_free[0] = (S_S.v, S_V.v)
        return (S_S.v, S_V.v)

    bankg = [(0, 0)] * 8

    def emit_bproj(wsrc, msl, rhs_fn, nmt, nkk, dst_fn, bias_col, accfn=None,
                   rhs_ready=None):
        tmark = [0] * nmt
        w = nmt * 128
        for kk in range(nkk):
            slot = kk % 2
            wt = wbuf[:, slot * 1024: slot * 1024 + w]
            p.wait("gpsimd", S_T, wslot[slot])
            S_LDW.inc(16)
            dma("gpsimd", wt,
                wsrc[kk * 128:(kk + 1) * 128, msl:msl + w], (S_LDW, 16))
            p.wait("tensor", S_LDW, S_LDW.v)
            if kk == 0:
                rr = rhs_ready if rhs_ready is not None else quad_free[0]
                p.wait("tensor", S_S, max(rr[0], quad_free[1][0]))
                p.wait("tensor", S_V, max(rr[1], quad_free[1][1]))
            rhs = rhs_fn(kk, 0)
            for m in range(nmt):
                if kk == 0:
                    p.wait("tensor", S_S, bankg[m][0])
                    p.wait("tensor", S_V, bankg[m][1])
                last = kk == nkk - 1
                sem = (S_T, 1) if (last or m == nmt - 1) else None
                if sem:
                    S_T.inc()
                mm(bank(m), wt[:, m * 128:(m + 1) * 128], rhs, kk == 0, last,
                   sem=sem)
                if last:
                    tmark[m] = S_T.v
            wslot[slot] = S_T.v
        for m in range(nmt):
            if accfn is not None:
                accfn(m, 0, bank(m), tmark[m])
                bankg[m] = (S_S.v, 0)
            else:
                b = (biasf[:, bias_col + m: bias_col + m + 1]
                     if bias_col is not None else None)
                if m % 2 == 0:
                    p.wait("scalar", S_T, tmark[m])
                    S_S.inc()
                    act(dst_fn(m, 0), bank(m), AF.Copy,
                        bias=(0.0 if b is None else b), sem=(S_S, 1))
                    bankg[m] = (S_S.v, 0)
                else:
                    p.wait("vector", S_T, tmark[m])
                    S_V.inc()
                    if b is None:
                        vop("tensor_copy", dst_fn(m, 0), bank(m), sem=(S_V, 1))
                    else:
                        vop("tensor_single_scalar", dst_fn(m, 0), bank(m), b,
                            ALU.add, sem=(S_V, 1))
                    bankg[m] = (0, S_V.v)
        quad_free[0] = (S_S.v, S_V.v)

    def emit_resadd(s, bias_col):
        def accfn(m, half, ps, tm):
            slot = m % 2
            sl = slice(s * SW, (s + 1) * SW)
            p.wait("scalar", S_T, tm)
            p.wait("scalar", S_V, ra_guard[slot])
            S_S.inc()
            dst = tmpf[:, :] if slot == 0 else mrows[:, :]
            act(dst, ps, AF.Copy,
                bias=biasf[:, bias_col + m: bias_col + m + 1], sem=(S_S, 1))
            p.wait("vector", S_S, S_S.v)
            S_V.inc()
            vop("tensor_add", x[:, m, sl], x[:, m, sl], dst, sem=(S_V, 1))
            ra_guard[slot] = S_V.v
        return accfn

    def emit_vproj(wsrc2d, wcol0, actT, tcglob_fn, bc_add, rhs_ready=None):
        for nh2 in range(2):
            vt = [0] * 4
            for kk in range(NK):
                slot = kk % 2
                wt = wbuf[:, slot * 1024: slot * 1024 + 512]
                c0 = wcol0 + nh2 * 512
                p.wait("gpsimd", S_T, wslot[slot])
                S_LDW.inc(16)
                dma("gpsimd", wt[:, 0:512],
                    wsrc2d[kk * 128:(kk + 1) * 128, c0:c0 + 512], (S_LDW, 16))
                p.wait("tensor", S_LDW, S_LDW.v)
                if kk == 0 and nh2 == 0:
                    rr = rhs_ready if rhs_ready is not None else quad_free[0]
                    p.wait("tensor", S_S, max(rr[0], quad_free[1][0]))
                    p.wait("tensor", S_V, max(rr[1], quad_free[1][1]))
                for tc in range(4):
                    if kk == 0:
                        p.wait("tensor", S_S, bankg[tc][0])
                        p.wait("tensor", S_V, bankg[tc][1])
                    last = kk == NK - 1
                    sem = (S_T, 1) if (last or tc == 3) else None
                    if sem:
                        S_T.inc()
                    mm(bank(tc), actT[:, kk, tc * 128:(tc + 1) * 128],
                       wt[:, 0:512], kk == 0, last, sem=sem)
                    if last:
                        vt[tc] = S_T.v
                wslot[slot] = S_T.v
            for tc in range(4):
                p.wait("vector", S_T, vt[tc])
                tcg = tcglob_fn(tc)
                S_V.inc()
                vop("tensor_add",
                    vbuf[:, tcg, :].rearrange("p (h n) -> p h n", h=NH)
                    [:, nh2 * 8:nh2 * 8 + 8, 0:64],
                    bank(tc).rearrange("p (h n) -> p h n", h=8),
                    bc_add[:, nh2 * 512:nh2 * 512 + 512].rearrange("p (h n) -> p h n", h=8),
                    sem=(S_V, 1))
                bankg[tc] = (0, S_V.v)
        quad_free[0] = (quad_free[0][0], S_V.v)

    def emit_negc(sq_src_fn, n_batches, negc_dst, colq, colk):
        p.wait("scalar", S_V, S_V.v)
        S_S.inc()
        act(smax[:, 10:11], smax[:, colq:colq + 1], AF.Ln, sem=(S_S, 1))
        p.wait("scalar", S_S, S_S.v)
        S_S.inc()
        act(smax[:, 11:12], smax[:, colk:colk + 1], AF.Ln, sem=(S_S, 1))
        p.wait("vector", S_S, S_S.v)
        S_V.inc()
        vop("tensor_add", smax[:, 12:13], smax[:, 10:11], smax[:, 11:12], sem=(S_V, 1))
        p.wait("scalar", S_V, S_V.v)
        S_S.inc()
        act(smax[:, 13:14], smax[:, 12:13], AF.Exp, bias=lnsc[:, 0:1], scale=0.5,
            sem=(S_S, 1))
        p.wait("vector", S_S, S_S.v)
        S_V.inc()
        vop("tensor_copy", rzhl[0:16, 0:1], smax[:, 13:14], sem=(S_V, 1))
        p.wait("tensor", S_V, S_V.v)
        p.wait("tensor", S_S, max(bankg[4][0], bankg[5][0]))
        p.wait("tensor", S_V, max(bankg[4][1], bankg[5][1]))
        S_T.inc()
        mm(pall[0:1, 2048:2064], rzhl[0:16, 0:1], id16[:, :], True, True, sem=(S_T, 1))
        p.wait("scalar", S_T, S_T.v)
        S_S.inc()
        act(rowh[0:1, 0:16], pall[0:1, 2048:2064], AF.Copy, sem=(S_S, 1))
        p.wait("tensor", S_S, S_S.v)
        S_T.inc()
        mm(pall[:, 2560:2576], monesc[0:1, :], rowh[0:1, 0:16], True, True, sem=(S_T, 1))
        p.wait("vector", S_T, S_T.v)
        S_V.inc()
        vop("tensor_copy", negc_dst[:, :], pall[:, 2560:2576], sem=(S_V, 1))
        p.wait("vector", S_V, S_V.v)
        S_V.inc()
        vop("tensor_scalar", negc_dst[:, :], negc_dst[:, :], -60.0, 0.0,
            ALU.max, ALU.min, sem=(S_V, 1))
        p.wait("scalar", S_V, S_V.v)

    def emit_norms(srcT, sl, dstcol):
        p.wait("vector", S_T, S_T.v)
        p.wait("vector", S_S, S_S.v)
        p.wait("vector", S_V, quad_free[0][1])
        S_V.inc()
        vop("tensor_mul", t1buf[:, :, :], srcT, srcT, sem=(S_V, 1))
        p.wait("tensor", S_V, S_V.v)
        p.wait("tensor", S_S, bankg[5][0])
        p.wait("tensor", S_V, bankg[5][1])
        for k in range(NK):
            sem = (S_T, 1) if k == NK - 1 else None
            if sem:
                S_T.inc()
            mm(bank(5, 16), indn[:, k, :], t1buf[:, k, :], k == 0, k == NK - 1, sem=sem)
        p.wait("vector", S_T, S_T.v)
        S_V.inc()
        vop("reduce_max", smax[:, dstcol:dstcol + 1], bank(5, 16), AX.X, sem=(S_V, 1))
        bankg[5] = (0, S_V.v)
        quad_free[1] = (quad_free[1][0], S_V.v)

    def emit_normalize(zdone, outstrip3):
        p.wait("vector", S_T, zdone)
        S_V.inc()
        vop("reciprocal", rz[:, :], pall[0:16, 5 * 512:6 * 512], sem=(S_V, 1))
        zmm_guard[0] = 0
        quad_free[1] = (quad_free[1][0], S_V.v)
        p.wait("vector", S_V, S_V.v)
        S_V.inc()
        vop("tensor_copy", rzhl[0:16, :], rz[:, :], sem=(S_V, 1))
        p.wait("vector", S_V, S_V.v)
        S_V.inc()
        vop("tensor_sub", rzhl[32:48, :], rz[:, :], rzhl[0:16, :], sem=(S_V, 1))
        p.wait("tensor", S_V, S_V.v)
        for h in range(NH):
            kkt, rb = h // 2, (h % 2) * 64
            pdst = pall[rb:rb + 64, 4 * 512:4 * 512 + 512]
            S_T.inc()
            mm(pdst, inds48[:, h, :], rzhl[:, :], True, True, sem=(S_T, 1),
               tp=(0, rb))
            p.wait("vector", S_T, S_T.v)
            S_V.inc()
            vop("tensor_mul", outstrip3[rb:rb + 64, kkt, :],
                t1buf[rb:rb + 64, kkt, :], pdst, sem=(S_V, 1))
            if h < NH - 1:
                p.wait("tensor", S_V, S_V.v)
        p.wait("tensor", S_V, S_V.v)

    def tap2d(nm, buf3, cast=True):
        if not (_DBG or _DBG2 or _DBG3 or _DBG4):
            return
        barrier()
        p.wait("gpsimd", S_T, S_T.v)
        p.wait("gpsimd", S_S, S_S.v)
        p.wait("gpsimd", S_V, S_V.v)
        for k in range(NK):
            S_LDW.inc(16)
            dma("gpsimd", dbg[nm][k * 128:(k + 1) * 128, :], buf3[:, k, 0:SW], (S_LDW, 16))
        p.wait("gpsimd", S_LDW, S_LDW.v)
        for eng in ("tensor", "vector", "scalar"):
            p.wait(eng, S_LDW, S_LDW.v)

    def tap_raw(nm, ap_):
        if not (_DBG or _DBG2 or _DBG3 or _DBG4):
            return
        barrier()
        p.wait("gpsimd", S_T, S_T.v)
        p.wait("gpsimd", S_S, S_S.v)
        p.wait("gpsimd", S_V, S_V.v)
        S_LDW.inc(16)
        dma("gpsimd", dbg[nm], ap_, (S_LDW, 16))
        p.wait("gpsimd", S_LDW, S_LDW.v)
        for eng in ("tensor", "vector", "scalar"):
            p.wait(eng, S_LDW, S_LDW.v)

    x_store_done = 0

    for e in range(NE):
        S_LDW.rotate(e)
        p.wait("sync", S_ST, x_store_done)
        for k in range(NK):
            S_LDX.inc(16)
            dma("sync", x[:, k, :], xT[e, k * 128:(k + 1) * 128, :], (S_LDX, 16))
        p.wait("vector", S_LDX, S_LDX.v)
        p.wait("tensor", S_LDX, S_LDX.v)
        p.wait("scalar", S_LDX, S_LDX.v)

        if n_sub >= 1:
            emit_bcline(bvrow)
            for s in range(NSTR):
                lnr = emit_ln(s)
                if e == 0 and s == 0:
                    tap2d("d_ln", lnbuf)
                emit_bproj(WqkvT, 0,
                           lambda kk, hf: lnbuf[:, kk, :],
                           NK, NK,
                           lambda m, hf: qbuf[:, m, 0:SW],
                           BQKV_Q, rhs_ready=lnr)
                emit_bproj(WqkvT, DM,
                           lambda kk, hf: lnbuf[:, kk, :],
                           NK, NK,
                           lambda m, hf: kbuf[:, m, 0:SW],
                           BQKV_K, rhs_ready=lnr)
                if e == 0 and s == 0:
                    tap2d("d_q", qbuf)
                    tap2d("d_k", kbuf)
                emit_vproj(WqkvT, 2 * DM, lnbuf, lambda tc: tc, bcline,
                           rhs_ready=lnr)
                if e == 0 and s == 0:
                    tap_raw("d_v", vbuf[:, 0:4, :])
                emit_norms(qbuf[:, :, 0:SW], s, 0)
                emit_norms(kbuf[:, :, 0:SW], s, 1)
                emit_negc(None, 1, negcA, 0, 1)
                if e == 0 and s == 0:
                    tap_raw("d_negc", negcA[:, :])
                barrier()
                expm = [0, 0]
                pvev = [(0, 0), (0, 0)]
                for pair in range(NH // 2):
                    for hh in range(2):
                        h = 2 * pair + hh
                        rb = (h % 2) * 64
                        p.wait("tensor", S_S, expm[hh])
                        for c in range(4):
                            sem = (S_T, 1) if c == 3 else None
                            if sem:
                                S_T.inc()
                            mm(bank(hh)[:, c * 128:(c + 1) * 128],
                               kbuf[rb:rb + 64, pair, c * 128:(c + 1) * 128],
                               qbuf[rb:rb + 64, pair, c * 128:(c + 1) * 128],
                               True, True, sem=sem, tp=(rb, 0))
                        tsc = S_T.v
                        p.wait("scalar", S_T, tsc)
                        S_S.inc()
                        act(lnbuf[:, hh, :], bank(hh), AF.Exp,
                            bias=negcA[:, h:h + 1], scale=SCALE, sem=(S_S, 1))
                        expm[hh] = S_S.v
                        p.wait("vector", S_S, S_S.v)
                        S_V.inc()
                        vop("tensor_mul", lnbuf[:, hh, :], lnbuf[:, hh, :],
                            amask[:, :], sem=(S_V, 1))
                        vmask = S_V.v
                        p.wait("tensor", S_V, vmask)
                        p.wait("tensor", S_S, pvev[hh][0])
                        p.wait("tensor", S_V, pvev[hh][1])
                        for c in range(4):
                            sem = (S_T, 1) if c == 3 else None
                            if sem:
                                S_T.inc()
                            mm(bank(2 + hh, 65)[:, c * 128:(c + 1) * 128],
                               vbuf[:, c, h * 65:(h + 1) * 65],
                               lnbuf[:, hh, c * 128:(c + 1) * 128],
                               True, True, sem=sem)
                        tpv = S_T.v
                        kkt = h // 2
                        p.wait("scalar", S_T, tpv)
                        S_S.inc()
                        act(t1buf[rb:rb + 64, kkt, :], bank(2 + hh, 65)[0:64, :],
                            AF.Copy, sem=(S_S, 1))
                        sev = S_S.v
                        p.wait("vector", S_T, tpv)
                        p.wait("vector", S_T, zmm_guard[0])
                        S_V.inc()
                        vop("tensor_copy", zhi[0:1, :], bank(2 + hh, 65)[64:65, :],
                            sem=(S_V, 1))
                        p.wait("vector", S_V, S_V.v)
                        S_V.inc()
                        vop("scalar_tensor_tensor", zlo[0:1, :],
                            bank(2 + hh, 65)[64:65, :], 1.0, zhi[0:1, :],
                            ALU.mult, ALU.subtract, sem=(S_V, 1))
                        vz = S_V.v
                        pvev[hh] = (sev, vz)
                        p.wait("tensor", S_V, vz)
                        mm(pall[0:16, 5 * 512:6 * 512], ehc[0:1, h * 16:(h + 1) * 16],
                           zhi[0:1, :], h == 0, False)
                        S_T.inc()
                        mm(pall[0:16, 5 * 512:6 * 512], ehc[0:1, h * 16:(h + 1) * 16],
                           zlo[0:1, :], False, h == NH - 1, sem=(S_T, 1))
                        zmm_guard[0] = S_T.v
                        if _BARR & 1:
                            barrier()
                    if _BARR & 2:
                        barrier()
                zdone = S_T.v
                barrier()
                if e == 0 and s == 0:
                    tap2d("d_tmpu", t1buf)
                emit_normalize(zdone, lnbuf)
                if e == 0 and s == 0:
                    tap2d("d_out", lnbuf)
                emit_bproj(WoutT, 0,
                           lambda kk, hf: lnbuf[:, kk, :],
                           NK, NK, None, None, accfn=emit_resadd(s, BOUT),
                           rhs_ready=(0, 0))
                if _DBG2 and e == 0 and s == NSTR - 1:
                    tap_raw("d_negc", negcA[:, :])
                    tap2d("d_tmpu", t1buf)
                    tap2d("d_out", lnbuf)
                    tap_raw("d_v", vbuf[:, 12:16, :])
                if e == 0 and s == 0:
                    barrier()
                    p.wait("sync", S_V, S_V.v)
                    if _DBG:
                        for k in range(NK):
                            S_LDW.inc(16)
                            dma("sync", dbg["d_x"][k * 128:(k + 1) * 128, :],
                                x[:, k, 0:SW], (S_LDW, 16))
                        p.wait("sync", S_LDW, S_LDW.v)
            barrier()

        def emit_ffn():
            for s in range(NSTR):
                lnr = emit_ln(s)
                emit_bproj(U1, 0,
                           lambda kk, hf: lnbuf[:, kk, :],
                           2, NK,
                           lambda m, hf: kbuf[:, m, 0:SW],
                           None, rhs_ready=lnr)
                barrier()
                for m in range(32):
                    qd = (m % 2) * 4
                    slot = m % 2
                    wt = wffn[:, slot * 512: slot * 512 + 256]
                    p.wait("gpsimd", S_T, f1slot[slot])
                    S_LDW.inc(32)
                    dma("gpsimd", wt[:, 0:128], V1[0:128, m * 128:(m + 1) * 128],
                        (S_LDW, 16))
                    dma("gpsimd", wt[:, 128:256], V1[128:256, m * 128:(m + 1) * 128],
                        (S_LDW, 16))
                    p.wait("tensor", S_LDW, S_LDW.v)
                    p.wait("tensor", S_S, fq_free[qd // 4])
                    for kk in range(2):
                        sem = (S_T, 1) if kk == 1 else None
                        if sem:
                            S_T.inc()
                        mm(bank(qd), wt[:, kk * 128:kk * 128 + 128], kbuf[:, kk, 0:512],
                           kk == 0, kk == 1, sem=sem)
                    f1slot[slot] = S_T.v
                    p.wait("scalar", S_T, S_T.v)
                    S_S.inc()
                    act(qbuf.ap().rearrange("p k (a t) -> p (k a) t", a=4)[:, m, :],
                        bank(qd), AF.Gelu_apprx_tanh,
                        bias=biasf[:, B1 + m:B1 + m + 1], sem=(S_S, 1))
                    fq_free[qd // 4] = S_S.v
                p.wait("tensor", S_S, S_S.v)
                for m in range(2):
                    qd = (m % 2) * 4
                    for kk in range(32):
                        slot = kk % 4
                        wt = wffn[:, slot * 256: slot * 256 + 128]
                        p.wait("gpsimd", S_T, f2slot[slot])
                        S_LDW.inc(16)
                        dma("gpsimd", wt, U2[kk * 128:(kk + 1) * 128,
                                             m * 128:(m + 1) * 128], (S_LDW, 16))
                        p.wait("tensor", S_LDW, S_LDW.v)
                        if kk == 0:
                            p.wait("tensor", S_S, fq_free[qd // 4])
                        sem = (S_T, 1)
                        S_T.inc()
                        mm(bank(qd),
                           wt, qbuf.ap().rearrange("p k (a t) -> p (k a) t", a=4)[:, kk, :],
                           kk == 0, kk == 31, sem=sem)
                        f2slot[slot] = S_T.v
                    p.wait("scalar", S_T, S_T.v)
                    S_S.inc()
                    act(kbuf[:, 2 + m, 0:512], bank(qd), AF.Copy, sem=(S_S, 1))
                    fq_free[qd // 4] = S_S.v
                barrier()
                emit_bproj(V2, 0,
                           lambda kk, hf: kbuf[:, 2 + kk, 0:SW],
                           NK, 2, None, None, accfn=emit_resadd(s, B2),
                           rhs_ready=(0, 0))
            barrier()
        fq_free = [0, 0]

        if n_sub >= 2:
            emit_ffn()

        if n_sub >= 3:
            emit_bcline(blrow)
            p.wait("vector", S_V, S_V.v)
            S_V.inc()
            vop("memset", smax[:, 0:10], 0.0, sem=(S_V, 1))
            p.wait("vector", S_V, S_V.v)
            for s in range(NSTR):
                lnr = emit_ln(s)
                sl = slice(s * SW, (s + 1) * SW)
                emit_bproj(Wq, 0,
                           lambda kk, hf: lnbuf[:, kk, :],
                           NK, NK,
                           lambda m, hf, s_=s: qbuf[:, m, s_ * SW:(s_ + 1) * SW],
                           BQ, rhs_ready=lnr)
                emit_norms(qbuf[:, :, sl], s, 2)
                p.wait("vector", S_V, S_V.v)
                S_V.inc()
                vop("tensor_max", smax[:, 0:1], smax[:, 0:1], smax[:, 2:3], sem=(S_V, 1))
                p.wait("vector", S_V, S_V.v)
                emit_bproj(Wk, 0,
                           lambda kk, hf: lnbuf[:, kk, :],
                           NK, NK,
                           lambda m, hf: t1buf[:, m, :],
                           BK, rhs_ready=lnr)
                emit_bproj(Wl, 0,
                           lambda kk, hf: t1buf[:, kk, :],
                           NK, NK,
                           lambda m, hf, s_=s: kbuf[:, m, s_ * SW:(s_ + 1) * SW],
                           BL)
                emit_norms(kbuf[:, :, sl], s, 3)
                p.wait("vector", S_V, S_V.v)
                S_V.inc()
                vop("tensor_max", smax[:, 1:2], smax[:, 1:2], smax[:, 3:4], sem=(S_V, 1))
                p.wait("vector", S_V, S_V.v)
                emit_bproj(Wv, 0,
                           lambda kk, hf: lnbuf[:, kk, :],
                           NK, NK,
                           lambda m, hf: t1buf[:, m, :],
                           BV, rhs_ready=lnr)
                emit_vproj(Wl, 0, t1buf, lambda tc, s_=s: s_ * 4 + tc, bcline)
                if _DBG4 and e == 0 and s == 0:
                    tap2d("d_q", qbuf)
                    tap2d("d_k", kbuf)
                    tap_raw("d_v", vbuf[:, 0:4, :])
            emit_negc(None, 1, negcB, 0, 1)
            barrier()
            zguard = [S_T.v]

            def emit_zmms(ha, hb, vzB_):
                p.wait("tensor", S_V, vzB_)
                mm(pall[0:16, 5 * 512:6 * 512], ehc[0:1, ha * 16:(ha + 1) * 16],
                   zhi[0:1, :], ha == 0, False)
                mm(pall[0:16, 5 * 512:6 * 512], ehc[0:1, ha * 16:(ha + 1) * 16],
                   zlo[0:1, :], False, False)
                mm(pall[0:16, 5 * 512:6 * 512], ehc[0:1, hb * 16:(hb + 1) * 16],
                   zhiB[0:1, :], False, False)
                S_T.inc()
                mm(pall[0:16, 5 * 512:6 * 512], ehc[0:1, hb * 16:(hb + 1) * 16],
                   zloB[0:1, :], False, hb == NH - 1, sem=(S_T, 1))
                zguard[0] = S_T.v

            if _KPIPE == 9:
                for s in range(NSTR):
                    qsl = slice(s * SW, (s + 1) * SW)
                    expm = [0, 0]
                    pvev = [(0, 0), (0, 0)]
                    for pair in range(NH // 2):
                        for hh in range(2):
                            h = 2 * pair + hh
                            rb = (h % 2) * 64
                            p.wait("tensor", S_S, pvev[hh][0])
                            p.wait("tensor", S_V, pvev[hh][1])
                            for kc in range(NC):
                                p.wait("tensor", S_S, expm[hh])
                                S_T.inc()
                                mm(bank(hh),
                                   kbuf[rb:rb + 64, pair, kc * 128:(kc + 1) * 128],
                                   qbuf[rb:rb + 64, pair, qsl], True, True,
                                   sem=(S_T, 1), tp=(rb, 0))
                                p.wait("scalar", S_T, S_T.v)
                                slot = hh * 3 + kc % 3
                                S_S.inc()
                                act(lnbuf[:, slot, :], bank(hh), AF.Exp,
                                    bias=negcB[:, h:h + 1], scale=SCALE, sem=(S_S, 1))
                                expm[hh] = S_S.v
                                p.wait("tensor", S_S, S_S.v)
                                S_T.inc()
                                mm(bank(2 + hh, 65), vbuf[:, kc, h * 65:(h + 1) * 65],
                                   lnbuf[:, slot, :], kc == 0, kc == NC - 1,
                                   sem=(S_T, 1))
                            tpv = S_T.v
                            kkt = h // 2
                            p.wait("scalar", S_T, tpv)
                            S_S.inc()
                            act(t1buf[rb:rb + 64, kkt, :], bank(2 + hh, 65)[0:64, :],
                                AF.Copy, sem=(S_S, 1))
                            sev = S_S.v
                            p.wait("vector", S_T, tpv)
                            p.wait("vector", S_T, zmm_guard[0])
                            S_V.inc()
                            vop("tensor_copy", zhi[0:1, :], bank(2 + hh, 65)[64:65, :],
                                sem=(S_V, 1))
                            p.wait("vector", S_V, S_V.v)
                            S_V.inc()
                            vop("scalar_tensor_tensor", zlo[0:1, :],
                                bank(2 + hh, 65)[64:65, :], 1.0, zhi[0:1, :],
                                ALU.mult, ALU.subtract, sem=(S_V, 1))
                            vz = S_V.v
                            pvev[hh] = (sev, vz)
                            p.wait("tensor", S_V, vz)
                            mm(pall[0:16, 5 * 512:6 * 512], ehc[0:1, h * 16:(h + 1) * 16],
                               zhi[0:1, :], h == 0, False)
                            S_T.inc()
                            mm(pall[0:16, 5 * 512:6 * 512], ehc[0:1, h * 16:(h + 1) * 16],
                               zlo[0:1, :], False, h == NH - 1, sem=(S_T, 1))
                            zmm_guard[0] = S_T.v
                    zdone = S_T.v
                    barrier()
                    emit_normalize(zdone, lnbuf)
                    emit_bproj(Wo, 0,
                               lambda kk, hf: lnbuf[:, kk, :],
                               NK, NK, None, None, accfn=emit_resadd(s, BO))
                    barrier()

            slotgA = [0, 0, 0]
            slotgB = [0, 0, 0]
            expdone = [0, 0, 0, 0]
            pvfreeA = (0, 0)
            pvfreeB = (0, 0)
            for s in ([] if _KPIPE == 9 else range(NSTR)):
                qsl = slice(s * SW, (s + 1) * SW)
                p.wait("tensor", S_S, quad_free[0][0])
                p.wait("tensor", S_V, quad_free[0][1])
                zpend = None
                for pr in range(NH // 2):
                    ha, hb = 2 * pr, 2 * pr + 1
                    expA = {}
                    expB = {}
                    pvA = {}
                    pvB = {}
                    qkA = {}
                    qkB = {}
                    for kc in range(NC + 1):
                        if _KPIPE == 0 and kc > 0:
                            barrier()
                        if kc < NC:
                            sbA, sbB = kc % 2, 2 + kc % 2
                            ks = slice(kc * 128, (kc + 1) * 128)
                            p.wait("tensor", S_S, expdone[sbA])
                            S_T.inc()
                            mm(bank(sbA), kbuf[0:64, pr, ks],
                               qbuf[0:64, pr, qsl], True, True,
                               sem=(S_T, 1), tp=(0, 0))
                            qkA[kc] = S_T.v
                            p.wait("tensor", S_S, expdone[sbB])
                            S_T.inc()
                            mm(bank(sbB), kbuf[64:128, pr, ks],
                               qbuf[64:128, pr, qsl], True, True,
                               sem=(S_T, 1), tp=(64, 0))
                            qkB[kc] = S_T.v
                        if kc == 2 and zpend is not None:
                            emit_zmms(*zpend)
                            zpend = None
                        if kc >= 1:
                            k2 = kc - 1
                            p.wait("tensor", S_S, expA[k2])
                            if k2 == 0:
                                p.wait("tensor", S_S, pvfreeA[0])
                                p.wait("tensor", S_V, pvfreeA[1])
                            S_T.inc()
                            mm(bank(4, 65), vbuf[:, k2, ha * 65:(ha + 1) * 65],
                               lnbuf[:, k2 % 3, :], k2 == 0, k2 == NC - 1,
                               sem=(S_T, 1))
                            pvA[k2] = S_T.v
                            slotgA[k2 % 3] = S_T.v
                            p.wait("tensor", S_S, expB[k2])
                            if k2 == 0:
                                p.wait("tensor", S_S, pvfreeB[0])
                                p.wait("tensor", S_V, pvfreeB[1])
                            S_T.inc()
                            mm(bank(6, 65), vbuf[:, k2, hb * 65:(hb + 1) * 65],
                               lnbuf[:, 3 + k2 % 3, :], k2 == 0, k2 == NC - 1,
                               sem=(S_T, 1))
                            pvB[k2] = S_T.v
                            slotgB[k2 % 3] = S_T.v
                        if kc < NC:
                            p.wait("scalar", S_T, max(qkA[kc], slotgA[kc % 3]))
                            S_S.inc()
                            act(lnbuf[:, kc % 3, :], bank(kc % 2), AF.Exp,
                                bias=negcB[:, ha:ha + 1], scale=SCALE,
                                sem=(S_S, 1))
                            expA[kc] = S_S.v
                            expdone[kc % 2] = S_S.v
                            p.wait("scalar", S_T, max(qkB[kc], slotgB[kc % 3]))
                            S_S.inc()
                            act(lnbuf[:, 3 + kc % 3, :], bank(2 + kc % 2),
                                AF.Exp, bias=negcB[:, hb:hb + 1], scale=SCALE,
                                sem=(S_S, 1))
                            expB[kc] = S_S.v
                            expdone[2 + kc % 2] = S_S.v
                    p.wait("scalar", S_T, pvA[NC - 1])
                    S_S.inc()
                    act(t1buf[0:64, pr, :], bank(4, 65)[0:64, :], AF.Copy,
                        sem=(S_S, 1))
                    sevA = S_S.v
                    p.wait("scalar", S_T, pvB[NC - 1])
                    S_S.inc()
                    act(t1buf[64:128, pr, :], bank(6, 65)[0:64, :], AF.Copy,
                        sem=(S_S, 1))
                    sevB = S_S.v
                    p.wait("vector", S_T, max(pvA[NC - 1], zguard[0]))
                    S_V.inc()
                    vop("tensor_copy", zhi[0:1, :], bank(4, 65)[64:65, :],
                        sem=(S_V, 1))
                    p.wait("vector", S_V, S_V.v)
                    S_V.inc()
                    vop("scalar_tensor_tensor", zlo[0:1, :],
                        bank(4, 65)[64:65, :], 1.0, zhi[0:1, :],
                        ALU.mult, ALU.subtract, sem=(S_V, 1))
                    vzA = S_V.v
                    p.wait("vector", S_T, pvB[NC - 1])
                    S_V.inc()
                    vop("tensor_copy", zhiB[0:1, :], bank(6, 65)[64:65, :],
                        sem=(S_V, 1))
                    p.wait("vector", S_V, S_V.v)
                    S_V.inc()
                    vop("scalar_tensor_tensor", zloB[0:1, :],
                        bank(6, 65)[64:65, :], 1.0, zhiB[0:1, :],
                        ALU.mult, ALU.subtract, sem=(S_V, 1))
                    vzB = S_V.v
                    pvfreeA = (sevA, vzA)
                    pvfreeB = (sevB, vzB)
                    zpend = (ha, hb, vzB)
                emit_zmms(*zpend)
                zdone = zguard[0]
                barrier()
                if _DBG4 and e == 0 and s == 0:
                    tap2d("d_tmpu", t1buf)
                emit_normalize(zdone, lnbuf)
                if _DBG4 and e == 0 and s == 0:
                    tap_raw("d_zg", rz[:, :])
                    tap2d("d_out", lnbuf)
                emit_bproj(Wo, 0,
                           lambda kk, hf: lnbuf[:, kk, :],
                           NK, NK, None, None, accfn=emit_resadd(s, BO),
                           rhs_ready=(0, 0))
                barrier()

        if n_sub >= 4:
            if n_sub >= 3 and _KPIPE != 9:
                f1slot[0] = max(f1slot[0], zguard[0])
                f1slot[1] = max(f1slot[1], zguard[0])
                for _i in range(4):
                    f2slot[_i] = max(f2slot[_i], zguard[0])
            emit_ffn()

        barrier()
        p.wait("gpsimd", S_V, S_V.v)
        p.wait("gpsimd", S_S, S_S.v)
        for k in range(NK):
            for q4 in range(4):
                S_ST.inc(16)
                dma("gpsimd", outT[e, k * 128:(k + 1) * 128, q4 * 512:(q4 + 1) * 512],
                    x[:, k, q4 * 512:(q4 + 1) * 512], (S_ST, 16))
        x_store_done = S_ST.v

    if _DBG3:
        p.wait("gpsimd", S_T, S_T.v)
        p.wait("gpsimd", S_V, S_V.v)
        p.wait("gpsimd", S_S, S_S.v)
        p.wait("gpsimd", S_ST, S_ST.v)
        p.wait("gpsimd", S_LDF, S_LDF.v)
        for k in range(NK):
            S_LDW.inc(16)
            dma("gpsimd", dbg["d_x"][k * 128:(k + 1) * 128, :],
                x[:, k, 1536:2048], (S_LDW, 16))
            S_LDW.inc(16)
            dma("gpsimd", dbg["d_ln"][k * 128:(k + 1) * 128, :],
                lnbuf[:, k, :], (S_LDW, 16))
            S_LDW.inc(16)
            dma("gpsimd", dbg["d_tmpu"][k * 128:(k + 1) * 128, :],
                t1buf[:, k, :], (S_LDW, 16))
            S_LDW.inc(16)
            dma("gpsimd", dbg["d_q"][k * 128:(k + 1) * 128, :],
                qbuf[:, k, 0:SW], (S_LDW, 16))
            S_LDW.inc(16)
            dma("gpsimd", dbg["d_k"][k * 128:(k + 1) * 128, :],
                kbuf[:, k, 0:SW], (S_LDW, 16))
        S_LDW.inc(16)
        dma("gpsimd", dbg["d_negc"], negcA[:, :], (S_LDW, 16))
        S_LDW.inc(16)
        dma("gpsimd", dbg["d_v"], vbuf[:, 0:4, :], (S_LDW, 16))
        S_LDW.inc(16)
        dma("gpsimd", dbg["d_out"][0:16, :], rz[:, :], (S_LDW, 16))
        p.wait("gpsimd", S_LDW, S_LDW.v)
    p.wait("sync", S_ST, S_ST.v)
    import os
    if os.environ.get("SEMTOTALS"):
        print("SEMTOTALS", dict(ldx=S_LDX.v, ldw=S_LDW.v, ldf=S_LDF.v,
                                t=S_T.v, v=S_V.v, s=S_S.v, st=S_ST.v))
    with nc.Block() as block:
        @block.tensor
        def _(eng):
            for f in p.st["tensor"]:
                f(eng)

        @block.vector
        def _(eng):
            for f in p.st["vector"]:
                f(eng)

        @block.scalar
        def _(eng):
            for f in p.st["scalar"]:
                f(eng)

        @block.sync
        def _(eng):
            for f in p.st["sync"]:
                f(eng)

        @block.gpsimd
        def _(eng):
            for f in p.st["gpsimd"]:
                f(eng)

    stack.close()
    return nc



def _consts():
    onesc = np.ones((1, 128), np.float32)
    monesc = -np.ones((1, 128), np.float32)
    onescol = np.ones((128, 1), np.float32)
    id16 = np.eye(16, dtype=np.float32)
    indn = np.zeros((128, 8 * 16), np.float32)
    for k in range(8):
        for h2 in range(2):
            indn[h2 * 64:(h2 + 1) * 64, k * 16 + 2 * k + h2] = 1.0
    inds48 = np.zeros((48, NH * 64), np.float32)
    for h in range(NH):
        inds48[h, h * 64:(h + 1) * 64] = 1.0
        inds48[32 + h, h * 64:(h + 1) * 64] = 1.0
    sel33 = np.zeros((33, 128), np.float32)
    sel33[0, :] = 1.0
    sel33[32, :] = 1.0
    amask = np.zeros((128, 512), np.float32)
    for c in range(4):
        for b in range(2):
            amask[b * 64:(b + 1) * 64, c * 128 + b * 64: c * 128 + (b + 1) * 64] = 1.0
    ehc = np.zeros((1, 256), np.float32)
    for h in range(NH):
        ehc[0, h * 16 + h] = 1.0
    return dict(onesc=onesc, monesc=monesc, onescol=onescol, id16=id16,
                indn=indn, inds48=inds48, sel33=sel33, amask=amask, ehc=ehc)


def _pack_bias(b, ncol):
    return np.ascontiguousarray(np.asarray(b, np.float32).reshape(ncol, 128).T)


def kernel(L0_attn_a_x, L0_attn_a_W_qkv, L0_attn_a_b_qkv, L0_attn_a_W_out,
           L0_attn_a_b_out, L0_attn_b_Wq, L0_attn_b_bq, L0_attn_b_Wk,
           L0_attn_b_bk, L0_attn_b_Wv, L0_attn_b_bv, L0_attn_b_Wl,
           L0_attn_b_bl, L0_attn_b_Wo, L0_attn_b_bo, L0_ffn_U1, L0_ffn_V1,
           L0_ffn_b1, L0_ffn_U2, L0_ffn_V2, L0_ffn_b2):
    f32 = lambda a: np.ascontiguousarray(np.asarray(a), dtype=np.float32)
    xfull = f32(L0_attn_a_x)
    bqkv = f32(L0_attn_a_b_qkv)
    biasf = np.zeros((128, 112), np.float32)
    biasf[:, BQKV_Q:BQKV_Q + 8] = _pack_bias(bqkv[0:1024], 8)
    biasf[:, BQKV_K:BQKV_K + 8] = _pack_bias(bqkv[1024:2048], 8)
    biasf[:, BQ:BQ + 8] = _pack_bias(L0_attn_b_bq, 8)
    biasf[:, BK:BK + 8] = _pack_bias(L0_attn_b_bk, 8)
    biasf[:, BL:BL + 8] = _pack_bias(L0_attn_b_bl, 8)
    biasf[:, BO:BO + 8] = _pack_bias(L0_attn_b_bo, 8)
    biasf[:, BOUT:BOUT + 8] = _pack_bias(L0_attn_a_b_out, 8)
    biasf[:, B2:B2 + 8] = _pack_bias(L0_ffn_b2, 8)
    biasf[:, B1:B1 + 32] = _pack_bias(L0_ffn_b1, 32)
    biasf[:, BV:BV + 8] = _pack_bias(L0_attn_b_bv, 8)

    common = dict(
        WqkvT=np.ascontiguousarray(f32(L0_attn_a_W_qkv).T),
        WoutT=np.ascontiguousarray(f32(L0_attn_a_W_out).T),
        Wq=f32(L0_attn_b_Wq), Wk=f32(L0_attn_b_Wk), Wv=f32(L0_attn_b_Wv),
        Wl=f32(L0_attn_b_Wl), Wo=f32(L0_attn_b_Wo),
        U1=f32(L0_ffn_U1), V1=f32(L0_ffn_V1), U2=f32(L0_ffn_U2),
        V2=f32(L0_ffn_V2), biasf=biasf,
        bvrow=np.ascontiguousarray(bqkv[2048:3072].reshape(1, DM)),
        blrow=np.ascontiguousarray(f32(L0_attn_b_bl).reshape(1, DM)),
        **_consts(),
    )
    nc = build(_N_SUB)
    in_maps = []
    for c in range(NCORE):
        xs = xfull[c * NE:(c + 1) * NE]
        in_maps.append(dict(common, xT=np.ascontiguousarray(xs.transpose(0, 2, 1))))
    res = run_bass_kernel_spmd(nc, in_maps, core_ids=list(range(NCORE)))
    outs = [np.asarray(res.results[c]["outT"]).transpose(0, 2, 1)
            for c in range(NCORE)]
    return np.ascontiguousarray(np.concatenate(outs, axis=0), dtype=np.float32)


# version 05
# speedup vs baseline: 1.2350x; 1.2350x over previous
import numpy as np
import concourse.bass as bass
import concourse.mybir as mybir
from concourse.bass_utils import run_bass_kernel_spmd

F32 = mybir.dt.float32
BF16 = mybir.dt.bfloat16
AF = mybir.ActivationFunctionType
ALU = mybir.AluOpType
AX = mybir.AxisListType

NCORE = 8
NE = 4
TOK = 2048
DM = 1024
NH = 16
NK = 8
NSTR = 4
SW = 512
NC = 16
SCALE = 0.125
DFF = 4096
RANK = 256
_N_SUB = 4
_DBG = False
_BARR = 0
import os as _os
_BARR = int(_os.environ.get('KBARR', '0'))
_DBG2 = int(_os.environ.get('KDBG2', '0'))
_KPIPE = int(_os.environ.get('KPIPE', '2'))
_DBG3 = int(_os.environ.get('KDBG3', '0'))
_DBG4 = int(_os.environ.get('KDBG4', '0'))

BQKV_Q, BQKV_K, BQ, BK, BL, BO, BOUT, B2, B1, BV = 0, 8, 16, 24, 32, 40, 48, 56, 64, 96


class Sem:
    def __init__(self, h):
        self.h = h
        self.v = 0

    def inc(self, n=1):
        self.v += n
        return self.v


class RotSem:
    def __init__(self, sems):
        self.sems = sems
        self.cur = sems[0]

    @property
    def h(self):
        return self.cur.h

    @property
    def v(self):
        return self.cur.v

    def inc(self, n=1):
        return self.cur.inc(n)

    def rotate(self, i):
        self.cur = self.sems[i]


class Prog:
    ENGINES = ("tensor", "vector", "scalar", "sync", "gpsimd")

    def __init__(self):
        self.st = {e: [] for e in self.ENGINES}

    def emit(self, eng, fn):
        self.st[eng].append(fn)

    def wait(self, eng, sem, val):
        if val > 0:
            self.emit(eng, lambda e, s=sem.h, v=val: e.wait_ge(s, v))


def build(n_sub=4):
    nc = bass.Bass("TRN2", target_bir_lowering=False)
    p = Prog()

    def din(name, shape):
        return nc.dram_tensor(name, list(shape), F32, kind="ExternalInput").ap()

    xT = din("xT", [NE, DM, TOK])
    WqkvT = din("WqkvT", [DM, 3 * DM])
    WoutT = din("WoutT", [DM, DM])
    Wq, Wk, Wv, Wl, Wo = (din(n, [DM, DM]) for n in ("Wq", "Wk", "Wv", "Wl", "Wo"))
    U1 = din("U1", [DM, RANK])
    V1 = din("V1", [RANK, DFF])
    U2 = din("U2", [DFF, RANK])
    V2 = din("V2", [RANK, DM])
    biasf_in = din("biasf", [128, 112])
    bvrow = din("bvrow", [1, DM])
    blrow = din("blrow", [1, DM])
    onesc_in = din("onesc", [1, 128])
    monesc_in = din("monesc", [1, 128])
    onescol_in = din("onescol", [128, 1])
    id16_in = din("id16", [16, 16])
    indn_in = din("indn", [128, 8 * 16])
    inds48_in = din("inds48", [48, NH * 64])
    sel33_in = din("sel33", [33, 128])
    amask_in = din("amask", [128, 512])
    ehc_in = din("ehc", [1, 256])
    outT = nc.dram_tensor("outT", [NE, DM, TOK], F32, kind="ExternalOutput").ap()
    dbg = {}
    if _DBG or _DBG2 or _DBG3 or _DBG4:
        for nm, shape in (("d_ln", [NK * 128, SW]), ("d_q", [NK * 128, SW]),
                          ("d_k", [NK * 128, SW]), ("d_v", [128, 4 * NH * 65]),
                          ("d_negc", [128, 16]), ("d_tmpu", [NK * 128, SW]),
                          ("d_zg", [16, SW]), ("d_out", [NK * 128, SW]),
                          ("d_x", [NK * 128, SW])):
            dbg[nm] = nc.dram_tensor(nm, shape, F32, kind="ExternalOutput").ap()

    from contextlib import ExitStack
    stack = ExitStack()
    E = stack.enter_context
    ledger = [0]

    def sbt(name, shape, dt):
        n = 1
        for s in shape[1:]:
            n *= s
        ledger[0] += n * (2 if dt == BF16 else 4)
        return E(nc.sbuf_tensor(name, list(shape), dt))

    x = sbt("x", [128, NK, TOK], F32)
    qbuf = sbt("qbuf", [128, NK, TOK], BF16)
    kbuf = sbt("kbuf", [128, NK, TOK], BF16)
    vbuf = sbt("vbuf", [128, NC, NH * 65], BF16)
    lnbuf = sbt("lnbuf", [128, NK, SW], BF16)
    t1buf = sbt("t1buf", [128, NK, SW], BF16)
    wbuf = sbt("wbuf", [128, 2048], BF16)
    wffn = sbt("wffn", [128, 1024], BF16)
    bcline = sbt("bcline", [128, DM], BF16)
    rowh = sbt("rowh", [1, DM], BF16)
    zst = sbt("zst", [1, SW], F32)
    rz = sbt("rz", [16, SW], F32)
    rzhl = sbt("rzhl", [48, SW], BF16)
    mrows = sbt("mrows", [128, SW], F32)
    hlr = sbt("hlr", [33, SW], BF16)
    hlm = sbt("hlm", [33, SW], BF16)
    tmpf = sbt("tmpf", [128, SW], F32)
    negcA = sbt("negcA", [128, 16], F32)
    negcB = sbt("negcB", [128, 16], F32)
    smax = sbt("smax", [16, 16], F32)
    lnsc = sbt("lnsc", [16, 1], F32)
    epsc = sbt("epsc", [128, 1], F32)
    biasf = sbt("biasf_s", [128, 112], F32)
    onesc = sbt("onesc_s", [1, 128], BF16)
    monesc = sbt("monesc_s", [1, 128], BF16)
    onescol = sbt("onescol_s", [128, 1], BF16)
    id16 = sbt("id16_s", [16, 16], BF16)
    indn = sbt("indn_s", [128, 8, 16], BF16)
    inds48 = sbt("inds48_s", [48, NH, 64], BF16)
    sel33 = sbt("sel33_s", [33, 128], BF16)
    amask = sbt("amask_s", [128, 512], BF16)
    ehc = sbt("ehc_s", [1, 256], BF16)
    zhi = sbt("zhi", [1, SW], BF16)
    zlo = sbt("zlo", [1, SW], BF16)
    zhiB = wffn[0:1, 0:512]
    zloB = wffn[0:1, 512:1024]
    assert ledger[0] <= 211500, f"SBUF over budget: {ledger[0]}"

    pall = E(nc.psum_tensor("pall", [128, 4096], F32))

    def bank(i, rows=128, w=512):
        return pall[0:rows, i * 512:i * 512 + w]

    S_LDX = Sem(E(nc.semaphore("ldx")))
    S_LDW = RotSem([Sem(E(nc.semaphore(f"ldw{i}"))) for i in range(NE)])
    S_LDF = Sem(E(nc.semaphore("ldf")))
    S_T = Sem(E(nc.semaphore("t")))
    S_V = Sem(E(nc.semaphore("v")))
    S_S = Sem(E(nc.semaphore("s")))
    S_ST = Sem(E(nc.semaphore("st")))

    def mm(out, lhsT, rhs, start, stop, sem=None, tp=None):
        hv = None if sem is None else (sem[0].h, sem[1])
        def f(e, o=out, l=lhsT, r=rhs, a=start, b=stop, sm=hv, t=tp):
            i = e.matmul(o, l, r, start=a, stop=b, tile_position=t)
            if sm is not None:
                i.then_inc(sm[0], sm[1])
        p.emit("tensor", f)

    def act(out, in_, func, bias=0.0, scale=1.0, sem=None):
        if func == AF.Copy and not isinstance(bias, float):
            func = AF.Identity
        hv = None if sem is None else (sem[0].h, sem[1])
        def f(e, o=out, i=in_, fn=func, b=bias, sc=scale, sm=hv):
            ins = e.activation(o, i, fn, bias=b, scale=sc)
            if sm is not None:
                ins.then_inc(sm[0], sm[1])
        p.emit("scalar", f)

    def vop(name, *args, sem=None, **kw):
        hv = None if sem is None else (sem[0].h, sem[1])
        def f(e, nm=name, a=args, sm=hv, k=kw):
            ins = getattr(e, nm)(*a, **k)
            if sm is not None:
                ins.then_inc(sm[0], sm[1])
        p.emit("vector", f)

    def dma(eng, out, in_, sem):
        hv = (sem[0].h, sem[1])
        def f(e, o=out, i=in_, sm=hv):
            e.dma_start(out=o, in_=i).then_inc(sm[0], sm[1])
        p.emit(eng, f)

    def barrier():
        p.wait("tensor", S_V, S_V.v)
        p.wait("tensor", S_S, S_S.v)
        p.wait("vector", S_T, S_T.v)
        p.wait("vector", S_S, S_S.v)
        p.wait("scalar", S_T, S_T.v)
        p.wait("scalar", S_V, S_V.v)

    for dst, src in ((biasf[:, :], biasf_in), (onesc[:, :], onesc_in),
                     (monesc[:, :], monesc_in), (onescol[:, :], onescol_in),
                     (id16[:, :], id16_in), (indn[:, :, :], indn_in),
                     (inds48[:, :, :], inds48_in), (sel33[:, :], sel33_in),
                     (amask[:, :], amask_in), (ehc[:, :], ehc_in)):
        S_LDW.inc(16)
        dma("gpsimd", dst, src, (S_LDW, 16))
    p.wait("vector", S_LDW, S_LDW.v)
    vop("memset", lnsc[:, :], float(np.log(SCALE)) + 0.03)
    vop("memset", epsc[:, :], 1e-5)
    vop("memset", hlr[:, :], 0.0)
    vop("memset", hlm[:, :], 0.0)
    vop("memset", rzhl[:, :], 0.0)
    S_V.inc()
    vop("memset", vbuf[:, :, :], 1.0, sem=(S_V, 1))
    p.wait("vector", S_V, S_V.v)

    wslot = [0, 0]
    quad_free = [(0, 0), (0, 0)]
    zmm_guard = [0]
    ffn_mm_done = [0]
    ra_guard = [0, 0]

    def emit_bcline(row_dram):
        S_LDW.inc(16)
        dma("gpsimd", rowh[:, :], row_dram, (S_LDW, 16))
        barrier()
        p.wait("tensor", S_LDW, S_LDW.v)
        mm(bank(0), onesc[0:1, :], rowh[0:1, 0:512], True, True)
        S_T.inc()
        mm(bank(1), onesc[0:1, :], rowh[0:1, 512:1024], True, True, sem=(S_T, 1))
        p.wait("vector", S_T, S_T.v)
        vop("tensor_copy", bcline[:, 0:512], bank(0))
        S_V.inc()
        vop("tensor_copy", bcline[:, 512:1024], bank(1), sem=(S_V, 1))
        quad_free[0] = (S_S.v, S_V.v)

    def emit_ln(s):
        sl = slice(s * SW, (s + 1) * SW)
        barrier()
        S_V.inc()
        vop("tensor_copy", lnbuf[:, :, :], x[:, :, sl], sem=(S_V, 1))
        p.wait("vector", S_V, S_V.v)
        S_V.inc()
        vop("tensor_mul", t1buf[:, :, :], lnbuf[:, :, :], lnbuf[:, :, :], sem=(S_V, 1))
        p.wait("tensor", S_V, S_V.v)
        for k in range(NK):
            sem = (S_T, 1) if k == NK - 1 else None
            if sem:
                S_T.inc()
            mm(bank(0, 1), onescol[:, 0:1], lnbuf[:, k, :], k == 0, k == NK - 1, sem=sem)
        tm = S_T.v
        for k in range(NK):
            sem = (S_T, 1) if k == NK - 1 else None
            if sem:
                S_T.inc()
            mm(bank(1, 1), onescol[:, 0:1], t1buf[:, k, :], k == 0, k == NK - 1, sem=sem)
        tsq = S_T.v
        p.wait("vector", S_T, tm)
        S_V.inc()
        vop("tensor_copy", mrows[0:1, :], bank(0, 1), sem=(S_V, 1))
        p.wait("vector", S_T, tsq)
        S_V.inc()
        vop("tensor_copy", rz[0:1, :], bank(1, 1), sem=(S_V, 1))
        p.wait("vector", S_V, S_V.v)
        S_V.inc()
        vop("tensor_scalar", mrows[0:1, :], mrows[0:1, :], 1.0 / DM, None, ALU.mult,
            sem=(S_V, 1))
        p.wait("vector", S_V, S_V.v)
        S_V.inc()
        vop("tensor_mul", zst[0:1, :], mrows[0:1, :], mrows[0:1, :], sem=(S_V, 1))
        p.wait("vector", S_V, S_V.v)
        S_V.inc()
        vop("scalar_tensor_tensor", rz[0:1, :], rz[0:1, :], 1.0 / DM,
            zst[0:1, :], ALU.mult, ALU.subtract, sem=(S_V, 1))
        p.wait("scalar", S_V, S_V.v)
        S_S.inc()
        act(rz[0:1, :], rz[0:1, :], AF.Ln, bias=epsc[0:1, 0:1], sem=(S_S, 1))
        p.wait("scalar", S_S, S_S.v)
        S_S.inc()
        act(rz[0:1, :], rz[0:1, :], AF.Exp, scale=-0.5, sem=(S_S, 1))
        p.wait("vector", S_S, S_S.v)
        S_V.inc()
        vop("tensor_mul", mrows[0:1, :], mrows[0:1, :], rz[0:1, :], sem=(S_V, 1))
        p.wait("vector", S_V, S_V.v)
        S_V.inc()
        vop("tensor_copy", hlr[0:1, :], rz[0:1, :], sem=(S_V, 1))
        p.wait("vector", S_V, S_V.v)
        S_V.inc()
        vop("tensor_sub", hlr[32:33, :], rz[0:1, :], hlr[0:1, :], sem=(S_V, 1))
        p.wait("vector", S_V, S_V.v)
        S_V.inc()
        vop("tensor_copy", hlm[0:1, :], mrows[0:1, :], sem=(S_V, 1))
        p.wait("vector", S_V, S_V.v)
        S_V.inc()
        vop("tensor_sub", hlm[32:33, :], mrows[0:1, :], hlm[0:1, :], sem=(S_V, 1))
        p.wait("tensor", S_V, S_V.v)
        mm(bank(2), sel33[:, :], hlr[:, :], True, True)
        S_T.inc()
        mm(bank(3), sel33[:, :], hlm[:, :], True, True, sem=(S_T, 1))
        p.wait("vector", S_T, S_T.v)
        for k in range(NK):
            vop("scalar_tensor_tensor", tmpf[:, :], x[:, k, sl], 1.0,
                bank(2), ALU.mult, ALU.mult)
            sem = (S_V, 1) if k == NK - 1 else None
            if sem:
                S_V.inc()
            vop("tensor_sub", lnbuf[:, k, :], tmpf[:, :], bank(3), sem=sem)
        p.wait("tensor", S_V, S_V.v)
        p.wait("scalar", S_V, S_V.v)
        quad_free[0] = (S_S.v, S_V.v)
        return (S_S.v, S_V.v)

    bankg = [(0, 0)] * 8

    def emit_bproj(wsrc, msl, rhs_fn, nmt, nkk, dst_fn, bias_col, accfn=None,
                   rhs_ready=None):
        tmark = [0] * nmt
        w = nmt * 128
        for kk in range(nkk):
            slot = kk % 2
            wt = wbuf[:, slot * 1024: slot * 1024 + w]
            p.wait("gpsimd", S_T, wslot[slot])
            S_LDW.inc(16)
            dma("gpsimd", wt,
                wsrc[kk * 128:(kk + 1) * 128, msl:msl + w], (S_LDW, 16))
            p.wait("tensor", S_LDW, S_LDW.v)
            if kk == 0:
                rr = rhs_ready if rhs_ready is not None else quad_free[0]
                p.wait("tensor", S_S, max(rr[0], quad_free[1][0]))
                p.wait("tensor", S_V, max(rr[1], quad_free[1][1]))
            rhs = rhs_fn(kk, 0)
            for m in range(nmt):
                if kk == 0:
                    p.wait("tensor", S_S, bankg[m][0])
                    p.wait("tensor", S_V, bankg[m][1])
                last = kk == nkk - 1
                sem = (S_T, 1) if (last or m == nmt - 1) else None
                if sem:
                    S_T.inc()
                mm(bank(m), wt[:, m * 128:(m + 1) * 128], rhs, kk == 0, last,
                   sem=sem)
                if last:
                    tmark[m] = S_T.v
            wslot[slot] = S_T.v
        for m in range(nmt):
            if accfn is not None:
                accfn(m, 0, bank(m), tmark[m])
                bankg[m] = (S_S.v, 0)
            else:
                b = (biasf[:, bias_col + m: bias_col + m + 1]
                     if bias_col is not None else None)
                if m % 2 == 0:
                    p.wait("scalar", S_T, tmark[m])
                    S_S.inc()
                    act(dst_fn(m, 0), bank(m), AF.Copy,
                        bias=(0.0 if b is None else b), sem=(S_S, 1))
                    bankg[m] = (S_S.v, 0)
                else:
                    p.wait("vector", S_T, tmark[m])
                    S_V.inc()
                    if b is None:
                        vop("tensor_copy", dst_fn(m, 0), bank(m), sem=(S_V, 1))
                    else:
                        vop("tensor_single_scalar", dst_fn(m, 0), bank(m), b,
                            ALU.add, sem=(S_V, 1))
                    bankg[m] = (0, S_V.v)
        quad_free[0] = (S_S.v, S_V.v)

    def emit_resadd(s, bias_col):
        def accfn(m, half, ps, tm):
            slot = m % 2
            sl = slice(s * SW, (s + 1) * SW)
            p.wait("scalar", S_T, tm)
            p.wait("scalar", S_V, ra_guard[slot])
            S_S.inc()
            dst = tmpf[:, :] if slot == 0 else mrows[:, :]
            act(dst, ps, AF.Copy,
                bias=biasf[:, bias_col + m: bias_col + m + 1], sem=(S_S, 1))
            p.wait("vector", S_S, S_S.v)
            S_V.inc()
            vop("tensor_add", x[:, m, sl], x[:, m, sl], dst, sem=(S_V, 1))
            ra_guard[slot] = S_V.v
        return accfn

    def emit_vproj(wsrc2d, wcol0, actT, tcglob_fn, bc_add, rhs_ready=None):
        for nh2 in range(2):
            vt = [0] * 4
            for kk in range(NK):
                slot = kk % 2
                wt = wbuf[:, slot * 1024: slot * 1024 + 512]
                c0 = wcol0 + nh2 * 512
                p.wait("gpsimd", S_T, wslot[slot])
                S_LDW.inc(16)
                dma("gpsimd", wt[:, 0:512],
                    wsrc2d[kk * 128:(kk + 1) * 128, c0:c0 + 512], (S_LDW, 16))
                p.wait("tensor", S_LDW, S_LDW.v)
                if kk == 0 and nh2 == 0:
                    rr = rhs_ready if rhs_ready is not None else quad_free[0]
                    p.wait("tensor", S_S, max(rr[0], quad_free[1][0]))
                    p.wait("tensor", S_V, max(rr[1], quad_free[1][1]))
                for tc in range(4):
                    if kk == 0:
                        p.wait("tensor", S_S, bankg[tc][0])
                        p.wait("tensor", S_V, bankg[tc][1])
                    last = kk == NK - 1
                    sem = (S_T, 1) if (last or tc == 3) else None
                    if sem:
                        S_T.inc()
                    mm(bank(tc), actT[:, kk, tc * 128:(tc + 1) * 128],
                       wt[:, 0:512], kk == 0, last, sem=sem)
                    if last:
                        vt[tc] = S_T.v
                wslot[slot] = S_T.v
            for tc in range(4):
                p.wait("vector", S_T, vt[tc])
                tcg = tcglob_fn(tc)
                S_V.inc()
                vop("tensor_add",
                    vbuf[:, tcg, :].rearrange("p (h n) -> p h n", h=NH)
                    [:, nh2 * 8:nh2 * 8 + 8, 0:64],
                    bank(tc).rearrange("p (h n) -> p h n", h=8),
                    bc_add[:, nh2 * 512:nh2 * 512 + 512].rearrange("p (h n) -> p h n", h=8),
                    sem=(S_V, 1))
                bankg[tc] = (0, S_V.v)
        quad_free[0] = (quad_free[0][0], S_V.v)

    def emit_negc(sq_src_fn, n_batches, negc_dst, colq, colk):
        p.wait("scalar", S_V, S_V.v)
        S_S.inc()
        act(smax[:, 10:11], smax[:, colq:colq + 1], AF.Ln, sem=(S_S, 1))
        p.wait("scalar", S_S, S_S.v)
        S_S.inc()
        act(smax[:, 11:12], smax[:, colk:colk + 1], AF.Ln, sem=(S_S, 1))
        p.wait("vector", S_S, S_S.v)
        S_V.inc()
        vop("tensor_add", smax[:, 12:13], smax[:, 10:11], smax[:, 11:12], sem=(S_V, 1))
        p.wait("scalar", S_V, S_V.v)
        S_S.inc()
        act(smax[:, 13:14], smax[:, 12:13], AF.Exp, bias=lnsc[:, 0:1], scale=0.5,
            sem=(S_S, 1))
        p.wait("vector", S_S, S_S.v)
        S_V.inc()
        vop("tensor_copy", rzhl[0:16, 0:1], smax[:, 13:14], sem=(S_V, 1))
        p.wait("tensor", S_V, S_V.v)
        p.wait("tensor", S_S, max(bankg[4][0], bankg[5][0]))
        p.wait("tensor", S_V, max(bankg[4][1], bankg[5][1]))
        S_T.inc()
        mm(pall[0:1, 2048:2064], rzhl[0:16, 0:1], id16[:, :], True, True, sem=(S_T, 1))
        p.wait("scalar", S_T, S_T.v)
        S_S.inc()
        act(rowh[0:1, 0:16], pall[0:1, 2048:2064], AF.Copy, sem=(S_S, 1))
        p.wait("tensor", S_S, S_S.v)
        S_T.inc()
        mm(pall[:, 2560:2576], monesc[0:1, :], rowh[0:1, 0:16], True, True, sem=(S_T, 1))
        p.wait("vector", S_T, S_T.v)
        S_V.inc()
        vop("tensor_copy", negc_dst[:, :], pall[:, 2560:2576], sem=(S_V, 1))
        p.wait("vector", S_V, S_V.v)
        S_V.inc()
        vop("tensor_scalar", negc_dst[:, :], negc_dst[:, :], -60.0, 0.0,
            ALU.max, ALU.min, sem=(S_V, 1))
        p.wait("scalar", S_V, S_V.v)

    def emit_norms(srcT, sl, dstcol):
        p.wait("vector", S_T, S_T.v)
        p.wait("vector", S_S, S_S.v)
        p.wait("vector", S_V, quad_free[0][1])
        S_V.inc()
        vop("tensor_mul", t1buf[:, :, :], srcT, srcT, sem=(S_V, 1))
        p.wait("tensor", S_V, S_V.v)
        p.wait("tensor", S_S, bankg[5][0])
        p.wait("tensor", S_V, bankg[5][1])
        for k in range(NK):
            sem = (S_T, 1) if k == NK - 1 else None
            if sem:
                S_T.inc()
            mm(bank(5, 16), indn[:, k, :], t1buf[:, k, :], k == 0, k == NK - 1, sem=sem)
        p.wait("vector", S_T, S_T.v)
        S_V.inc()
        vop("reduce_max", smax[:, dstcol:dstcol + 1], bank(5, 16), AX.X, sem=(S_V, 1))
        bankg[5] = (0, S_V.v)
        quad_free[1] = (quad_free[1][0], S_V.v)

    def emit_normalize(zdone, outstrip3):
        p.wait("vector", S_T, zdone)
        S_V.inc()
        vop("reciprocal", rz[:, :], pall[0:16, 5 * 512:6 * 512], sem=(S_V, 1))
        zmm_guard[0] = 0
        quad_free[1] = (quad_free[1][0], S_V.v)
        p.wait("vector", S_V, S_V.v)
        S_V.inc()
        vop("tensor_copy", rzhl[0:16, :], rz[:, :], sem=(S_V, 1))
        p.wait("vector", S_V, S_V.v)
        S_V.inc()
        vop("tensor_sub", rzhl[32:48, :], rz[:, :], rzhl[0:16, :], sem=(S_V, 1))
        p.wait("tensor", S_V, S_V.v)
        for h in range(NH):
            kkt, rb = h // 2, (h % 2) * 64
            pdst = pall[rb:rb + 64, 4 * 512:4 * 512 + 512]
            S_T.inc()
            mm(pdst, inds48[:, h, :], rzhl[:, :], True, True, sem=(S_T, 1),
               tp=(0, rb))
            p.wait("vector", S_T, S_T.v)
            S_V.inc()
            vop("tensor_mul", outstrip3[rb:rb + 64, kkt, :],
                t1buf[rb:rb + 64, kkt, :], pdst, sem=(S_V, 1))
            if h < NH - 1:
                p.wait("tensor", S_V, S_V.v)
        p.wait("tensor", S_V, S_V.v)

    def tap2d(nm, buf3, cast=True):
        if not (_DBG or _DBG2 or _DBG3 or _DBG4):
            return
        barrier()
        p.wait("gpsimd", S_T, S_T.v)
        p.wait("gpsimd", S_S, S_S.v)
        p.wait("gpsimd", S_V, S_V.v)
        for k in range(NK):
            S_LDW.inc(16)
            dma("gpsimd", dbg[nm][k * 128:(k + 1) * 128, :], buf3[:, k, 0:SW], (S_LDW, 16))
        p.wait("gpsimd", S_LDW, S_LDW.v)
        for eng in ("tensor", "vector", "scalar"):
            p.wait(eng, S_LDW, S_LDW.v)

    def tap_raw(nm, ap_):
        if not (_DBG or _DBG2 or _DBG3 or _DBG4):
            return
        barrier()
        p.wait("gpsimd", S_T, S_T.v)
        p.wait("gpsimd", S_S, S_S.v)
        p.wait("gpsimd", S_V, S_V.v)
        S_LDW.inc(16)
        dma("gpsimd", dbg[nm], ap_, (S_LDW, 16))
        p.wait("gpsimd", S_LDW, S_LDW.v)
        for eng in ("tensor", "vector", "scalar"):
            p.wait(eng, S_LDW, S_LDW.v)

    x_store_done = 0

    for e in range(NE):
        S_LDW.rotate(e)
        p.wait("sync", S_ST, x_store_done)
        for k in range(NK):
            S_LDX.inc(16)
            dma("sync", x[:, k, :], xT[e, k * 128:(k + 1) * 128, :], (S_LDX, 16))
        p.wait("vector", S_LDX, S_LDX.v)
        p.wait("tensor", S_LDX, S_LDX.v)
        p.wait("scalar", S_LDX, S_LDX.v)

        if n_sub >= 1:
            emit_bcline(bvrow)
            for s in range(NSTR):
                lnr = emit_ln(s)
                if e == 0 and s == 0:
                    tap2d("d_ln", lnbuf)
                emit_bproj(WqkvT, 0,
                           lambda kk, hf: lnbuf[:, kk, :],
                           NK, NK,
                           lambda m, hf: qbuf[:, m, 0:SW],
                           BQKV_Q, rhs_ready=lnr)
                emit_bproj(WqkvT, DM,
                           lambda kk, hf: lnbuf[:, kk, :],
                           NK, NK,
                           lambda m, hf: kbuf[:, m, 0:SW],
                           BQKV_K, rhs_ready=lnr)
                if e == 0 and s == 0:
                    tap2d("d_q", qbuf)
                    tap2d("d_k", kbuf)
                emit_vproj(WqkvT, 2 * DM, lnbuf, lambda tc: tc, bcline,
                           rhs_ready=lnr)
                if e == 0 and s == 0:
                    tap_raw("d_v", vbuf[:, 0:4, :])
                emit_norms(qbuf[:, :, 0:SW], s, 0)
                emit_norms(kbuf[:, :, 0:SW], s, 1)
                emit_negc(None, 1, negcA, 0, 1)
                if e == 0 and s == 0:
                    tap_raw("d_negc", negcA[:, :])
                barrier()
                expm = [0, 0]
                pvev = [(0, 0), (0, 0)]
                for pair in range(NH // 2):
                    for hh in range(2):
                        h = 2 * pair + hh
                        rb = (h % 2) * 64
                        p.wait("tensor", S_S, expm[hh])
                        for c in range(4):
                            sem = (S_T, 1) if c == 3 else None
                            if sem:
                                S_T.inc()
                            mm(bank(hh)[:, c * 128:(c + 1) * 128],
                               kbuf[rb:rb + 64, pair, c * 128:(c + 1) * 128],
                               qbuf[rb:rb + 64, pair, c * 128:(c + 1) * 128],
                               True, True, sem=sem, tp=(rb, 0))
                        tsc = S_T.v
                        p.wait("scalar", S_T, tsc)
                        S_S.inc()
                        act(lnbuf[:, hh, :], bank(hh), AF.Exp,
                            bias=negcA[:, h:h + 1], scale=SCALE, sem=(S_S, 1))
                        expm[hh] = S_S.v
                        p.wait("vector", S_S, S_S.v)
                        S_V.inc()
                        vop("tensor_mul", lnbuf[:, hh, :], lnbuf[:, hh, :],
                            amask[:, :], sem=(S_V, 1))
                        vmask = S_V.v
                        p.wait("tensor", S_V, vmask)
                        p.wait("tensor", S_S, pvev[hh][0])
                        p.wait("tensor", S_V, pvev[hh][1])
                        for c in range(4):
                            sem = (S_T, 1) if c == 3 else None
                            if sem:
                                S_T.inc()
                            mm(bank(2 + hh, 65)[:, c * 128:(c + 1) * 128],
                               vbuf[:, c, h * 65:(h + 1) * 65],
                               lnbuf[:, hh, c * 128:(c + 1) * 128],
                               True, True, sem=sem)
                        tpv = S_T.v
                        kkt = h // 2
                        p.wait("scalar", S_T, tpv)
                        S_S.inc()
                        act(t1buf[rb:rb + 64, kkt, :], bank(2 + hh, 65)[0:64, :],
                            AF.Copy, sem=(S_S, 1))
                        sev = S_S.v
                        p.wait("vector", S_T, tpv)
                        p.wait("vector", S_T, zmm_guard[0])
                        S_V.inc()
                        vop("tensor_copy", zhi[0:1, :], bank(2 + hh, 65)[64:65, :],
                            sem=(S_V, 1))
                        p.wait("vector", S_V, S_V.v)
                        S_V.inc()
                        vop("scalar_tensor_tensor", zlo[0:1, :],
                            bank(2 + hh, 65)[64:65, :], 1.0, zhi[0:1, :],
                            ALU.mult, ALU.subtract, sem=(S_V, 1))
                        vz = S_V.v
                        pvev[hh] = (sev, vz)
                        p.wait("tensor", S_V, vz)
                        mm(pall[0:16, 5 * 512:6 * 512], ehc[0:1, h * 16:(h + 1) * 16],
                           zhi[0:1, :], h == 0, False)
                        S_T.inc()
                        mm(pall[0:16, 5 * 512:6 * 512], ehc[0:1, h * 16:(h + 1) * 16],
                           zlo[0:1, :], False, h == NH - 1, sem=(S_T, 1))
                        zmm_guard[0] = S_T.v
                        if _BARR & 1:
                            barrier()
                    if _BARR & 2:
                        barrier()
                zdone = S_T.v
                barrier()
                if e == 0 and s == 0:
                    tap2d("d_tmpu", t1buf)
                emit_normalize(zdone, lnbuf)
                if e == 0 and s == 0:
                    tap2d("d_out", lnbuf)
                emit_bproj(WoutT, 0,
                           lambda kk, hf: lnbuf[:, kk, :],
                           NK, NK, None, None, accfn=emit_resadd(s, BOUT),
                           rhs_ready=(0, 0))
                if _DBG2 and e == 0 and s == NSTR - 1:
                    tap_raw("d_negc", negcA[:, :])
                    tap2d("d_tmpu", t1buf)
                    tap2d("d_out", lnbuf)
                    tap_raw("d_v", vbuf[:, 12:16, :])
                if e == 0 and s == 0:
                    barrier()
                    p.wait("sync", S_V, S_V.v)
                    if _DBG:
                        for k in range(NK):
                            S_LDW.inc(16)
                            dma("sync", dbg["d_x"][k * 128:(k + 1) * 128, :],
                                x[:, k, 0:SW], (S_LDW, 16))
                        p.wait("sync", S_LDW, S_LDW.v)
            barrier()

        def ffn_tile(i):
            off = 512 + 512 * (i // 8)
            return kbuf[:, i % 8, off:off + 512]

        def emit_ffn_arena(guard):
            p.wait("gpsimd", S_T, guard)
            for a in range(4):
                for c in range(2):
                    S_LDF.inc(16)
                    dma("gpsimd", ffn_tile(16 + a)[:, c * 256:(c + 1) * 256],
                        U1[(2 * a + c) * 128:(2 * a + c + 1) * 128, :],
                        (S_LDF, 16))
            for i in range(16):
                kk2, c = i // 8, i % 8
                S_LDF.inc(16)
                dma("gpsimd", ffn_tile(i),
                    V1[kk2 * 128:(kk2 + 1) * 128, c * 512:(c + 1) * 512],
                    (S_LDF, 16))
            for kk in range(2):
                for h in range(2):
                    S_LDF.inc(16)
                    dma("gpsimd", ffn_tile(20 + kk * 2 + h),
                        V2[kk * 128:(kk + 1) * 128, h * 512:(h + 1) * 512],
                        (S_LDF, 16))
            mark_w = S_LDF.v
            for kk in range(32):
                S_LDF.inc(16)
                dma("gpsimd",
                    vbuf[:, 8 + kk // 4, (kk % 4) * 256:(kk % 4) * 256 + 256],
                    U2[kk * 128:(kk + 1) * 128, :], (S_LDF, 16))
            return mark_w, S_LDF.v

        def emit_ffn(arena_w, arena_u2):
            hsl = qbuf.ap().rearrange("p k (a t) -> p (k a) t", a=4)
            for s in range(NSTR):
                lnr = emit_ln(s)
                if s == 0:
                    p.wait("tensor", S_LDF, arena_w)
                tm = [0, 0]
                for kk in range(NK):
                    lt = ffn_tile(16 + kk // 2)
                    co = (kk % 2) * 256
                    for m in range(2):
                        if kk == 0:
                            p.wait("tensor", S_S,
                                   max(lnr[0], bankg[m][0], quad_free[1][0]))
                            p.wait("tensor", S_V,
                                   max(lnr[1], bankg[m][1], quad_free[1][1]))
                        last = kk == NK - 1
                        sem = (S_T, 1) if (last or m == 1) else None
                        if sem:
                            S_T.inc()
                        mm(bank(m), lt[:, co + m * 128:co + (m + 1) * 128],
                           lnbuf[:, kk, :], kk == 0, last, sem=sem)
                        if last:
                            tm[m] = S_T.v
                p.wait("scalar", S_T, tm[0])
                S_S.inc()
                act(kbuf[:, 0, 0:SW], bank(0), AF.Copy, sem=(S_S, 1))
                bankg[0] = (S_S.v, 0)
                p.wait("vector", S_T, tm[1])
                S_V.inc()
                vop("tensor_copy", kbuf[:, 1, 0:SW], bank(1), sem=(S_V, 1))
                bankg[1] = (0, S_V.v)
                pready = (S_S.v, S_V.v)
                gm = {}
                u2m = 0
                for m in range(33):
                    if m < 32:
                        bk = 4 + (m % 2)
                        if m < 2:
                            p.wait("tensor", S_S, max(pready[0], bankg[bk][0]))
                            p.wait("tensor", S_V, max(pready[1], bankg[bk][1]))
                        else:
                            p.wait("tensor", S_S, gm[m - 2])
                        for kk2 in range(2):
                            ti = kk2 * 8 + m // 4
                            sem = (S_T, 1) if kk2 == 1 else None
                            if sem:
                                S_T.inc()
                            mm(bank(bk),
                               ffn_tile(ti)[:, (m % 4) * 128:(m % 4 + 1) * 128],
                               kbuf[:, kk2, 0:SW], kk2 == 0, kk2 == 1, sem=sem)
                        vmm = S_T.v
                    if m >= 1:
                        m2 = m - 1
                        p.wait("tensor", S_S, gm[m2])
                        if m2 == 0:
                            p.wait("tensor", S_LDF, arena_u2)
                            p.wait("tensor", S_S, max(bankg[6][0], bankg[7][0]))
                            p.wait("tensor", S_V, max(bankg[6][1], bankg[7][1]))
                        u2 = vbuf[:, 8 + m2 // 4,
                                  (m2 % 4) * 256:(m2 % 4) * 256 + 256]
                        mm(bank(6), u2[:, 0:128], hsl[:, m2, :], m2 == 0, m2 == 31)
                        S_T.inc()
                        mm(bank(7), u2[:, 128:256], hsl[:, m2, :], m2 == 0,
                           m2 == 31, sem=(S_T, 1))
                        u2m = S_T.v
                    if m < 32:
                        p.wait("scalar", S_T, vmm)
                        S_S.inc()
                        act(hsl[:, m, :], bank(4 + (m % 2)), AF.Gelu_apprx_tanh,
                            bias=biasf[:, B1 + m:B1 + m + 1], sem=(S_S, 1))
                        gm[m] = S_S.v
                p.wait("scalar", S_T, u2m)
                S_S.inc()
                act(kbuf[:, 2, 0:SW], bank(6), AF.Copy, sem=(S_S, 1))
                bankg[6] = (S_S.v, 0)
                p.wait("vector", S_T, u2m)
                S_V.inc()
                vop("tensor_copy", kbuf[:, 3, 0:SW], bank(7), sem=(S_V, 1))
                bankg[7] = (0, S_V.v)
                qready = (S_S.v, S_V.v)
                bankg[4] = (gm[30], 0)
                bankg[5] = (gm[31], 0)
                tmark = [0] * 8
                ra = emit_resadd(s, B2)
                for kk in range(2):
                    for m in range(8):
                        if kk == 0:
                            p.wait("tensor", S_S, max(qready[0], bankg[m][0]))
                            p.wait("tensor", S_V, max(qready[1], bankg[m][1]))
                        last = kk == 1
                        sem = (S_T, 1) if (last or m == 7) else None
                        if sem:
                            S_T.inc()
                        mm(bank(m),
                           ffn_tile(20 + kk * 2 + m // 4)
                           [:, (m % 4) * 128:(m % 4 + 1) * 128],
                           kbuf[:, 2 + kk, 0:SW], kk == 0, last, sem=sem)
                        if last:
                            tmark[m] = S_T.v
                for m in range(8):
                    ra(m, 0, bank(m), tmark[m])
                    bankg[m] = (S_S.v, 0)
                quad_free[0] = (S_S.v, S_V.v)
            barrier()
            ffn_mm_done[0] = S_T.v

        if n_sub >= 2:
            am1, am2 = emit_ffn_arena(ffn_mm_done[0])
            emit_ffn(am1, am2)

        if n_sub >= 3:
            emit_bcline(blrow)
            p.wait("vector", S_V, S_V.v)
            S_V.inc()
            vop("memset", smax[:, 0:10], 0.0, sem=(S_V, 1))
            p.wait("vector", S_V, S_V.v)
            p.wait("vector", S_T, ffn_mm_done[0])
            S_V.inc()
            vop("memset",
                vbuf[:, 8:16, :].rearrange("p c (h n) -> p (c h) n", h=NH)
                [:, :, 64:65], 1.0, sem=(S_V, 1))
            p.wait("vector", S_V, S_V.v)
            for s in range(NSTR):
                lnr = emit_ln(s)
                sl = slice(s * SW, (s + 1) * SW)
                emit_bproj(Wq, 0,
                           lambda kk, hf: lnbuf[:, kk, :],
                           NK, NK,
                           lambda m, hf, s_=s: qbuf[:, m, s_ * SW:(s_ + 1) * SW],
                           BQ, rhs_ready=lnr)
                emit_norms(qbuf[:, :, sl], s, 2)
                p.wait("vector", S_V, S_V.v)
                S_V.inc()
                vop("tensor_max", smax[:, 0:1], smax[:, 0:1], smax[:, 2:3], sem=(S_V, 1))
                p.wait("vector", S_V, S_V.v)
                emit_bproj(Wk, 0,
                           lambda kk, hf: lnbuf[:, kk, :],
                           NK, NK,
                           lambda m, hf: t1buf[:, m, :],
                           BK, rhs_ready=lnr)
                emit_bproj(Wl, 0,
                           lambda kk, hf: t1buf[:, kk, :],
                           NK, NK,
                           lambda m, hf, s_=s: kbuf[:, m, s_ * SW:(s_ + 1) * SW],
                           BL)
                emit_norms(kbuf[:, :, sl], s, 3)
                p.wait("vector", S_V, S_V.v)
                S_V.inc()
                vop("tensor_max", smax[:, 1:2], smax[:, 1:2], smax[:, 3:4], sem=(S_V, 1))
                p.wait("vector", S_V, S_V.v)
                emit_bproj(Wv, 0,
                           lambda kk, hf: lnbuf[:, kk, :],
                           NK, NK,
                           lambda m, hf: t1buf[:, m, :],
                           BV, rhs_ready=lnr)
                emit_vproj(Wl, 0, t1buf, lambda tc, s_=s: s_ * 4 + tc, bcline)
                if _DBG4 and e == 0 and s == 0:
                    tap2d("d_q", qbuf)
                    tap2d("d_k", kbuf)
                    tap_raw("d_v", vbuf[:, 0:4, :])
            emit_negc(None, 1, negcB, 0, 1)
            barrier()
            zguard = [S_T.v]

            def emit_zmms(ha, hb, vzB_):
                p.wait("tensor", S_V, vzB_)
                mm(pall[0:16, 5 * 512:6 * 512], ehc[0:1, ha * 16:(ha + 1) * 16],
                   zhi[0:1, :], ha == 0, False)
                mm(pall[0:16, 5 * 512:6 * 512], ehc[0:1, ha * 16:(ha + 1) * 16],
                   zlo[0:1, :], False, False)
                mm(pall[0:16, 5 * 512:6 * 512], ehc[0:1, hb * 16:(hb + 1) * 16],
                   zhiB[0:1, :], False, False)
                S_T.inc()
                mm(pall[0:16, 5 * 512:6 * 512], ehc[0:1, hb * 16:(hb + 1) * 16],
                   zloB[0:1, :], False, hb == NH - 1, sem=(S_T, 1))
                zguard[0] = S_T.v

            if _KPIPE == 9:
                for s in range(NSTR):
                    qsl = slice(s * SW, (s + 1) * SW)
                    expm = [0, 0]
                    pvev = [(0, 0), (0, 0)]
                    for pair in range(NH // 2):
                        for hh in range(2):
                            h = 2 * pair + hh
                            rb = (h % 2) * 64
                            p.wait("tensor", S_S, pvev[hh][0])
                            p.wait("tensor", S_V, pvev[hh][1])
                            for kc in range(NC):
                                p.wait("tensor", S_S, expm[hh])
                                S_T.inc()
                                mm(bank(hh),
                                   kbuf[rb:rb + 64, pair, kc * 128:(kc + 1) * 128],
                                   qbuf[rb:rb + 64, pair, qsl], True, True,
                                   sem=(S_T, 1), tp=(rb, 0))
                                p.wait("scalar", S_T, S_T.v)
                                slot = hh * 3 + kc % 3
                                S_S.inc()
                                act(lnbuf[:, slot, :], bank(hh), AF.Exp,
                                    bias=negcB[:, h:h + 1], scale=SCALE, sem=(S_S, 1))
                                expm[hh] = S_S.v
                                p.wait("tensor", S_S, S_S.v)
                                S_T.inc()
                                mm(bank(2 + hh, 65), vbuf[:, kc, h * 65:(h + 1) * 65],
                                   lnbuf[:, slot, :], kc == 0, kc == NC - 1,
                                   sem=(S_T, 1))
                            tpv = S_T.v
                            kkt = h // 2
                            p.wait("scalar", S_T, tpv)
                            S_S.inc()
                            act(t1buf[rb:rb + 64, kkt, :], bank(2 + hh, 65)[0:64, :],
                                AF.Copy, sem=(S_S, 1))
                            sev = S_S.v
                            p.wait("vector", S_T, tpv)
                            p.wait("vector", S_T, zmm_guard[0])
                            S_V.inc()
                            vop("tensor_copy", zhi[0:1, :], bank(2 + hh, 65)[64:65, :],
                                sem=(S_V, 1))
                            p.wait("vector", S_V, S_V.v)
                            S_V.inc()
                            vop("scalar_tensor_tensor", zlo[0:1, :],
                                bank(2 + hh, 65)[64:65, :], 1.0, zhi[0:1, :],
                                ALU.mult, ALU.subtract, sem=(S_V, 1))
                            vz = S_V.v
                            pvev[hh] = (sev, vz)
                            p.wait("tensor", S_V, vz)
                            mm(pall[0:16, 5 * 512:6 * 512], ehc[0:1, h * 16:(h + 1) * 16],
                               zhi[0:1, :], h == 0, False)
                            S_T.inc()
                            mm(pall[0:16, 5 * 512:6 * 512], ehc[0:1, h * 16:(h + 1) * 16],
                               zlo[0:1, :], False, h == NH - 1, sem=(S_T, 1))
                            zmm_guard[0] = S_T.v
                    zdone = S_T.v
                    barrier()
                    emit_normalize(zdone, lnbuf)
                    emit_bproj(Wo, 0,
                               lambda kk, hf: lnbuf[:, kk, :],
                               NK, NK, None, None, accfn=emit_resadd(s, BO))
                    barrier()

            slotgA = [0, 0, 0]
            slotgB = [0, 0, 0]
            expdone = [0, 0, 0, 0]
            pvfreeA = (0, 0)
            pvfreeB = (0, 0)
            for s in ([] if _KPIPE == 9 else range(NSTR)):
                qsl = slice(s * SW, (s + 1) * SW)
                p.wait("tensor", S_S, quad_free[0][0])
                p.wait("tensor", S_V, quad_free[0][1])
                zpend = None
                for pr in range(NH // 2):
                    ha, hb = 2 * pr, 2 * pr + 1
                    expA = {}
                    expB = {}
                    pvA = {}
                    pvB = {}
                    qkA = {}
                    qkB = {}
                    for kc in range(NC + 1):
                        if _KPIPE == 0 and kc > 0:
                            barrier()
                        if kc < NC:
                            sbA, sbB = kc % 2, 2 + kc % 2
                            ks = slice(kc * 128, (kc + 1) * 128)
                            p.wait("tensor", S_S, expdone[sbA])
                            S_T.inc()
                            mm(bank(sbA), kbuf[0:64, pr, ks],
                               qbuf[0:64, pr, qsl], True, True,
                               sem=(S_T, 1), tp=(0, 0))
                            qkA[kc] = S_T.v
                            p.wait("tensor", S_S, expdone[sbB])
                            S_T.inc()
                            mm(bank(sbB), kbuf[64:128, pr, ks],
                               qbuf[64:128, pr, qsl], True, True,
                               sem=(S_T, 1), tp=(64, 0))
                            qkB[kc] = S_T.v
                        if kc == 2 and zpend is not None:
                            emit_zmms(*zpend)
                            zpend = None
                        if kc >= 1:
                            k2 = kc - 1
                            p.wait("tensor", S_S, expA[k2])
                            if k2 == 0:
                                p.wait("tensor", S_S, pvfreeA[0])
                                p.wait("tensor", S_V, pvfreeA[1])
                            S_T.inc()
                            mm(bank(4, 65), vbuf[:, k2, ha * 65:(ha + 1) * 65],
                               lnbuf[:, k2 % 3, :], k2 == 0, k2 == NC - 1,
                               sem=(S_T, 1))
                            pvA[k2] = S_T.v
                            slotgA[k2 % 3] = S_T.v
                            p.wait("tensor", S_S, expB[k2])
                            if k2 == 0:
                                p.wait("tensor", S_S, pvfreeB[0])
                                p.wait("tensor", S_V, pvfreeB[1])
                            S_T.inc()
                            mm(bank(6, 65), vbuf[:, k2, hb * 65:(hb + 1) * 65],
                               lnbuf[:, 3 + k2 % 3, :], k2 == 0, k2 == NC - 1,
                               sem=(S_T, 1))
                            pvB[k2] = S_T.v
                            slotgB[k2 % 3] = S_T.v
                        if kc < NC:
                            p.wait("scalar", S_T, max(qkA[kc], slotgA[kc % 3]))
                            S_S.inc()
                            act(lnbuf[:, kc % 3, :], bank(kc % 2), AF.Exp,
                                bias=negcB[:, ha:ha + 1], scale=SCALE,
                                sem=(S_S, 1))
                            expA[kc] = S_S.v
                            expdone[kc % 2] = S_S.v
                            p.wait("scalar", S_T, max(qkB[kc], slotgB[kc % 3]))
                            S_S.inc()
                            act(lnbuf[:, 3 + kc % 3, :], bank(2 + kc % 2),
                                AF.Exp, bias=negcB[:, hb:hb + 1], scale=SCALE,
                                sem=(S_S, 1))
                            expB[kc] = S_S.v
                            expdone[2 + kc % 2] = S_S.v
                    p.wait("scalar", S_T, pvA[NC - 1])
                    S_S.inc()
                    act(t1buf[0:64, pr, :], bank(4, 65)[0:64, :], AF.Copy,
                        sem=(S_S, 1))
                    sevA = S_S.v
                    p.wait("scalar", S_T, pvB[NC - 1])
                    S_S.inc()
                    act(t1buf[64:128, pr, :], bank(6, 65)[0:64, :], AF.Copy,
                        sem=(S_S, 1))
                    sevB = S_S.v
                    p.wait("vector", S_T, max(pvA[NC - 1], zguard[0]))
                    S_V.inc()
                    vop("tensor_copy", zhi[0:1, :], bank(4, 65)[64:65, :],
                        sem=(S_V, 1))
                    p.wait("vector", S_V, S_V.v)
                    S_V.inc()
                    vop("scalar_tensor_tensor", zlo[0:1, :],
                        bank(4, 65)[64:65, :], 1.0, zhi[0:1, :],
                        ALU.mult, ALU.subtract, sem=(S_V, 1))
                    vzA = S_V.v
                    p.wait("vector", S_T, pvB[NC - 1])
                    S_V.inc()
                    vop("tensor_copy", zhiB[0:1, :], bank(6, 65)[64:65, :],
                        sem=(S_V, 1))
                    p.wait("vector", S_V, S_V.v)
                    S_V.inc()
                    vop("scalar_tensor_tensor", zloB[0:1, :],
                        bank(6, 65)[64:65, :], 1.0, zhiB[0:1, :],
                        ALU.mult, ALU.subtract, sem=(S_V, 1))
                    vzB = S_V.v
                    pvfreeA = (sevA, vzA)
                    pvfreeB = (sevB, vzB)
                    zpend = (ha, hb, vzB)
                emit_zmms(*zpend)
                zdone = zguard[0]
                barrier()
                if _DBG4 and e == 0 and s == 0:
                    tap2d("d_tmpu", t1buf)
                emit_normalize(zdone, lnbuf)
                if _DBG4 and e == 0 and s == 0:
                    tap_raw("d_zg", rz[:, :])
                    tap2d("d_out", lnbuf)
                emit_bproj(Wo, 0,
                           lambda kk, hf: lnbuf[:, kk, :],
                           NK, NK, None, None, accfn=emit_resadd(s, BO),
                           rhs_ready=(0, 0))
                barrier()

        if n_sub >= 4:
            am1, am2 = emit_ffn_arena(max(zguard[0], zmm_guard[0])
                                      if n_sub >= 3 else ffn_mm_done[0])
            emit_ffn(am1, am2)

        barrier()
        p.wait("gpsimd", S_V, S_V.v)
        p.wait("gpsimd", S_S, S_S.v)
        for k in range(NK):
            for q4 in range(4):
                S_ST.inc(16)
                dma("gpsimd", outT[e, k * 128:(k + 1) * 128, q4 * 512:(q4 + 1) * 512],
                    x[:, k, q4 * 512:(q4 + 1) * 512], (S_ST, 16))
        x_store_done = S_ST.v

    if _DBG3:
        p.wait("gpsimd", S_T, S_T.v)
        p.wait("gpsimd", S_V, S_V.v)
        p.wait("gpsimd", S_S, S_S.v)
        p.wait("gpsimd", S_ST, S_ST.v)
        p.wait("gpsimd", S_LDF, S_LDF.v)
        for k in range(NK):
            S_LDW.inc(16)
            dma("gpsimd", dbg["d_x"][k * 128:(k + 1) * 128, :],
                x[:, k, 1536:2048], (S_LDW, 16))
            S_LDW.inc(16)
            dma("gpsimd", dbg["d_ln"][k * 128:(k + 1) * 128, :],
                lnbuf[:, k, :], (S_LDW, 16))
            S_LDW.inc(16)
            dma("gpsimd", dbg["d_tmpu"][k * 128:(k + 1) * 128, :],
                t1buf[:, k, :], (S_LDW, 16))
            S_LDW.inc(16)
            dma("gpsimd", dbg["d_q"][k * 128:(k + 1) * 128, :],
                qbuf[:, k, 0:SW], (S_LDW, 16))
            S_LDW.inc(16)
            dma("gpsimd", dbg["d_k"][k * 128:(k + 1) * 128, :],
                kbuf[:, k, 0:SW], (S_LDW, 16))
        S_LDW.inc(16)
        dma("gpsimd", dbg["d_negc"], negcA[:, :], (S_LDW, 16))
        S_LDW.inc(16)
        dma("gpsimd", dbg["d_v"], vbuf[:, 0:4, :], (S_LDW, 16))
        S_LDW.inc(16)
        dma("gpsimd", dbg["d_out"][0:16, :], rz[:, :], (S_LDW, 16))
        p.wait("gpsimd", S_LDW, S_LDW.v)
    p.wait("sync", S_ST, S_ST.v)
    import os
    if os.environ.get("SEMTOTALS"):
        print("SEMTOTALS", dict(ldx=S_LDX.v, ldw=S_LDW.v, ldf=S_LDF.v,
                                t=S_T.v, v=S_V.v, s=S_S.v, st=S_ST.v))
    with nc.Block() as block:
        @block.tensor
        def _(eng):
            for f in p.st["tensor"]:
                f(eng)

        @block.vector
        def _(eng):
            for f in p.st["vector"]:
                f(eng)

        @block.scalar
        def _(eng):
            for f in p.st["scalar"]:
                f(eng)

        @block.sync
        def _(eng):
            for f in p.st["sync"]:
                f(eng)

        @block.gpsimd
        def _(eng):
            for f in p.st["gpsimd"]:
                f(eng)

    stack.close()
    return nc



def _consts():
    onesc = np.ones((1, 128), np.float32)
    monesc = -np.ones((1, 128), np.float32)
    onescol = np.ones((128, 1), np.float32)
    id16 = np.eye(16, dtype=np.float32)
    indn = np.zeros((128, 8 * 16), np.float32)
    for k in range(8):
        for h2 in range(2):
            indn[h2 * 64:(h2 + 1) * 64, k * 16 + 2 * k + h2] = 1.0
    inds48 = np.zeros((48, NH * 64), np.float32)
    for h in range(NH):
        inds48[h, h * 64:(h + 1) * 64] = 1.0
        inds48[32 + h, h * 64:(h + 1) * 64] = 1.0
    sel33 = np.zeros((33, 128), np.float32)
    sel33[0, :] = 1.0
    sel33[32, :] = 1.0
    amask = np.zeros((128, 512), np.float32)
    for c in range(4):
        for b in range(2):
            amask[b * 64:(b + 1) * 64, c * 128 + b * 64: c * 128 + (b + 1) * 64] = 1.0
    ehc = np.zeros((1, 256), np.float32)
    for h in range(NH):
        ehc[0, h * 16 + h] = 1.0
    return dict(onesc=onesc, monesc=monesc, onescol=onescol, id16=id16,
                indn=indn, inds48=inds48, sel33=sel33, amask=amask, ehc=ehc)


def _pack_bias(b, ncol):
    return np.ascontiguousarray(np.asarray(b, np.float32).reshape(ncol, 128).T)


def kernel(L0_attn_a_x, L0_attn_a_W_qkv, L0_attn_a_b_qkv, L0_attn_a_W_out,
           L0_attn_a_b_out, L0_attn_b_Wq, L0_attn_b_bq, L0_attn_b_Wk,
           L0_attn_b_bk, L0_attn_b_Wv, L0_attn_b_bv, L0_attn_b_Wl,
           L0_attn_b_bl, L0_attn_b_Wo, L0_attn_b_bo, L0_ffn_U1, L0_ffn_V1,
           L0_ffn_b1, L0_ffn_U2, L0_ffn_V2, L0_ffn_b2):
    f32 = lambda a: np.ascontiguousarray(np.asarray(a), dtype=np.float32)
    xfull = f32(L0_attn_a_x)
    bqkv = f32(L0_attn_a_b_qkv)
    biasf = np.zeros((128, 112), np.float32)
    biasf[:, BQKV_Q:BQKV_Q + 8] = _pack_bias(bqkv[0:1024], 8)
    biasf[:, BQKV_K:BQKV_K + 8] = _pack_bias(bqkv[1024:2048], 8)
    biasf[:, BQ:BQ + 8] = _pack_bias(L0_attn_b_bq, 8)
    biasf[:, BK:BK + 8] = _pack_bias(L0_attn_b_bk, 8)
    biasf[:, BL:BL + 8] = _pack_bias(L0_attn_b_bl, 8)
    biasf[:, BO:BO + 8] = _pack_bias(L0_attn_b_bo, 8)
    biasf[:, BOUT:BOUT + 8] = _pack_bias(L0_attn_a_b_out, 8)
    biasf[:, B2:B2 + 8] = _pack_bias(L0_ffn_b2, 8)
    biasf[:, B1:B1 + 32] = _pack_bias(L0_ffn_b1, 32)
    biasf[:, BV:BV + 8] = _pack_bias(L0_attn_b_bv, 8)

    common = dict(
        WqkvT=np.ascontiguousarray(f32(L0_attn_a_W_qkv).T),
        WoutT=np.ascontiguousarray(f32(L0_attn_a_W_out).T),
        Wq=f32(L0_attn_b_Wq), Wk=f32(L0_attn_b_Wk), Wv=f32(L0_attn_b_Wv),
        Wl=f32(L0_attn_b_Wl), Wo=f32(L0_attn_b_Wo),
        U1=f32(L0_ffn_U1), V1=f32(L0_ffn_V1), U2=f32(L0_ffn_U2),
        V2=f32(L0_ffn_V2), biasf=biasf,
        bvrow=np.ascontiguousarray(bqkv[2048:3072].reshape(1, DM)),
        blrow=np.ascontiguousarray(f32(L0_attn_b_bl).reshape(1, DM)),
        **_consts(),
    )
    nc = build(_N_SUB)
    in_maps = []
    for c in range(NCORE):
        xs = xfull[c * NE:(c + 1) * NE]
        in_maps.append(dict(common, xT=np.ascontiguousarray(xs.transpose(0, 2, 1))))
    res = run_bass_kernel_spmd(nc, in_maps, core_ids=list(range(NCORE)))
    outs = [np.asarray(res.results[c]["outT"]).transpose(0, 2, 1)
            for c in range(NCORE)]
    return np.ascontiguousarray(np.concatenate(outs, axis=0), dtype=np.float32)


# version 06
# speedup vs baseline: 1.2993x; 1.0521x over previous
import numpy as np
import concourse.bass as bass
import concourse.mybir as mybir
from concourse.bass_utils import run_bass_kernel_spmd

F32 = mybir.dt.float32
BF16 = mybir.dt.bfloat16
AF = mybir.ActivationFunctionType
ALU = mybir.AluOpType
AX = mybir.AxisListType

NCORE = 8
NE = 4
TOK = 2048
DM = 1024
NH = 16
NK = 8
NSTR = 4
SW = 512
NC = 16
SCALE = 0.125
DFF = 4096
RANK = 256
_N_SUB = 4
_DBG = False
_BARR = 0
import os as _os
_BARR = int(_os.environ.get('KBARR', '0'))
_DBG2 = int(_os.environ.get('KDBG2', '0'))
_KPIPE = int(_os.environ.get('KPIPE', '2'))
_DBG3 = int(_os.environ.get('KDBG3', '0'))
_DBG4 = int(_os.environ.get('KDBG4', '0'))

BQKV_Q, BQKV_K, BQ, BK, BL, BO, BOUT, B2, B1, BV = 0, 8, 16, 24, 32, 40, 48, 56, 64, 96


class Sem:
    def __init__(self, h):
        self.h = h
        self.v = 0

    def inc(self, n=1):
        self.v += n
        return self.v


class RotSem:
    def __init__(self, sems):
        self.sems = sems
        self.cur = sems[0]

    @property
    def h(self):
        return self.cur.h

    @property
    def v(self):
        return self.cur.v

    def inc(self, n=1):
        return self.cur.inc(n)

    def rotate(self, i):
        self.cur = self.sems[i]


class Prog:
    ENGINES = ("tensor", "vector", "scalar", "sync", "gpsimd")

    def __init__(self):
        self.st = {e: [] for e in self.ENGINES}

    def emit(self, eng, fn):
        self.st[eng].append(fn)

    def wait(self, eng, sem, val):
        if val > 0:
            self.emit(eng, lambda e, s=sem.h, v=val: e.wait_ge(s, v))


def build(n_sub=4):
    nc = bass.Bass("TRN2", target_bir_lowering=False)
    p = Prog()

    def din(name, shape):
        return nc.dram_tensor(name, list(shape), F32, kind="ExternalInput").ap()

    xT = din("xT", [NE, DM, TOK])
    WqkvT = din("WqkvT", [DM, 3 * DM])
    WoutT = din("WoutT", [DM, DM])
    Wq, Wk, Wv, Wl, Wo = (din(n, [DM, DM]) for n in ("Wq", "Wk", "Wv", "Wl", "Wo"))
    U1 = din("U1", [DM, RANK])
    V1 = din("V1", [RANK, DFF])
    U2 = din("U2", [DFF, RANK])
    V2 = din("V2", [RANK, DM])
    biasf_in = din("biasf", [128, 112])
    bvrow = din("bvrow", [1, DM])
    blrow = din("blrow", [1, DM])
    onesc_in = din("onesc", [1, 128])
    monesc_in = din("monesc", [1, 128])
    onescol_in = din("onescol", [128, 1])
    id16_in = din("id16", [16, 16])
    indn_in = din("indn", [128, 8 * 16])
    inds48_in = din("inds48", [48, NH * 64])
    sel33_in = din("sel33", [33, 128])
    amask_in = din("amask", [128, 512])
    ehc_in = din("ehc", [1, 256])
    outT = nc.dram_tensor("outT", [NE, DM, TOK], F32, kind="ExternalOutput").ap()
    dbg = {}
    if _DBG or _DBG2 or _DBG3 or _DBG4:
        for nm, shape in (("d_ln", [NK * 128, SW]), ("d_q", [NK * 128, SW]),
                          ("d_k", [NK * 128, SW]), ("d_v", [128, 4 * NH * 65]),
                          ("d_negc", [128, 16]), ("d_tmpu", [NK * 128, SW]),
                          ("d_zg", [16, SW]), ("d_out", [NK * 128, SW]),
                          ("d_x", [NK * 128, SW])):
            dbg[nm] = nc.dram_tensor(nm, shape, F32, kind="ExternalOutput").ap()

    from contextlib import ExitStack
    stack = ExitStack()
    E = stack.enter_context
    ledger = [0]

    def sbt(name, shape, dt):
        n = 1
        for s in shape[1:]:
            n *= s
        ledger[0] += n * (2 if dt == BF16 else 4)
        return E(nc.sbuf_tensor(name, list(shape), dt))

    x = sbt("x", [128, NK, TOK], F32)
    qbuf = sbt("qbuf", [128, NK, TOK], BF16)
    kbuf = sbt("kbuf", [128, NK, TOK], BF16)
    vbuf = sbt("vbuf", [128, NC, NH * 65], BF16)
    lnbuf = sbt("lnbuf", [128, NK, SW], BF16)
    t1buf = sbt("t1buf", [128, NK, SW], BF16)
    wbuf = sbt("wbuf", [128, 2048], BF16)
    wffn = sbt("wffn", [128, 1024], BF16)
    bcline = sbt("bcline", [128, DM], BF16)
    rowh = sbt("rowh", [1, DM], BF16)
    zst = sbt("zst", [1, SW], F32)
    rz = sbt("rz", [16, SW], F32)
    rzhl = sbt("rzhl", [48, SW], BF16)
    mrows = sbt("mrows", [128, SW], F32)
    hlr = sbt("hlr", [33, SW], BF16)
    hlm = sbt("hlm", [33, SW], BF16)
    tmpf = sbt("tmpf", [128, SW], F32)
    negcA = sbt("negcA", [128, 16], F32)
    negcB = sbt("negcB", [128, 16], F32)
    smax = sbt("smax", [16, 16], F32)
    lnsc = sbt("lnsc", [16, 1], F32)
    epsc = sbt("epsc", [128, 1], F32)
    biasf = sbt("biasf_s", [128, 112], F32)
    onesc = sbt("onesc_s", [1, 128], BF16)
    monesc = sbt("monesc_s", [1, 128], BF16)
    onescol = sbt("onescol_s", [128, 1], BF16)
    id16 = sbt("id16_s", [16, 16], BF16)
    indn = sbt("indn_s", [128, 8, 16], BF16)
    inds48 = sbt("inds48_s", [48, NH, 64], BF16)
    sel33 = sbt("sel33_s", [33, 128], BF16)
    amask = sbt("amask_s", [128, 512], BF16)
    ehc = sbt("ehc_s", [1, 256], BF16)
    zhi = sbt("zhi", [1, SW], BF16)
    zlo = sbt("zlo", [1, SW], BF16)
    zhiB = wffn[0:1, 0:512]
    zloB = wffn[0:1, 512:1024]
    assert ledger[0] <= 211500, f"SBUF over budget: {ledger[0]}"

    pall = E(nc.psum_tensor("pall", [128, 4096], F32))

    def bank(i, rows=128, w=512):
        return pall[0:rows, i * 512:i * 512 + w]

    S_LDX = Sem(E(nc.semaphore("ldx")))
    S_LDW = RotSem([Sem(E(nc.semaphore(f"ldw{i}"))) for i in range(NE)])
    S_LDF = Sem(E(nc.semaphore("ldf")))
    S_T = Sem(E(nc.semaphore("t")))
    S_V = Sem(E(nc.semaphore("v")))
    S_S = Sem(E(nc.semaphore("s")))
    S_ST = Sem(E(nc.semaphore("st")))
    S_G = Sem(E(nc.semaphore("g")))

    def mm(out, lhsT, rhs, start, stop, sem=None, tp=None):
        hv = None if sem is None else (sem[0].h, sem[1])
        def f(e, o=out, l=lhsT, r=rhs, a=start, b=stop, sm=hv, t=tp):
            i = e.matmul(o, l, r, start=a, stop=b, tile_position=t)
            if sm is not None:
                i.then_inc(sm[0], sm[1])
        p.emit("tensor", f)

    def act(out, in_, func, bias=0.0, scale=1.0, sem=None):
        if func == AF.Copy and not isinstance(bias, float):
            func = AF.Identity
        hv = None if sem is None else (sem[0].h, sem[1])
        def f(e, o=out, i=in_, fn=func, b=bias, sc=scale, sm=hv):
            ins = e.activation(o, i, fn, bias=b, scale=sc)
            if sm is not None:
                ins.then_inc(sm[0], sm[1])
        p.emit("scalar", f)

    def vop(name, *args, sem=None, **kw):
        hv = None if sem is None else (sem[0].h, sem[1])
        def f(e, nm=name, a=args, sm=hv, k=kw):
            ins = getattr(e, nm)(*a, **k)
            if sm is not None:
                ins.then_inc(sm[0], sm[1])
        p.emit("vector", f)

    def dma(eng, out, in_, sem):
        hv = (sem[0].h, sem[1])
        def f(e, o=out, i=in_, sm=hv):
            e.dma_start(out=o, in_=i).then_inc(sm[0], sm[1])
        p.emit(eng, f)

    def gvop(name, *args, sem=None, **kw):
        hv = None if sem is None else (sem[0].h, sem[1])
        def f(e, nm=name, a=args, sm=hv, k=kw):
            ins = getattr(e, nm)(*a, **k)
            if sm is not None:
                ins.then_inc(sm[0], sm[1])
        p.emit("gpsimd", f)

    def barrier():
        p.wait("tensor", S_V, S_V.v)
        p.wait("tensor", S_S, S_S.v)
        p.wait("vector", S_T, S_T.v)
        p.wait("vector", S_S, S_S.v)
        p.wait("scalar", S_T, S_T.v)
        p.wait("scalar", S_V, S_V.v)

    for dst, src in ((biasf[:, :], biasf_in), (onesc[:, :], onesc_in),
                     (monesc[:, :], monesc_in), (onescol[:, :], onescol_in),
                     (id16[:, :], id16_in), (indn[:, :, :], indn_in),
                     (inds48[:, :, :], inds48_in), (sel33[:, :], sel33_in),
                     (amask[:, :], amask_in), (ehc[:, :], ehc_in)):
        S_LDW.inc(16)
        dma("gpsimd", dst, src, (S_LDW, 16))
    p.wait("vector", S_LDW, S_LDW.v)
    vop("memset", lnsc[:, :], float(np.log(SCALE)) + 0.03)
    vop("memset", epsc[:, :], 1e-5)
    vop("memset", hlr[:, :], 0.0)
    vop("memset", hlm[:, :], 0.0)
    vop("memset", rzhl[:, :], 0.0)
    S_V.inc()
    vop("memset", vbuf[:, :, :], 1.0, sem=(S_V, 1))
    p.wait("vector", S_V, S_V.v)

    wslot = [0, 0]
    quad_free = [(0, 0), (0, 0)]
    zmm_guard = [0]
    ffn_mm_done = [0]
    ra_guard = [0, 0]

    def emit_bcline(row_dram):
        S_LDW.inc(16)
        dma("gpsimd", rowh[:, :], row_dram, (S_LDW, 16))
        barrier()
        p.wait("tensor", S_LDW, S_LDW.v)
        mm(bank(0), onesc[0:1, :], rowh[0:1, 0:512], True, True)
        S_T.inc()
        mm(bank(1), onesc[0:1, :], rowh[0:1, 512:1024], True, True, sem=(S_T, 1))
        p.wait("vector", S_T, S_T.v)
        vop("tensor_copy", bcline[:, 0:512], bank(0))
        S_V.inc()
        vop("tensor_copy", bcline[:, 512:1024], bank(1), sem=(S_V, 1))
        quad_free[0] = (S_S.v, S_V.v)

    def emit_ln(s):
        sl = slice(s * SW, (s + 1) * SW)
        barrier()
        cm = {}
        sqm = {}
        for k in range(NK):
            S_V.inc()
            vop("tensor_copy", lnbuf[:, k, :], x[:, k, sl], sem=(S_V, 1))
            cm[k] = S_V.v
            p.wait("scalar", S_V, cm[k])
            S_S.inc()
            act(t1buf[:, k, :], lnbuf[:, k, :], AF.Square, sem=(S_S, 1))
            sqm[k] = S_S.v
        for k in range(NK):
            p.wait("tensor", S_V, cm[k])
            sem = (S_T, 1) if k == NK - 1 else None
            if sem:
                S_T.inc()
            mm(bank(0, 1), onescol[:, 0:1], lnbuf[:, k, :], k == 0, k == NK - 1, sem=sem)
            if k == NK - 1:
                tm = S_T.v
            p.wait("tensor", S_S, sqm[k])
            sem = (S_T, 1) if k == NK - 1 else None
            if sem:
                S_T.inc()
            mm(bank(1, 1), onescol[:, 0:1], t1buf[:, k, :], k == 0, k == NK - 1, sem=sem)
        tsq = S_T.v
        p.wait("vector", S_T, tm)
        S_V.inc()
        vop("tensor_copy", mrows[0:1, :], bank(0, 1), sem=(S_V, 1))
        p.wait("vector", S_T, tsq)
        S_V.inc()
        vop("tensor_copy", rz[0:1, :], bank(1, 1), sem=(S_V, 1))
        p.wait("vector", S_V, S_V.v)
        S_V.inc()
        vop("tensor_scalar", mrows[0:1, :], mrows[0:1, :], 1.0 / DM, None, ALU.mult,
            sem=(S_V, 1))
        p.wait("vector", S_V, S_V.v)
        S_V.inc()
        vop("tensor_mul", zst[0:1, :], mrows[0:1, :], mrows[0:1, :], sem=(S_V, 1))
        p.wait("vector", S_V, S_V.v)
        S_V.inc()
        vop("scalar_tensor_tensor", rz[0:1, :], rz[0:1, :], 1.0 / DM,
            zst[0:1, :], ALU.mult, ALU.subtract, sem=(S_V, 1))
        p.wait("scalar", S_V, S_V.v)
        S_S.inc()
        act(rz[0:1, :], rz[0:1, :], AF.Ln, bias=epsc[0:1, 0:1], sem=(S_S, 1))
        p.wait("scalar", S_S, S_S.v)
        S_S.inc()
        act(rz[0:1, :], rz[0:1, :], AF.Exp, scale=-0.5, sem=(S_S, 1))
        p.wait("vector", S_S, S_S.v)
        S_V.inc()
        vop("tensor_mul", mrows[0:1, :], mrows[0:1, :], rz[0:1, :], sem=(S_V, 1))
        p.wait("vector", S_V, S_V.v)
        S_V.inc()
        vop("tensor_copy", hlr[0:1, :], rz[0:1, :], sem=(S_V, 1))
        p.wait("vector", S_V, S_V.v)
        S_V.inc()
        vop("tensor_sub", hlr[32:33, :], rz[0:1, :], hlr[0:1, :], sem=(S_V, 1))
        p.wait("vector", S_V, S_V.v)
        S_V.inc()
        vop("tensor_copy", hlm[0:1, :], mrows[0:1, :], sem=(S_V, 1))
        p.wait("vector", S_V, S_V.v)
        S_V.inc()
        vop("tensor_sub", hlm[32:33, :], mrows[0:1, :], hlm[0:1, :], sem=(S_V, 1))
        p.wait("tensor", S_V, S_V.v)
        mm(bank(2), sel33[:, :], hlr[:, :], True, True)
        S_T.inc()
        mm(bank(3), sel33[:, :], hlm[:, :], True, True, sem=(S_T, 1))
        p.wait("vector", S_T, S_T.v)
        for k in range(NK):
            vop("scalar_tensor_tensor", tmpf[:, :], x[:, k, sl], 1.0,
                bank(2), ALU.mult, ALU.mult)
            sem = (S_V, 1) if k == NK - 1 else None
            if sem:
                S_V.inc()
            vop("tensor_sub", lnbuf[:, k, :], tmpf[:, :], bank(3), sem=sem)
        p.wait("tensor", S_V, S_V.v)
        p.wait("scalar", S_V, S_V.v)
        quad_free[0] = (S_S.v, S_V.v)
        return (S_S.v, S_V.v)

    bankg = [(0, 0)] * 8

    def emit_bproj(wsrc, msl, rhs_fn, nmt, nkk, dst_fn, bias_col, accfn=None,
                   rhs_ready=None):
        tmark = [0] * nmt
        w = nmt * 128
        for kk in range(nkk):
            slot = kk % 2
            wt = wbuf[:, slot * 1024: slot * 1024 + w]
            p.wait("gpsimd", S_T, wslot[slot])
            S_LDW.inc(16)
            dma("gpsimd", wt,
                wsrc[kk * 128:(kk + 1) * 128, msl:msl + w], (S_LDW, 16))
            p.wait("tensor", S_LDW, S_LDW.v)
            if kk == 0:
                rr = rhs_ready if rhs_ready is not None else quad_free[0]
                p.wait("tensor", S_S, max(rr[0], quad_free[1][0]))
                p.wait("tensor", S_V, max(rr[1], quad_free[1][1]))
            rhs = rhs_fn(kk, 0)
            for m in range(nmt):
                if kk == 0:
                    p.wait("tensor", S_S, bankg[m][0])
                    p.wait("tensor", S_V, bankg[m][1])
                last = kk == nkk - 1
                sem = (S_T, 1) if (last or m == nmt - 1) else None
                if sem:
                    S_T.inc()
                mm(bank(m), wt[:, m * 128:(m + 1) * 128], rhs, kk == 0, last,
                   sem=sem)
                if last:
                    tmark[m] = S_T.v
            wslot[slot] = S_T.v
        for m in range(nmt):
            if accfn is not None:
                accfn(m, 0, bank(m), tmark[m])
                bankg[m] = (S_S.v, 0)
            else:
                b = (biasf[:, bias_col + m: bias_col + m + 1]
                     if bias_col is not None else None)
                if m % 2 == 0:
                    p.wait("scalar", S_T, tmark[m])
                    S_S.inc()
                    act(dst_fn(m, 0), bank(m), AF.Copy,
                        bias=(0.0 if b is None else b), sem=(S_S, 1))
                    bankg[m] = (S_S.v, 0)
                else:
                    p.wait("vector", S_T, tmark[m])
                    S_V.inc()
                    if b is None:
                        vop("tensor_copy", dst_fn(m, 0), bank(m), sem=(S_V, 1))
                    else:
                        vop("tensor_single_scalar", dst_fn(m, 0), bank(m), b,
                            ALU.add, sem=(S_V, 1))
                    bankg[m] = (0, S_V.v)
        quad_free[0] = (S_S.v, S_V.v)

    def emit_resadd(s, bias_col):
        def accfn(m, half, ps, tm):
            slot = m % 2
            sl = slice(s * SW, (s + 1) * SW)
            p.wait("scalar", S_T, tm)
            p.wait("scalar", S_V, ra_guard[slot])
            S_S.inc()
            dst = tmpf[:, :] if slot == 0 else mrows[:, :]
            act(dst, ps, AF.Copy,
                bias=biasf[:, bias_col + m: bias_col + m + 1], sem=(S_S, 1))
            p.wait("vector", S_S, S_S.v)
            S_V.inc()
            vop("tensor_add", x[:, m, sl], x[:, m, sl], dst, sem=(S_V, 1))
            ra_guard[slot] = S_V.v
        return accfn

    def emit_vproj(wsrc2d, wcol0, actT, tcglob_fn, bc_add, rhs_ready=None):
        for nh2 in range(2):
            vt = [0] * 4
            for kk in range(NK):
                slot = kk % 2
                wt = wbuf[:, slot * 1024: slot * 1024 + 512]
                c0 = wcol0 + nh2 * 512
                p.wait("gpsimd", S_T, wslot[slot])
                S_LDW.inc(16)
                dma("gpsimd", wt[:, 0:512],
                    wsrc2d[kk * 128:(kk + 1) * 128, c0:c0 + 512], (S_LDW, 16))
                p.wait("tensor", S_LDW, S_LDW.v)
                if kk == 0 and nh2 == 0:
                    rr = rhs_ready if rhs_ready is not None else quad_free[0]
                    p.wait("tensor", S_S, max(rr[0], quad_free[1][0]))
                    p.wait("tensor", S_V, max(rr[1], quad_free[1][1]))
                for tc in range(4):
                    if kk == 0:
                        p.wait("tensor", S_S, bankg[tc][0])
                        p.wait("tensor", S_V, bankg[tc][1])
                    last = kk == NK - 1
                    sem = (S_T, 1) if (last or tc == 3) else None
                    if sem:
                        S_T.inc()
                    mm(bank(tc), actT[:, kk, tc * 128:(tc + 1) * 128],
                       wt[:, 0:512], kk == 0, last, sem=sem)
                    if last:
                        vt[tc] = S_T.v
                wslot[slot] = S_T.v
            for tc in range(4):
                p.wait("vector", S_T, vt[tc])
                tcg = tcglob_fn(tc)
                S_V.inc()
                vop("tensor_add",
                    vbuf[:, tcg, :].rearrange("p (h n) -> p h n", h=NH)
                    [:, nh2 * 8:nh2 * 8 + 8, 0:64],
                    bank(tc).rearrange("p (h n) -> p h n", h=8),
                    bc_add[:, nh2 * 512:nh2 * 512 + 512].rearrange("p (h n) -> p h n", h=8),
                    sem=(S_V, 1))
                bankg[tc] = (0, S_V.v)
        quad_free[0] = (quad_free[0][0], S_V.v)

    def emit_negc(sq_src_fn, n_batches, negc_dst, colq, colk):
        p.wait("scalar", S_V, S_V.v)
        S_S.inc()
        act(smax[:, 10:11], smax[:, colq:colq + 1], AF.Ln, sem=(S_S, 1))
        p.wait("scalar", S_S, S_S.v)
        S_S.inc()
        act(smax[:, 11:12], smax[:, colk:colk + 1], AF.Ln, sem=(S_S, 1))
        p.wait("vector", S_S, S_S.v)
        S_V.inc()
        vop("tensor_add", smax[:, 12:13], smax[:, 10:11], smax[:, 11:12], sem=(S_V, 1))
        p.wait("scalar", S_V, S_V.v)
        S_S.inc()
        act(smax[:, 13:14], smax[:, 12:13], AF.Exp, bias=lnsc[:, 0:1], scale=0.5,
            sem=(S_S, 1))
        p.wait("vector", S_S, S_S.v)
        S_V.inc()
        vop("tensor_copy", rzhl[0:16, 0:1], smax[:, 13:14], sem=(S_V, 1))
        p.wait("tensor", S_V, S_V.v)
        p.wait("tensor", S_S, max(bankg[4][0], bankg[5][0]))
        p.wait("tensor", S_V, max(bankg[4][1], bankg[5][1]))
        S_T.inc()
        mm(pall[0:1, 2048:2064], rzhl[0:16, 0:1], id16[:, :], True, True, sem=(S_T, 1))
        p.wait("scalar", S_T, S_T.v)
        S_S.inc()
        act(rowh[0:1, 0:16], pall[0:1, 2048:2064], AF.Copy, sem=(S_S, 1))
        p.wait("tensor", S_S, S_S.v)
        S_T.inc()
        mm(pall[:, 2560:2576], monesc[0:1, :], rowh[0:1, 0:16], True, True, sem=(S_T, 1))
        p.wait("vector", S_T, S_T.v)
        S_V.inc()
        vop("tensor_copy", negc_dst[:, :], pall[:, 2560:2576], sem=(S_V, 1))
        p.wait("vector", S_V, S_V.v)
        S_V.inc()
        vop("tensor_scalar", negc_dst[:, :], negc_dst[:, :], -60.0, 0.0,
            ALU.max, ALU.min, sem=(S_V, 1))
        p.wait("scalar", S_V, S_V.v)

    def emit_norms(srcT, sl, dstcol):
        p.wait("vector", S_T, S_T.v)
        p.wait("vector", S_S, S_S.v)
        p.wait("vector", S_V, quad_free[0][1])
        S_V.inc()
        vop("tensor_mul", t1buf[:, :, :], srcT, srcT, sem=(S_V, 1))
        p.wait("tensor", S_V, S_V.v)
        p.wait("tensor", S_S, bankg[5][0])
        p.wait("tensor", S_V, bankg[5][1])
        for k in range(NK):
            sem = (S_T, 1) if k == NK - 1 else None
            if sem:
                S_T.inc()
            mm(bank(5, 16), indn[:, k, :], t1buf[:, k, :], k == 0, k == NK - 1, sem=sem)
        p.wait("vector", S_T, S_T.v)
        S_V.inc()
        vop("reduce_max", smax[:, dstcol:dstcol + 1], bank(5, 16), AX.X, sem=(S_V, 1))
        bankg[5] = (0, S_V.v)
        quad_free[1] = (quad_free[1][0], S_V.v)

    def emit_normalize(zdone, outstrip3):
        p.wait("vector", S_T, zdone)
        S_V.inc()
        vop("reciprocal", rz[:, :], pall[0:16, 5 * 512:6 * 512], sem=(S_V, 1))
        zmm_guard[0] = 0
        quad_free[1] = (quad_free[1][0], S_V.v)
        p.wait("vector", S_V, S_V.v)
        S_V.inc()
        vop("tensor_copy", rzhl[0:16, :], rz[:, :], sem=(S_V, 1))
        p.wait("vector", S_V, S_V.v)
        S_V.inc()
        vop("tensor_sub", rzhl[32:48, :], rz[:, :], rzhl[0:16, :], sem=(S_V, 1))
        p.wait("tensor", S_V, S_V.v)
        for h in range(NH):
            kkt, rb = h // 2, (h % 2) * 64
            pdst = pall[rb:rb + 64, 4 * 512:4 * 512 + 512]
            S_T.inc()
            mm(pdst, inds48[:, h, :], rzhl[:, :], True, True, sem=(S_T, 1),
               tp=(0, rb))
            p.wait("vector", S_T, S_T.v)
            S_V.inc()
            vop("tensor_mul", outstrip3[rb:rb + 64, kkt, :],
                t1buf[rb:rb + 64, kkt, :], pdst, sem=(S_V, 1))
            if h < NH - 1:
                p.wait("tensor", S_V, S_V.v)
        p.wait("tensor", S_V, S_V.v)

    def tap2d(nm, buf3, cast=True):
        if not (_DBG or _DBG2 or _DBG3 or _DBG4):
            return
        barrier()
        p.wait("gpsimd", S_T, S_T.v)
        p.wait("gpsimd", S_S, S_S.v)
        p.wait("gpsimd", S_V, S_V.v)
        for k in range(NK):
            S_LDW.inc(16)
            dma("gpsimd", dbg[nm][k * 128:(k + 1) * 128, :], buf3[:, k, 0:SW], (S_LDW, 16))
        p.wait("gpsimd", S_LDW, S_LDW.v)
        for eng in ("tensor", "vector", "scalar"):
            p.wait(eng, S_LDW, S_LDW.v)

    def tap_raw(nm, ap_):
        if not (_DBG or _DBG2 or _DBG3 or _DBG4):
            return
        barrier()
        p.wait("gpsimd", S_T, S_T.v)
        p.wait("gpsimd", S_S, S_S.v)
        p.wait("gpsimd", S_V, S_V.v)
        S_LDW.inc(16)
        dma("gpsimd", dbg[nm], ap_, (S_LDW, 16))
        p.wait("gpsimd", S_LDW, S_LDW.v)
        for eng in ("tensor", "vector", "scalar"):
            p.wait(eng, S_LDW, S_LDW.v)

    x_store_done = 0

    for e in range(NE):
        S_LDW.rotate(e)
        p.wait("sync", S_ST, x_store_done)
        for k in range(NK):
            S_LDX.inc(16)
            dma("sync", x[:, k, :], xT[e, k * 128:(k + 1) * 128, :], (S_LDX, 16))
        p.wait("vector", S_LDX, S_LDX.v)
        p.wait("tensor", S_LDX, S_LDX.v)
        p.wait("scalar", S_LDX, S_LDX.v)

        if n_sub >= 1:
            emit_bcline(bvrow)
            azg = [S_T.v, S_T.v]
            expdA = [0, 0]
            pvfA = [(0, 0), (0, 0)]
            slotg3 = [0, 0, 0]
            for s in range(NSTR):
                lnr = emit_ln(s)
                if e == 0 and s == 0:
                    tap2d("d_ln", lnbuf)
                emit_bproj(WqkvT, 0,
                           lambda kk, hf: lnbuf[:, kk, :],
                           NK, NK,
                           lambda m, hf: qbuf[:, m, 0:SW],
                           BQKV_Q, rhs_ready=lnr)
                emit_bproj(WqkvT, DM,
                           lambda kk, hf: lnbuf[:, kk, :],
                           NK, NK,
                           lambda m, hf: kbuf[:, m, 0:SW],
                           BQKV_K, rhs_ready=lnr)
                if e == 0 and s == 0:
                    tap2d("d_q", qbuf)
                    tap2d("d_k", kbuf)
                emit_vproj(WqkvT, 2 * DM, lnbuf, lambda tc: tc, bcline,
                           rhs_ready=lnr)
                if e == 0 and s == 0:
                    tap_raw("d_v", vbuf[:, 0:4, :])
                emit_norms(qbuf[:, :, 0:SW], s, 0)
                emit_norms(kbuf[:, :, 0:SW], s, 1)
                emit_negc(None, 1, negcA, 0, 1)
                if e == 0 and s == 0:
                    tap_raw("d_negc", negcA[:, :])
                barrier()
                p.wait("tensor", S_S, quad_free[0][0])
                p.wait("tensor", S_V, quad_free[0][1])

                def emit_azmm(h2_, vzh_):
                    zh_, zl_ = (zhi, zlo) if h2_ % 2 == 0 else (zhiB, zloB)
                    p.wait("tensor", S_V, vzh_)
                    mm(pall[0:16, 5 * 512:6 * 512],
                       ehc[0:1, h2_ * 16:(h2_ + 1) * 16], zh_[0:1, :],
                       h2_ == 0, False)
                    S_T.inc()
                    mm(pall[0:16, 5 * 512:6 * 512],
                       ehc[0:1, h2_ * 16:(h2_ + 1) * 16], zl_[0:1, :],
                       False, h2_ == NH - 1, sem=(S_T, 1))
                    azg[h2_ % 2] = S_T.v

                qkm = {}
                expm_ = {}
                maskm = {}
                pvm = {}
                zq = []
                for h in range(NH + 1):
                    if h < NH:
                        sb = h % 2
                        pair, rb = h // 2, (h % 2) * 64
                        p.wait("tensor", S_S, expdA[sb])
                        for c in range(4):
                            sem = (S_T, 1) if c == 3 else None
                            if sem:
                                S_T.inc()
                            mm(bank(sb)[:, c * 128:(c + 1) * 128],
                               kbuf[rb:rb + 64, pair, c * 128:(c + 1) * 128],
                               qbuf[rb:rb + 64, pair, c * 128:(c + 1) * 128],
                               True, True, sem=sem, tp=(rb, 0))
                        qkm[h] = S_T.v
                    while zq and zq[0][0] <= h - 2:
                        emit_azmm(*zq.pop(0))
                    if h >= 1:
                        h2 = h - 1
                        pb = 2 + h2 % 2
                        rb2 = (h2 % 2) * 64
                        p.wait("tensor", S_G, maskm[h2])
                        p.wait("tensor", S_S, pvfA[h2 % 2][0])
                        p.wait("tensor", S_V, pvfA[h2 % 2][1])
                        for c in range(4):
                            sem = (S_T, 1) if c == 3 else None
                            if sem:
                                S_T.inc()
                            mm(bank(pb, 65)[:, c * 128:(c + 1) * 128],
                               vbuf[:, c, h2 * 65:(h2 + 1) * 65],
                               lnbuf[:, h2 % 3, c * 128:(c + 1) * 128],
                               True, True, sem=sem)
                        pvm[h2] = S_T.v
                        slotg3[h2 % 3] = S_T.v
                        p.wait("scalar", S_T, pvm[h2])
                        S_S.inc()
                        act(t1buf[rb2:rb2 + 64, h2 // 2, :],
                            bank(pb, 65)[0:64, :], AF.Copy, sem=(S_S, 1))
                        sevh = S_S.v
                        p.wait("vector", S_T, max(pvm[h2], azg[h2 % 2]))
                        zh, zl = (zhi, zlo) if h2 % 2 == 0 else (zhiB, zloB)
                        S_V.inc()
                        vop("tensor_copy", zh[0:1, :], bank(pb, 65)[64:65, :],
                            sem=(S_V, 1))
                        p.wait("vector", S_V, S_V.v)
                        S_V.inc()
                        vop("scalar_tensor_tensor", zl[0:1, :],
                            bank(pb, 65)[64:65, :], 1.0, zh[0:1, :],
                            ALU.mult, ALU.subtract, sem=(S_V, 1))
                        vzh = S_V.v
                        pvfA[h2 % 2] = (sevh, vzh)
                        zq.append((h2, vzh))
                    if h < NH:
                        p.wait("scalar", S_T, max(qkm[h], slotg3[h % 3]))
                        S_S.inc()
                        act(lnbuf[:, h % 3, :], bank(h % 2), AF.Exp,
                            bias=negcA[:, h:h + 1], scale=SCALE, sem=(S_S, 1))
                        expm_[h] = S_S.v
                        expdA[h % 2] = S_S.v
                        p.wait("gpsimd", S_S, expm_[h])
                        S_G.inc()
                        gvop("tensor_mul", lnbuf[:, h % 3, :], lnbuf[:, h % 3, :],
                             amask[:, :], sem=(S_G, 1))
                        maskm[h] = S_G.v
                while zq:
                    emit_azmm(*zq.pop(0))
                zdone = azg[(NH - 1) % 2]
                barrier()
                if e == 0 and s == 0:
                    tap2d("d_tmpu", t1buf)
                emit_normalize(zdone, lnbuf)
                if e == 0 and s == 0:
                    tap2d("d_out", lnbuf)
                emit_bproj(WoutT, 0,
                           lambda kk, hf: lnbuf[:, kk, :],
                           NK, NK, None, None, accfn=emit_resadd(s, BOUT),
                           rhs_ready=(0, 0))
                if _DBG2 and e == 0 and s == NSTR - 1:
                    tap_raw("d_negc", negcA[:, :])
                    tap2d("d_tmpu", t1buf)
                    tap2d("d_out", lnbuf)
                    tap_raw("d_v", vbuf[:, 12:16, :])
                if e == 0 and s == 0:
                    barrier()
                    p.wait("sync", S_V, S_V.v)
                    if _DBG:
                        for k in range(NK):
                            S_LDW.inc(16)
                            dma("sync", dbg["d_x"][k * 128:(k + 1) * 128, :],
                                x[:, k, 0:SW], (S_LDW, 16))
                        p.wait("sync", S_LDW, S_LDW.v)
            barrier()

        def ffn_tile(i):
            off = 512 + 512 * (i // 8)
            return kbuf[:, i % 8, off:off + 512]

        def emit_ffn_arena(guard):
            p.wait("gpsimd", S_T, guard)
            for a in range(4):
                for c in range(2):
                    S_LDF.inc(16)
                    dma("gpsimd", ffn_tile(16 + a)[:, c * 256:(c + 1) * 256],
                        U1[(2 * a + c) * 128:(2 * a + c + 1) * 128, :],
                        (S_LDF, 16))
            for i in range(16):
                kk2, c = i // 8, i % 8
                S_LDF.inc(16)
                dma("gpsimd", ffn_tile(i),
                    V1[kk2 * 128:(kk2 + 1) * 128, c * 512:(c + 1) * 512],
                    (S_LDF, 16))
            for kk in range(2):
                for h in range(2):
                    S_LDF.inc(16)
                    dma("gpsimd", ffn_tile(20 + kk * 2 + h),
                        V2[kk * 128:(kk + 1) * 128, h * 512:(h + 1) * 512],
                        (S_LDF, 16))
            mark_w = S_LDF.v
            for kk in range(32):
                S_LDF.inc(16)
                dma("gpsimd",
                    vbuf[:, 8 + kk // 4, (kk % 4) * 256:(kk % 4) * 256 + 256],
                    U2[kk * 128:(kk + 1) * 128, :], (S_LDF, 16))
            return mark_w, S_LDF.v

        def emit_ffn(arena_w, arena_u2):
            hsl = qbuf.ap().rearrange("p k (a t) -> p (k a) t", a=4)
            for s in range(NSTR):
                lnr = emit_ln(s)
                if s == 0:
                    p.wait("tensor", S_LDF, arena_w)
                tm = [0, 0]
                for kk in range(NK):
                    lt = ffn_tile(16 + kk // 2)
                    co = (kk % 2) * 256
                    for m in range(2):
                        if kk == 0:
                            p.wait("tensor", S_S,
                                   max(lnr[0], bankg[m][0], quad_free[1][0]))
                            p.wait("tensor", S_V,
                                   max(lnr[1], bankg[m][1], quad_free[1][1]))
                        last = kk == NK - 1
                        sem = (S_T, 1) if (last or m == 1) else None
                        if sem:
                            S_T.inc()
                        mm(bank(m), lt[:, co + m * 128:co + (m + 1) * 128],
                           lnbuf[:, kk, :], kk == 0, last, sem=sem)
                        if last:
                            tm[m] = S_T.v
                p.wait("scalar", S_T, tm[0])
                S_S.inc()
                act(kbuf[:, 0, 0:SW], bank(0), AF.Copy, sem=(S_S, 1))
                bankg[0] = (S_S.v, 0)
                p.wait("vector", S_T, tm[1])
                S_V.inc()
                vop("tensor_copy", kbuf[:, 1, 0:SW], bank(1), sem=(S_V, 1))
                bankg[1] = (0, S_V.v)
                pready = (S_S.v, S_V.v)
                gm = {}
                u2m = 0
                for m in range(33):
                    if m < 32:
                        bk = 4 + (m % 2)
                        if m < 2:
                            p.wait("tensor", S_S, max(pready[0], bankg[bk][0]))
                            p.wait("tensor", S_V, max(pready[1], bankg[bk][1]))
                        else:
                            p.wait("tensor", S_S, gm[m - 2])
                        for kk2 in range(2):
                            ti = kk2 * 8 + m // 4
                            sem = (S_T, 1) if kk2 == 1 else None
                            if sem:
                                S_T.inc()
                            mm(bank(bk),
                               ffn_tile(ti)[:, (m % 4) * 128:(m % 4 + 1) * 128],
                               kbuf[:, kk2, 0:SW], kk2 == 0, kk2 == 1, sem=sem)
                        vmm = S_T.v
                    if m >= 1:
                        m2 = m - 1
                        p.wait("tensor", S_S, gm[m2])
                        if m2 == 0:
                            p.wait("tensor", S_LDF, arena_u2)
                            p.wait("tensor", S_S, max(bankg[6][0], bankg[7][0]))
                            p.wait("tensor", S_V, max(bankg[6][1], bankg[7][1]))
                        u2 = vbuf[:, 8 + m2 // 4,
                                  (m2 % 4) * 256:(m2 % 4) * 256 + 256]
                        mm(bank(6), u2[:, 0:128], hsl[:, m2, :], m2 == 0, m2 == 31)
                        S_T.inc()
                        mm(bank(7), u2[:, 128:256], hsl[:, m2, :], m2 == 0,
                           m2 == 31, sem=(S_T, 1))
                        u2m = S_T.v
                    if m < 32:
                        p.wait("scalar", S_T, vmm)
                        S_S.inc()
                        act(hsl[:, m, :], bank(4 + (m % 2)), AF.Gelu_apprx_tanh,
                            bias=biasf[:, B1 + m:B1 + m + 1], sem=(S_S, 1))
                        gm[m] = S_S.v
                p.wait("scalar", S_T, u2m)
                S_S.inc()
                act(kbuf[:, 2, 0:SW], bank(6), AF.Copy, sem=(S_S, 1))
                bankg[6] = (S_S.v, 0)
                p.wait("vector", S_T, u2m)
                S_V.inc()
                vop("tensor_copy", kbuf[:, 3, 0:SW], bank(7), sem=(S_V, 1))
                bankg[7] = (0, S_V.v)
                qready = (S_S.v, S_V.v)
                bankg[4] = (gm[30], 0)
                bankg[5] = (gm[31], 0)
                tmark = [0] * 8
                ra = emit_resadd(s, B2)
                for kk in range(2):
                    for m in range(8):
                        if kk == 0:
                            p.wait("tensor", S_S, max(qready[0], bankg[m][0]))
                            p.wait("tensor", S_V, max(qready[1], bankg[m][1]))
                        last = kk == 1
                        sem = (S_T, 1) if (last or m == 7) else None
                        if sem:
                            S_T.inc()
                        mm(bank(m),
                           ffn_tile(20 + kk * 2 + m // 4)
                           [:, (m % 4) * 128:(m % 4 + 1) * 128],
                           kbuf[:, 2 + kk, 0:SW], kk == 0, last, sem=sem)
                        if last:
                            tmark[m] = S_T.v
                for m in range(8):
                    ra(m, 0, bank(m), tmark[m])
                    bankg[m] = (S_S.v, 0)
                quad_free[0] = (S_S.v, S_V.v)
            barrier()
            ffn_mm_done[0] = S_T.v

        if n_sub >= 2:
            am1, am2 = emit_ffn_arena(ffn_mm_done[0])
            emit_ffn(am1, am2)

        if n_sub >= 3:
            emit_bcline(blrow)
            p.wait("vector", S_V, S_V.v)
            S_V.inc()
            vop("memset", smax[:, 0:10], 0.0, sem=(S_V, 1))
            p.wait("vector", S_V, S_V.v)
            p.wait("vector", S_T, ffn_mm_done[0])
            S_V.inc()
            vop("memset",
                vbuf[:, 8:16, :].rearrange("p c (h n) -> p (c h) n", h=NH)
                [:, :, 64:65], 1.0, sem=(S_V, 1))
            p.wait("vector", S_V, S_V.v)
            for s in range(NSTR):
                lnr = emit_ln(s)
                sl = slice(s * SW, (s + 1) * SW)
                emit_bproj(Wq, 0,
                           lambda kk, hf: lnbuf[:, kk, :],
                           NK, NK,
                           lambda m, hf, s_=s: qbuf[:, m, s_ * SW:(s_ + 1) * SW],
                           BQ, rhs_ready=lnr)
                emit_norms(qbuf[:, :, sl], s, 2)
                p.wait("vector", S_V, S_V.v)
                S_V.inc()
                vop("tensor_max", smax[:, 0:1], smax[:, 0:1], smax[:, 2:3], sem=(S_V, 1))
                p.wait("vector", S_V, S_V.v)
                emit_bproj(Wk, 0,
                           lambda kk, hf: lnbuf[:, kk, :],
                           NK, NK,
                           lambda m, hf: t1buf[:, m, :],
                           BK, rhs_ready=lnr)
                emit_bproj(Wl, 0,
                           lambda kk, hf: t1buf[:, kk, :],
                           NK, NK,
                           lambda m, hf, s_=s: kbuf[:, m, s_ * SW:(s_ + 1) * SW],
                           BL)
                emit_norms(kbuf[:, :, sl], s, 3)
                p.wait("vector", S_V, S_V.v)
                S_V.inc()
                vop("tensor_max", smax[:, 1:2], smax[:, 1:2], smax[:, 3:4], sem=(S_V, 1))
                p.wait("vector", S_V, S_V.v)
                emit_bproj(Wv, 0,
                           lambda kk, hf: lnbuf[:, kk, :],
                           NK, NK,
                           lambda m, hf: t1buf[:, m, :],
                           BV, rhs_ready=lnr)
                emit_vproj(Wl, 0, t1buf, lambda tc, s_=s: s_ * 4 + tc, bcline)
                if _DBG4 and e == 0 and s == 0:
                    tap2d("d_q", qbuf)
                    tap2d("d_k", kbuf)
                    tap_raw("d_v", vbuf[:, 0:4, :])
            emit_negc(None, 1, negcB, 0, 1)
            barrier()
            zguard = [S_T.v]

            def emit_zmms(ha, hb, vzB_):
                p.wait("tensor", S_V, vzB_)
                mm(pall[0:16, 5 * 512:6 * 512], ehc[0:1, ha * 16:(ha + 1) * 16],
                   zhi[0:1, :], ha == 0, False)
                mm(pall[0:16, 5 * 512:6 * 512], ehc[0:1, ha * 16:(ha + 1) * 16],
                   zlo[0:1, :], False, False)
                mm(pall[0:16, 5 * 512:6 * 512], ehc[0:1, hb * 16:(hb + 1) * 16],
                   zhiB[0:1, :], False, False)
                S_T.inc()
                mm(pall[0:16, 5 * 512:6 * 512], ehc[0:1, hb * 16:(hb + 1) * 16],
                   zloB[0:1, :], False, hb == NH - 1, sem=(S_T, 1))
                zguard[0] = S_T.v

            if _KPIPE == 9:
                for s in range(NSTR):
                    qsl = slice(s * SW, (s + 1) * SW)
                    expm = [0, 0]
                    pvev = [(0, 0), (0, 0)]
                    for pair in range(NH // 2):
                        for hh in range(2):
                            h = 2 * pair + hh
                            rb = (h % 2) * 64
                            p.wait("tensor", S_S, pvev[hh][0])
                            p.wait("tensor", S_V, pvev[hh][1])
                            for kc in range(NC):
                                p.wait("tensor", S_S, expm[hh])
                                S_T.inc()
                                mm(bank(hh),
                                   kbuf[rb:rb + 64, pair, kc * 128:(kc + 1) * 128],
                                   qbuf[rb:rb + 64, pair, qsl], True, True,
                                   sem=(S_T, 1), tp=(rb, 0))
                                p.wait("scalar", S_T, S_T.v)
                                slot = hh * 3 + kc % 3
                                S_S.inc()
                                act(lnbuf[:, slot, :], bank(hh), AF.Exp,
                                    bias=negcB[:, h:h + 1], scale=SCALE, sem=(S_S, 1))
                                expm[hh] = S_S.v
                                p.wait("tensor", S_S, S_S.v)
                                S_T.inc()
                                mm(bank(2 + hh, 65), vbuf[:, kc, h * 65:(h + 1) * 65],
                                   lnbuf[:, slot, :], kc == 0, kc == NC - 1,
                                   sem=(S_T, 1))
                            tpv = S_T.v
                            kkt = h // 2
                            p.wait("scalar", S_T, tpv)
                            S_S.inc()
                            act(t1buf[rb:rb + 64, kkt, :], bank(2 + hh, 65)[0:64, :],
                                AF.Copy, sem=(S_S, 1))
                            sev = S_S.v
                            p.wait("vector", S_T, tpv)
                            p.wait("vector", S_T, zmm_guard[0])
                            S_V.inc()
                            vop("tensor_copy", zhi[0:1, :], bank(2 + hh, 65)[64:65, :],
                                sem=(S_V, 1))
                            p.wait("vector", S_V, S_V.v)
                            S_V.inc()
                            vop("scalar_tensor_tensor", zlo[0:1, :],
                                bank(2 + hh, 65)[64:65, :], 1.0, zhi[0:1, :],
                                ALU.mult, ALU.subtract, sem=(S_V, 1))
                            vz = S_V.v
                            pvev[hh] = (sev, vz)
                            p.wait("tensor", S_V, vz)
                            mm(pall[0:16, 5 * 512:6 * 512], ehc[0:1, h * 16:(h + 1) * 16],
                               zhi[0:1, :], h == 0, False)
                            S_T.inc()
                            mm(pall[0:16, 5 * 512:6 * 512], ehc[0:1, h * 16:(h + 1) * 16],
                               zlo[0:1, :], False, h == NH - 1, sem=(S_T, 1))
                            zmm_guard[0] = S_T.v
                    zdone = S_T.v
                    barrier()
                    emit_normalize(zdone, lnbuf)
                    emit_bproj(Wo, 0,
                               lambda kk, hf: lnbuf[:, kk, :],
                               NK, NK, None, None, accfn=emit_resadd(s, BO))
                    barrier()

            slotgA = [0, 0, 0]
            slotgB = [0, 0, 0]
            expdone = [0, 0, 0, 0]
            pvfreeA = (0, 0)
            pvfreeB = (0, 0)
            for s in ([] if _KPIPE == 9 else range(NSTR)):
                qsl = slice(s * SW, (s + 1) * SW)
                p.wait("tensor", S_S, quad_free[0][0])
                p.wait("tensor", S_V, quad_free[0][1])
                zpend = None
                for pr in range(NH // 2):
                    ha, hb = 2 * pr, 2 * pr + 1
                    expA = {}
                    expB = {}
                    pvA = {}
                    pvB = {}
                    qkA = {}
                    qkB = {}
                    for kc in range(NC + 1):
                        if _KPIPE == 0 and kc > 0:
                            barrier()
                        if kc < NC:
                            sbA, sbB = kc % 2, 2 + kc % 2
                            ks = slice(kc * 128, (kc + 1) * 128)
                            p.wait("tensor", S_S, expdone[sbA])
                            S_T.inc()
                            mm(bank(sbA), kbuf[0:64, pr, ks],
                               qbuf[0:64, pr, qsl], True, True,
                               sem=(S_T, 1), tp=(0, 0))
                            qkA[kc] = S_T.v
                            p.wait("tensor", S_S, expdone[sbB])
                            S_T.inc()
                            mm(bank(sbB), kbuf[64:128, pr, ks],
                               qbuf[64:128, pr, qsl], True, True,
                               sem=(S_T, 1), tp=(64, 0))
                            qkB[kc] = S_T.v
                        if kc == 2 and zpend is not None:
                            emit_zmms(*zpend)
                            zpend = None
                        if kc >= 1:
                            k2 = kc - 1
                            p.wait("tensor", S_S, expA[k2])
                            if k2 == 0:
                                p.wait("tensor", S_S, pvfreeA[0])
                                p.wait("tensor", S_V, pvfreeA[1])
                            S_T.inc()
                            mm(bank(4, 65), vbuf[:, k2, ha * 65:(ha + 1) * 65],
                               lnbuf[:, k2 % 3, :], k2 == 0, k2 == NC - 1,
                               sem=(S_T, 1))
                            pvA[k2] = S_T.v
                            slotgA[k2 % 3] = S_T.v
                            p.wait("tensor", S_S, expB[k2])
                            if k2 == 0:
                                p.wait("tensor", S_S, pvfreeB[0])
                                p.wait("tensor", S_V, pvfreeB[1])
                            S_T.inc()
                            mm(bank(6, 65), vbuf[:, k2, hb * 65:(hb + 1) * 65],
                               lnbuf[:, 3 + k2 % 3, :], k2 == 0, k2 == NC - 1,
                               sem=(S_T, 1))
                            pvB[k2] = S_T.v
                            slotgB[k2 % 3] = S_T.v
                        if kc < NC:
                            p.wait("scalar", S_T, max(qkA[kc], slotgA[kc % 3]))
                            S_S.inc()
                            act(lnbuf[:, kc % 3, :], bank(kc % 2), AF.Exp,
                                bias=negcB[:, ha:ha + 1], scale=SCALE,
                                sem=(S_S, 1))
                            expA[kc] = S_S.v
                            expdone[kc % 2] = S_S.v
                            p.wait("scalar", S_T, max(qkB[kc], slotgB[kc % 3]))
                            S_S.inc()
                            act(lnbuf[:, 3 + kc % 3, :], bank(2 + kc % 2),
                                AF.Exp, bias=negcB[:, hb:hb + 1], scale=SCALE,
                                sem=(S_S, 1))
                            expB[kc] = S_S.v
                            expdone[2 + kc % 2] = S_S.v
                    p.wait("scalar", S_T, pvA[NC - 1])
                    S_S.inc()
                    act(t1buf[0:64, pr, :], bank(4, 65)[0:64, :], AF.Copy,
                        sem=(S_S, 1))
                    sevA = S_S.v
                    p.wait("scalar", S_T, pvB[NC - 1])
                    S_S.inc()
                    act(t1buf[64:128, pr, :], bank(6, 65)[0:64, :], AF.Copy,
                        sem=(S_S, 1))
                    sevB = S_S.v
                    p.wait("vector", S_T, max(pvA[NC - 1], zguard[0]))
                    S_V.inc()
                    vop("tensor_copy", zhi[0:1, :], bank(4, 65)[64:65, :],
                        sem=(S_V, 1))
                    p.wait("vector", S_V, S_V.v)
                    S_V.inc()
                    vop("scalar_tensor_tensor", zlo[0:1, :],
                        bank(4, 65)[64:65, :], 1.0, zhi[0:1, :],
                        ALU.mult, ALU.subtract, sem=(S_V, 1))
                    vzA = S_V.v
                    p.wait("vector", S_T, pvB[NC - 1])
                    S_V.inc()
                    vop("tensor_copy", zhiB[0:1, :], bank(6, 65)[64:65, :],
                        sem=(S_V, 1))
                    p.wait("vector", S_V, S_V.v)
                    S_V.inc()
                    vop("scalar_tensor_tensor", zloB[0:1, :],
                        bank(6, 65)[64:65, :], 1.0, zhiB[0:1, :],
                        ALU.mult, ALU.subtract, sem=(S_V, 1))
                    vzB = S_V.v
                    pvfreeA = (sevA, vzA)
                    pvfreeB = (sevB, vzB)
                    zpend = (ha, hb, vzB)
                emit_zmms(*zpend)
                zdone = zguard[0]
                barrier()
                if _DBG4 and e == 0 and s == 0:
                    tap2d("d_tmpu", t1buf)
                emit_normalize(zdone, lnbuf)
                if _DBG4 and e == 0 and s == 0:
                    tap_raw("d_zg", rz[:, :])
                    tap2d("d_out", lnbuf)
                emit_bproj(Wo, 0,
                           lambda kk, hf: lnbuf[:, kk, :],
                           NK, NK, None, None, accfn=emit_resadd(s, BO),
                           rhs_ready=(0, 0))
                barrier()

        if n_sub >= 4:
            am1, am2 = emit_ffn_arena(max(zguard[0], zmm_guard[0])
                                      if n_sub >= 3 else ffn_mm_done[0])
            emit_ffn(am1, am2)

        barrier()
        p.wait("gpsimd", S_V, S_V.v)
        p.wait("gpsimd", S_S, S_S.v)
        for k in range(NK):
            for q4 in range(4):
                S_ST.inc(16)
                dma("gpsimd", outT[e, k * 128:(k + 1) * 128, q4 * 512:(q4 + 1) * 512],
                    x[:, k, q4 * 512:(q4 + 1) * 512], (S_ST, 16))
        x_store_done = S_ST.v

    if _DBG3:
        p.wait("gpsimd", S_T, S_T.v)
        p.wait("gpsimd", S_V, S_V.v)
        p.wait("gpsimd", S_S, S_S.v)
        p.wait("gpsimd", S_ST, S_ST.v)
        p.wait("gpsimd", S_LDF, S_LDF.v)
        for k in range(NK):
            S_LDW.inc(16)
            dma("gpsimd", dbg["d_x"][k * 128:(k + 1) * 128, :],
                x[:, k, 1536:2048], (S_LDW, 16))
            S_LDW.inc(16)
            dma("gpsimd", dbg["d_ln"][k * 128:(k + 1) * 128, :],
                lnbuf[:, k, :], (S_LDW, 16))
            S_LDW.inc(16)
            dma("gpsimd", dbg["d_tmpu"][k * 128:(k + 1) * 128, :],
                t1buf[:, k, :], (S_LDW, 16))
            S_LDW.inc(16)
            dma("gpsimd", dbg["d_q"][k * 128:(k + 1) * 128, :],
                qbuf[:, k, 0:SW], (S_LDW, 16))
            S_LDW.inc(16)
            dma("gpsimd", dbg["d_k"][k * 128:(k + 1) * 128, :],
                kbuf[:, k, 0:SW], (S_LDW, 16))
        S_LDW.inc(16)
        dma("gpsimd", dbg["d_negc"], negcA[:, :], (S_LDW, 16))
        S_LDW.inc(16)
        dma("gpsimd", dbg["d_v"], vbuf[:, 0:4, :], (S_LDW, 16))
        S_LDW.inc(16)
        dma("gpsimd", dbg["d_out"][0:16, :], rz[:, :], (S_LDW, 16))
        p.wait("gpsimd", S_LDW, S_LDW.v)
    p.wait("sync", S_ST, S_ST.v)
    import os
    if os.environ.get("SEMTOTALS"):
        print("SEMTOTALS", dict(ldx=S_LDX.v, ldw=S_LDW.v, ldf=S_LDF.v,
                                t=S_T.v, v=S_V.v, s=S_S.v, st=S_ST.v))
    with nc.Block() as block:
        @block.tensor
        def _(eng):
            for f in p.st["tensor"]:
                f(eng)

        @block.vector
        def _(eng):
            for f in p.st["vector"]:
                f(eng)

        @block.scalar
        def _(eng):
            for f in p.st["scalar"]:
                f(eng)

        @block.sync
        def _(eng):
            for f in p.st["sync"]:
                f(eng)

        @block.gpsimd
        def _(eng):
            for f in p.st["gpsimd"]:
                f(eng)

    stack.close()
    return nc



def _consts():
    onesc = np.ones((1, 128), np.float32)
    monesc = -np.ones((1, 128), np.float32)
    onescol = np.ones((128, 1), np.float32)
    id16 = np.eye(16, dtype=np.float32)
    indn = np.zeros((128, 8 * 16), np.float32)
    for k in range(8):
        for h2 in range(2):
            indn[h2 * 64:(h2 + 1) * 64, k * 16 + 2 * k + h2] = 1.0
    inds48 = np.zeros((48, NH * 64), np.float32)
    for h in range(NH):
        inds48[h, h * 64:(h + 1) * 64] = 1.0
        inds48[32 + h, h * 64:(h + 1) * 64] = 1.0
    sel33 = np.zeros((33, 128), np.float32)
    sel33[0, :] = 1.0
    sel33[32, :] = 1.0
    amask = np.zeros((128, 512), np.float32)
    for c in range(4):
        for b in range(2):
            amask[b * 64:(b + 1) * 64, c * 128 + b * 64: c * 128 + (b + 1) * 64] = 1.0
    ehc = np.zeros((1, 256), np.float32)
    for h in range(NH):
        ehc[0, h * 16 + h] = 1.0
    return dict(onesc=onesc, monesc=monesc, onescol=onescol, id16=id16,
                indn=indn, inds48=inds48, sel33=sel33, amask=amask, ehc=ehc)


def _pack_bias(b, ncol):
    return np.ascontiguousarray(np.asarray(b, np.float32).reshape(ncol, 128).T)


def kernel(L0_attn_a_x, L0_attn_a_W_qkv, L0_attn_a_b_qkv, L0_attn_a_W_out,
           L0_attn_a_b_out, L0_attn_b_Wq, L0_attn_b_bq, L0_attn_b_Wk,
           L0_attn_b_bk, L0_attn_b_Wv, L0_attn_b_bv, L0_attn_b_Wl,
           L0_attn_b_bl, L0_attn_b_Wo, L0_attn_b_bo, L0_ffn_U1, L0_ffn_V1,
           L0_ffn_b1, L0_ffn_U2, L0_ffn_V2, L0_ffn_b2):
    f32 = lambda a: np.ascontiguousarray(np.asarray(a), dtype=np.float32)
    xfull = f32(L0_attn_a_x)
    bqkv = f32(L0_attn_a_b_qkv)
    biasf = np.zeros((128, 112), np.float32)
    biasf[:, BQKV_Q:BQKV_Q + 8] = _pack_bias(bqkv[0:1024], 8)
    biasf[:, BQKV_K:BQKV_K + 8] = _pack_bias(bqkv[1024:2048], 8)
    biasf[:, BQ:BQ + 8] = _pack_bias(L0_attn_b_bq, 8)
    biasf[:, BK:BK + 8] = _pack_bias(L0_attn_b_bk, 8)
    biasf[:, BL:BL + 8] = _pack_bias(L0_attn_b_bl, 8)
    biasf[:, BO:BO + 8] = _pack_bias(L0_attn_b_bo, 8)
    biasf[:, BOUT:BOUT + 8] = _pack_bias(L0_attn_a_b_out, 8)
    biasf[:, B2:B2 + 8] = _pack_bias(L0_ffn_b2, 8)
    biasf[:, B1:B1 + 32] = _pack_bias(L0_ffn_b1, 32)
    biasf[:, BV:BV + 8] = _pack_bias(L0_attn_b_bv, 8)

    common = dict(
        WqkvT=np.ascontiguousarray(f32(L0_attn_a_W_qkv).T),
        WoutT=np.ascontiguousarray(f32(L0_attn_a_W_out).T),
        Wq=f32(L0_attn_b_Wq), Wk=f32(L0_attn_b_Wk), Wv=f32(L0_attn_b_Wv),
        Wl=f32(L0_attn_b_Wl), Wo=f32(L0_attn_b_Wo),
        U1=f32(L0_ffn_U1), V1=f32(L0_ffn_V1), U2=f32(L0_ffn_U2),
        V2=f32(L0_ffn_V2), biasf=biasf,
        bvrow=np.ascontiguousarray(bqkv[2048:3072].reshape(1, DM)),
        blrow=np.ascontiguousarray(f32(L0_attn_b_bl).reshape(1, DM)),
        **_consts(),
    )
    nc = build(_N_SUB)
    in_maps = []
    for c in range(NCORE):
        xs = xfull[c * NE:(c + 1) * NE]
        in_maps.append(dict(common, xT=np.ascontiguousarray(xs.transpose(0, 2, 1))))
    res = run_bass_kernel_spmd(nc, in_maps, core_ids=list(range(NCORE)))
    outs = [np.asarray(res.results[c]["outT"]).transpose(0, 2, 1)
            for c in range(NCORE)]
    return np.ascontiguousarray(np.concatenate(outs, axis=0), dtype=np.float32)


# version 07
# speedup vs baseline: 1.4515x; 1.1171x over previous
import numpy as np
import concourse.bass as bass
import concourse.mybir as mybir
from concourse.bass_utils import run_bass_kernel_spmd

F32 = mybir.dt.float32
BF16 = mybir.dt.bfloat16
AF = mybir.ActivationFunctionType
ALU = mybir.AluOpType
AX = mybir.AxisListType

NCORE = 8
NE = 4
TOK = 2048
DM = 1024
NH = 16
NK = 8
NSTR = 4
SW = 512
NC = 16
SCALE = 0.125
DFF = 4096
RANK = 256
_N_SUB = 4
_DBG = False
_BARR = 0
import os as _os
_BARR = int(_os.environ.get('KBARR', '0'))
_DBG2 = int(_os.environ.get('KDBG2', '0'))
_KPIPE = int(_os.environ.get('KPIPE', '2'))
_DBG3 = int(_os.environ.get('KDBG3', '0'))
_DBG4 = int(_os.environ.get('KDBG4', '0'))

BQKV_Q, BQKV_K, BQ, BK, BL, BO, BOUT, B2, B1, BV = 0, 8, 16, 24, 32, 40, 48, 56, 64, 96


class Sem:
    def __init__(self, h):
        self.h = h
        self.v = 0

    def inc(self, n=1):
        self.v += n
        return self.v


class RotSem:
    def __init__(self, sems):
        self.sems = sems
        self.cur = sems[0]

    @property
    def h(self):
        return self.cur.h

    @property
    def v(self):
        return self.cur.v

    def inc(self, n=1):
        return self.cur.inc(n)

    def rotate(self, i):
        self.cur = self.sems[i]


class Prog:
    ENGINES = ("tensor", "vector", "scalar", "sync", "gpsimd")

    def __init__(self):
        self.st = {e: [] for e in self.ENGINES}

    def emit(self, eng, fn):
        self.st[eng].append(fn)

    def wait(self, eng, sem, val):
        if val > 0:
            self.emit(eng, lambda e, s=sem.h, v=val: e.wait_ge(s, v))


def build(n_sub=4):
    nc = bass.Bass("TRN2", target_bir_lowering=False)
    p = Prog()

    def din(name, shape):
        return nc.dram_tensor(name, list(shape), F32, kind="ExternalInput").ap()

    xT = din("xT", [NE, DM, TOK])
    WqkvT = din("WqkvT", [DM, 3 * DM])
    WoutT = din("WoutT", [DM, DM])
    Wq, Wk, Wv, Wl, Wo = (din(n, [DM, DM]) for n in ("Wq", "Wk", "Wv", "Wl", "Wo"))
    U1 = din("U1", [DM, RANK])
    V1 = din("V1", [RANK, DFF])
    U2 = din("U2", [DFF, RANK])
    V2 = din("V2", [RANK, DM])
    biasf_in = din("biasf", [128, 112])
    bvrow = din("bvrow", [1, DM])
    blrow = din("blrow", [1, DM])
    onesc_in = din("onesc", [1, 128])
    monesc_in = din("monesc", [1, 128])
    onescol_in = din("onescol", [128, 1])
    id16_in = din("id16", [16, 16])
    indn_in = din("indn", [128, 8 * 16])
    inds48_in = din("inds48", [48, NH * 64])
    sel33_in = din("sel33", [33, 128])
    amask_in = din("amask", [128, 512])
    ehc_in = din("ehc", [1, 256])
    outT = nc.dram_tensor("outT", [NE, DM, TOK], F32, kind="ExternalOutput").ap()
    dbg = {}
    if _DBG or _DBG2 or _DBG3 or _DBG4:
        for nm, shape in (("d_ln", [NK * 128, SW]), ("d_q", [NK * 128, SW]),
                          ("d_k", [NK * 128, SW]), ("d_v", [128, 4 * NH * 65]),
                          ("d_negc", [128, 16]), ("d_tmpu", [NK * 128, SW]),
                          ("d_zg", [16, SW]), ("d_out", [NK * 128, SW]),
                          ("d_x", [NK * 128, SW])):
            dbg[nm] = nc.dram_tensor(nm, shape, F32, kind="ExternalOutput").ap()

    from contextlib import ExitStack
    stack = ExitStack()
    E = stack.enter_context
    ledger = [0]

    def sbt(name, shape, dt):
        n = 1
        for s in shape[1:]:
            n *= s
        ledger[0] += n * (2 if dt == BF16 else 4)
        return E(nc.sbuf_tensor(name, list(shape), dt))

    x = sbt("x", [128, NK, TOK], F32)
    qbuf = sbt("qbuf", [128, NK, TOK], BF16)
    kbuf = sbt("kbuf", [128, NK, TOK], BF16)
    vbuf = sbt("vbuf", [128, NC, NH * 65], BF16)
    lnbuf = sbt("lnbuf", [128, NK, SW], BF16)
    t1buf = sbt("t1buf", [128, NK, SW], BF16)
    wbuf = sbt("wbuf", [128, 3072], BF16)
    wffn = sbt("wffn", [128, 1024], BF16)
    bcline = sbt("bcline", [128, DM], BF16)
    rowh = sbt("rowh", [1, DM], BF16)
    rz = sbt("rz", [16, SW], F32)
    rzhl = sbt("rzhl", [48, SW], BF16)
    mrows = sbt("mrows", [128, SW], F32)
    hlr = sbt("hlr", [33, SW], BF16)
    hlm = sbt("hlm", [33, SW], BF16)
    tmpf = sbt("tmpf", [128, SW], F32)
    negcA = sbt("negcA", [128, 16], F32)
    negcB = sbt("negcB", [128, 16], F32)
    smax = sbt("smax", [16, 16], F32)
    lnsc = sbt("lnsc", [16, 1], F32)
    epsc = sbt("epsc", [128, 1], F32)
    biasf = sbt("biasf_s", [128, 112], F32)
    onesc = sbt("onesc_s", [1, 128], BF16)
    monesc = sbt("monesc_s", [1, 128], BF16)
    onescol = sbt("onescol_s", [128, 1], BF16)
    id16 = sbt("id16_s", [16, 16], BF16)
    indn = sbt("indn_s", [128, 8, 16], BF16)
    inds48 = sbt("inds48_s", [48, NH, 64], BF16)
    sel33 = sbt("sel33_s", [33, 128], BF16)
    amask = sbt("amask_s", [128, 512], BF16)
    ehc = sbt("ehc_s", [1, 256], BF16)
    zhi = sbt("zhi", [1, SW], BF16)
    zlo = sbt("zlo", [1, SW], BF16)
    zhiB = wffn[0:1, 0:512]
    zloB = wffn[0:1, 512:1024]
    assert ledger[0] <= 211500, f"SBUF over budget: {ledger[0]}"

    pall = E(nc.psum_tensor("pall", [128, 4096], F32))

    def bank(i, rows=128, w=512):
        return pall[0:rows, i * 512:i * 512 + w]

    S_LDX = Sem(E(nc.semaphore("ldx")))
    S_LDW = RotSem([Sem(E(nc.semaphore(f"ldw{i}"))) for i in range(NE)])
    S_LDF = Sem(E(nc.semaphore("ldf")))
    S_T = Sem(E(nc.semaphore("t")))
    S_V = Sem(E(nc.semaphore("v")))
    S_S = Sem(E(nc.semaphore("s")))
    S_ST = Sem(E(nc.semaphore("st")))
    S_G = Sem(E(nc.semaphore("g")))

    def mm(out, lhsT, rhs, start, stop, sem=None, tp=None):
        hv = None if sem is None else (sem[0].h, sem[1])
        def f(e, o=out, l=lhsT, r=rhs, a=start, b=stop, sm=hv, t=tp):
            i = e.matmul(o, l, r, start=a, stop=b, tile_position=t)
            if sm is not None:
                i.then_inc(sm[0], sm[1])
        p.emit("tensor", f)

    def act(out, in_, func, bias=0.0, scale=1.0, sem=None):
        if func == AF.Copy and not isinstance(bias, float):
            func = AF.Identity
        hv = None if sem is None else (sem[0].h, sem[1])
        def f(e, o=out, i=in_, fn=func, b=bias, sc=scale, sm=hv):
            ins = e.activation(o, i, fn, bias=b, scale=sc)
            if sm is not None:
                ins.then_inc(sm[0], sm[1])
        p.emit("scalar", f)

    def vop(name, *args, sem=None, **kw):
        hv = None if sem is None else (sem[0].h, sem[1])
        def f(e, nm=name, a=args, sm=hv, k=kw):
            ins = getattr(e, nm)(*a, **k)
            if sm is not None:
                ins.then_inc(sm[0], sm[1])
        p.emit("vector", f)

    def dma(eng, out, in_, sem):
        hv = (sem[0].h, sem[1])
        def f(e, o=out, i=in_, sm=hv):
            e.dma_start(out=o, in_=i).then_inc(sm[0], sm[1])
        p.emit(eng, f)

    def gvop(name, *args, sem=None, **kw):
        hv = None if sem is None else (sem[0].h, sem[1])
        def f(e, nm=name, a=args, sm=hv, k=kw):
            ins = getattr(e, nm)(*a, **k)
            if sm is not None:
                ins.then_inc(sm[0], sm[1])
        p.emit("gpsimd", f)

    def barrier():
        p.wait("tensor", S_V, S_V.v)
        p.wait("tensor", S_S, S_S.v)
        p.wait("vector", S_T, S_T.v)
        p.wait("vector", S_S, S_S.v)
        p.wait("scalar", S_T, S_T.v)
        p.wait("scalar", S_V, S_V.v)

    for dst, src in ((biasf[:, :], biasf_in), (onesc[:, :], onesc_in),
                     (monesc[:, :], monesc_in), (onescol[:, :], onescol_in),
                     (id16[:, :], id16_in), (indn[:, :, :], indn_in),
                     (inds48[:, :, :], inds48_in), (sel33[:, :], sel33_in),
                     (amask[:, :], amask_in), (ehc[:, :], ehc_in)):
        S_LDW.inc(16)
        dma("gpsimd", dst, src, (S_LDW, 16))
    p.wait("vector", S_LDW, S_LDW.v)
    vop("memset", lnsc[:, :], float(np.log(SCALE)) + 0.03)
    vop("memset", epsc[:, :], 1e-5)
    vop("memset", hlr[:, :], 0.0)
    vop("memset", hlm[:, :], 0.0)
    vop("memset", rzhl[:, :], 0.0)
    S_V.inc()
    vop("memset", vbuf[:, :, :], 1.0, sem=(S_V, 1))
    p.wait("vector", S_V, S_V.v)

    wslot = [0, 0, 0]
    quad_free = [(0, 0), (0, 0)]
    zmm_guard = [0]
    ffn_mm_done = [0]
    ra_guard = [0, 0]

    def emit_bcline(row_dram):
        S_LDW.inc(16)
        dma("gpsimd", rowh[:, :], row_dram, (S_LDW, 16))
        barrier()
        p.wait("tensor", S_LDW, S_LDW.v)
        mm(bank(0), onesc[0:1, :], rowh[0:1, 0:512], True, True)
        S_T.inc()
        mm(bank(1), onesc[0:1, :], rowh[0:1, 512:1024], True, True, sem=(S_T, 1))
        p.wait("vector", S_T, S_T.v)
        vop("tensor_copy", bcline[:, 0:512], bank(0))
        S_V.inc()
        vop("tensor_copy", bcline[:, 512:1024], bank(1), sem=(S_V, 1))
        quad_free[0] = (S_S.v, S_V.v)

    def emit_ln(s):
        sl = slice(s * SW, (s + 1) * SW)
        barrier()
        cm = {}
        sqm = {}
        for k in range(NK):
            S_V.inc()
            vop("tensor_copy", lnbuf[:, k, :], x[:, k, sl], sem=(S_V, 1))
            cm[k] = S_V.v
            p.wait("scalar", S_V, cm[k])
            S_S.inc()
            act(t1buf[:, k, :], lnbuf[:, k, :], AF.Square, sem=(S_S, 1))
            sqm[k] = S_S.v
        for k in range(NK):
            p.wait("tensor", S_V, cm[k])
            sem = (S_T, 1) if k == NK - 1 else None
            if sem:
                S_T.inc()
            mm(bank(0, 1), onescol[:, 0:1], lnbuf[:, k, :], k == 0, k == NK - 1, sem=sem)
            if k == NK - 1:
                tm = S_T.v
            p.wait("tensor", S_S, sqm[k])
            sem = (S_T, 1) if k == NK - 1 else None
            if sem:
                S_T.inc()
            mm(bank(1, 1), onescol[:, 0:1], t1buf[:, k, :], k == 0, k == NK - 1, sem=sem)
        tsq = S_T.v
        p.wait("vector", S_T, tm)
        S_V.inc()
        vop("tensor_copy", mrows[0:1, :], bank(0, 1), sem=(S_V, 1))
        p.wait("vector", S_T, tsq)
        S_V.inc()
        vop("tensor_copy", rz[0:1, :], bank(1, 1), sem=(S_V, 1))
        p.wait("vector", S_V, S_V.v)
        S_V.inc()
        vop("tensor_scalar", mrows[0:1, :], mrows[0:1, :], 1.0 / DM, None, ALU.mult,
            sem=(S_V, 1))
        p.wait("vector", S_V, S_V.v)
        S_V.inc()
        vop("tensor_mul", tmpf[0:1, :], mrows[0:1, :], mrows[0:1, :], sem=(S_V, 1))
        p.wait("vector", S_V, S_V.v)
        S_V.inc()
        vop("scalar_tensor_tensor", rz[0:1, :], rz[0:1, :], 1.0 / DM,
            tmpf[0:1, :], ALU.mult, ALU.subtract, sem=(S_V, 1))
        p.wait("scalar", S_V, S_V.v)
        S_S.inc()
        act(rz[0:1, :], rz[0:1, :], AF.Ln, bias=epsc[0:1, 0:1], sem=(S_S, 1))
        p.wait("scalar", S_S, S_S.v)
        S_S.inc()
        act(rz[0:1, :], rz[0:1, :], AF.Exp, scale=-0.5, sem=(S_S, 1))
        p.wait("vector", S_S, S_S.v)
        S_V.inc()
        vop("tensor_mul", mrows[0:1, :], mrows[0:1, :], rz[0:1, :], sem=(S_V, 1))
        p.wait("vector", S_V, S_V.v)
        S_V.inc()
        vop("tensor_copy", hlr[0:1, :], rz[0:1, :], sem=(S_V, 1))
        p.wait("vector", S_V, S_V.v)
        S_V.inc()
        vop("tensor_sub", hlr[32:33, :], rz[0:1, :], hlr[0:1, :], sem=(S_V, 1))
        p.wait("vector", S_V, S_V.v)
        S_V.inc()
        vop("tensor_copy", hlm[0:1, :], mrows[0:1, :], sem=(S_V, 1))
        p.wait("vector", S_V, S_V.v)
        S_V.inc()
        vop("tensor_sub", hlm[32:33, :], mrows[0:1, :], hlm[0:1, :], sem=(S_V, 1))
        p.wait("tensor", S_V, S_V.v)
        mm(bank(2), sel33[:, :], hlr[:, :], True, True)
        S_T.inc()
        mm(bank(3), sel33[:, :], hlm[:, :], True, True, sem=(S_T, 1))
        p.wait("vector", S_T, S_T.v)
        for k in range(NK):
            vop("scalar_tensor_tensor", tmpf[:, :], x[:, k, sl], 1.0,
                bank(2), ALU.mult, ALU.mult)
            sem = (S_V, 1) if k == NK - 1 else None
            if sem:
                S_V.inc()
            vop("tensor_sub", lnbuf[:, k, :], tmpf[:, :], bank(3), sem=sem)
        p.wait("tensor", S_V, S_V.v)
        p.wait("scalar", S_V, S_V.v)
        quad_free[0] = (S_S.v, S_V.v)
        return (S_S.v, S_V.v)

    bankg = [(0, 0)] * 8

    def emit_bproj(wsrc, msl, rhs_fn, nmt, nkk, dst_fn, bias_col, accfn=None,
                   rhs_ready=None):
        tmark = [0] * nmt
        w = nmt * 128
        for kk in range(nkk):
            slot = kk % 3
            wt = wbuf[:, slot * 1024: slot * 1024 + w]
            p.wait("gpsimd", S_T, wslot[slot])
            S_LDW.inc(16)
            dma("gpsimd", wt,
                wsrc[kk * 128:(kk + 1) * 128, msl:msl + w], (S_LDW, 16))
            p.wait("tensor", S_LDW, S_LDW.v)
            if kk == 0:
                rr = rhs_ready if rhs_ready is not None else quad_free[0]
                p.wait("tensor", S_S, max(rr[0], quad_free[1][0]))
                p.wait("tensor", S_V, max(rr[1], quad_free[1][1]))
            rhs = rhs_fn(kk, 0)
            for m in range(nmt):
                if kk == 0:
                    p.wait("tensor", S_S, bankg[m][0])
                    p.wait("tensor", S_V, bankg[m][1])
                last = kk == nkk - 1
                sem = (S_T, 1) if (last or m == nmt - 1) else None
                if sem:
                    S_T.inc()
                mm(bank(m), wt[:, m * 128:(m + 1) * 128], rhs, kk == 0, last,
                   sem=sem)
                if last:
                    tmark[m] = S_T.v
            wslot[slot] = S_T.v
        for m in range(nmt):
            if accfn is not None:
                accfn(m, 0, bank(m), tmark[m])
                bankg[m] = (S_S.v, 0)
            else:
                b = (biasf[:, bias_col + m: bias_col + m + 1]
                     if bias_col is not None else None)
                if m % 2 == 0:
                    p.wait("scalar", S_T, tmark[m])
                    S_S.inc()
                    act(dst_fn(m, 0), bank(m), AF.Copy,
                        bias=(0.0 if b is None else b), sem=(S_S, 1))
                    bankg[m] = (S_S.v, 0)
                else:
                    p.wait("vector", S_T, tmark[m])
                    S_V.inc()
                    if b is None:
                        vop("tensor_copy", dst_fn(m, 0), bank(m), sem=(S_V, 1))
                    else:
                        vop("tensor_single_scalar", dst_fn(m, 0), bank(m), b,
                            ALU.add, sem=(S_V, 1))
                    bankg[m] = (0, S_V.v)
        quad_free[0] = (S_S.v, S_V.v)

    def emit_resadd(s, bias_col):
        def accfn(m, half, ps, tm):
            slot = m % 2
            sl = slice(s * SW, (s + 1) * SW)
            p.wait("scalar", S_T, tm)
            p.wait("scalar", S_V, ra_guard[slot])
            S_S.inc()
            dst = tmpf[:, :] if slot == 0 else mrows[:, :]
            act(dst, ps, AF.Copy,
                bias=biasf[:, bias_col + m: bias_col + m + 1], sem=(S_S, 1))
            p.wait("vector", S_S, S_S.v)
            S_V.inc()
            vop("tensor_add", x[:, m, sl], x[:, m, sl], dst, sem=(S_V, 1))
            ra_guard[slot] = S_V.v
        return accfn

    def emit_vproj(wsrc2d, wcol0, actT, tcglob_fn, bc_add, rhs_ready=None):
        for nh2 in range(2):
            vt = [0] * 4
            for kk in range(NK):
                slot = kk % 3
                wt = wbuf[:, slot * 1024: slot * 1024 + 512]
                c0 = wcol0 + nh2 * 512
                p.wait("gpsimd", S_T, wslot[slot])
                S_LDW.inc(16)
                dma("gpsimd", wt[:, 0:512],
                    wsrc2d[kk * 128:(kk + 1) * 128, c0:c0 + 512], (S_LDW, 16))
                p.wait("tensor", S_LDW, S_LDW.v)
                if kk == 0 and nh2 == 0:
                    rr = rhs_ready if rhs_ready is not None else quad_free[0]
                    p.wait("tensor", S_S, max(rr[0], quad_free[1][0]))
                    p.wait("tensor", S_V, max(rr[1], quad_free[1][1]))
                for tc in range(4):
                    if kk == 0:
                        p.wait("tensor", S_S, bankg[tc][0])
                        p.wait("tensor", S_V, bankg[tc][1])
                    last = kk == NK - 1
                    sem = (S_T, 1) if (last or tc == 3) else None
                    if sem:
                        S_T.inc()
                    mm(bank(tc), actT[:, kk, tc * 128:(tc + 1) * 128],
                       wt[:, 0:512], kk == 0, last, sem=sem)
                    if last:
                        vt[tc] = S_T.v
                wslot[slot] = S_T.v
            for tc in range(4):
                p.wait("vector", S_T, vt[tc])
                tcg = tcglob_fn(tc)
                S_V.inc()
                vop("tensor_add",
                    vbuf[:, tcg, :].rearrange("p (h n) -> p h n", h=NH)
                    [:, nh2 * 8:nh2 * 8 + 8, 0:64],
                    bank(tc).rearrange("p (h n) -> p h n", h=8),
                    bc_add[:, nh2 * 512:nh2 * 512 + 512].rearrange("p (h n) -> p h n", h=8),
                    sem=(S_V, 1))
                bankg[tc] = (0, S_V.v)
        quad_free[0] = (quad_free[0][0], S_V.v)

    def emit_negc(sq_src_fn, n_batches, negc_dst, colq, colk):
        p.wait("scalar", S_V, S_V.v)
        S_S.inc()
        act(smax[:, 10:11], smax[:, colq:colq + 1], AF.Ln, sem=(S_S, 1))
        p.wait("scalar", S_S, S_S.v)
        S_S.inc()
        act(smax[:, 11:12], smax[:, colk:colk + 1], AF.Ln, sem=(S_S, 1))
        p.wait("vector", S_S, S_S.v)
        S_V.inc()
        vop("tensor_add", smax[:, 12:13], smax[:, 10:11], smax[:, 11:12], sem=(S_V, 1))
        p.wait("scalar", S_V, S_V.v)
        S_S.inc()
        act(smax[:, 13:14], smax[:, 12:13], AF.Exp, bias=lnsc[:, 0:1], scale=0.5,
            sem=(S_S, 1))
        p.wait("vector", S_S, S_S.v)
        S_V.inc()
        vop("tensor_copy", rzhl[0:16, 0:1], smax[:, 13:14], sem=(S_V, 1))
        p.wait("tensor", S_V, S_V.v)
        p.wait("tensor", S_S, max(bankg[4][0], bankg[5][0]))
        p.wait("tensor", S_V, max(bankg[4][1], bankg[5][1]))
        S_T.inc()
        mm(pall[0:1, 2048:2064], rzhl[0:16, 0:1], id16[:, :], True, True, sem=(S_T, 1))
        p.wait("scalar", S_T, S_T.v)
        S_S.inc()
        act(rowh[0:1, 0:16], pall[0:1, 2048:2064], AF.Copy, sem=(S_S, 1))
        p.wait("tensor", S_S, S_S.v)
        S_T.inc()
        mm(pall[:, 2560:2576], monesc[0:1, :], rowh[0:1, 0:16], True, True, sem=(S_T, 1))
        p.wait("vector", S_T, S_T.v)
        S_V.inc()
        vop("tensor_copy", negc_dst[:, :], pall[:, 2560:2576], sem=(S_V, 1))
        p.wait("vector", S_V, S_V.v)
        S_V.inc()
        vop("tensor_scalar", negc_dst[:, :], negc_dst[:, :], -60.0, 0.0,
            ALU.max, ALU.min, sem=(S_V, 1))
        p.wait("scalar", S_V, S_V.v)

    def emit_norms(srcT, sl, dstcol):
        p.wait("vector", S_T, S_T.v)
        p.wait("vector", S_S, S_S.v)
        p.wait("vector", S_V, quad_free[0][1])
        S_V.inc()
        vop("tensor_mul", t1buf[:, :, :], srcT, srcT, sem=(S_V, 1))
        p.wait("tensor", S_V, S_V.v)
        p.wait("tensor", S_S, bankg[5][0])
        p.wait("tensor", S_V, bankg[5][1])
        for k in range(NK):
            sem = (S_T, 1) if k == NK - 1 else None
            if sem:
                S_T.inc()
            mm(bank(5, 16), indn[:, k, :], t1buf[:, k, :], k == 0, k == NK - 1, sem=sem)
        p.wait("vector", S_T, S_T.v)
        S_V.inc()
        vop("reduce_max", smax[:, dstcol:dstcol + 1], bank(5, 16), AX.X, sem=(S_V, 1))
        bankg[5] = (0, S_V.v)
        quad_free[1] = (quad_free[1][0], S_V.v)

    def emit_normalize(zdone, outstrip3):
        p.wait("vector", S_T, zdone)
        S_V.inc()
        vop("reciprocal", rz[:, :], pall[0:16, 5 * 512:6 * 512], sem=(S_V, 1))
        zmm_guard[0] = 0
        quad_free[1] = (quad_free[1][0], S_V.v)
        p.wait("vector", S_V, S_V.v)
        S_V.inc()
        vop("tensor_copy", rzhl[0:16, :], rz[:, :], sem=(S_V, 1))
        p.wait("vector", S_V, S_V.v)
        S_V.inc()
        vop("tensor_sub", rzhl[32:48, :], rz[:, :], rzhl[0:16, :], sem=(S_V, 1))
        p.wait("tensor", S_V, S_V.v)
        mulm = [0, 0]
        for h in range(NH):
            kkt, rb = h // 2, (h % 2) * 64
            bk = 4 + 2 * (h % 2)
            pdst = pall[rb:rb + 64, bk * 512:bk * 512 + 512]
            p.wait("tensor", S_V, mulm[h % 2])
            S_T.inc()
            mm(pdst, inds48[:, h, :], rzhl[:, :], True, True, sem=(S_T, 1),
               tp=(0, rb))
            p.wait("vector", S_T, S_T.v)
            S_V.inc()
            vop("tensor_mul", outstrip3[rb:rb + 64, kkt, :],
                t1buf[rb:rb + 64, kkt, :], pdst, sem=(S_V, 1))
            mulm[h % 2] = S_V.v
        p.wait("tensor", S_V, S_V.v)

    def tap2d(nm, buf3, cast=True):
        if not (_DBG or _DBG2 or _DBG3 or _DBG4):
            return
        barrier()
        p.wait("gpsimd", S_T, S_T.v)
        p.wait("gpsimd", S_S, S_S.v)
        p.wait("gpsimd", S_V, S_V.v)
        for k in range(NK):
            S_LDW.inc(16)
            dma("gpsimd", dbg[nm][k * 128:(k + 1) * 128, :], buf3[:, k, 0:SW], (S_LDW, 16))
        p.wait("gpsimd", S_LDW, S_LDW.v)
        for eng in ("tensor", "vector", "scalar"):
            p.wait(eng, S_LDW, S_LDW.v)

    def tap_raw(nm, ap_):
        if not (_DBG or _DBG2 or _DBG3 or _DBG4):
            return
        barrier()
        p.wait("gpsimd", S_T, S_T.v)
        p.wait("gpsimd", S_S, S_S.v)
        p.wait("gpsimd", S_V, S_V.v)
        S_LDW.inc(16)
        dma("gpsimd", dbg[nm], ap_, (S_LDW, 16))
        p.wait("gpsimd", S_LDW, S_LDW.v)
        for eng in ("tensor", "vector", "scalar"):
            p.wait(eng, S_LDW, S_LDW.v)

    x_store_done = 0

    for e in range(NE):
        S_LDW.rotate(e)
        p.wait("sync", S_ST, x_store_done)
        for k in range(NK):
            S_LDX.inc(16)
            dma("sync", x[:, k, :], xT[e, k * 128:(k + 1) * 128, :], (S_LDX, 16))
        p.wait("vector", S_LDX, S_LDX.v)
        p.wait("tensor", S_LDX, S_LDX.v)
        p.wait("scalar", S_LDX, S_LDX.v)

        if n_sub >= 1:
            emit_bcline(bvrow)
            azg = [S_T.v, S_T.v]
            expdA = [0, 0]
            pvfA = [(0, 0), (0, 0)]
            slotg3 = [0, 0, 0]
            for s in range(NSTR):
                lnr = emit_ln(s)
                if e == 0 and s == 0:
                    tap2d("d_ln", lnbuf)
                emit_bproj(WqkvT, 0,
                           lambda kk, hf: lnbuf[:, kk, :],
                           NK, NK,
                           lambda m, hf: qbuf[:, m, 0:SW],
                           BQKV_Q, rhs_ready=lnr)
                emit_bproj(WqkvT, DM,
                           lambda kk, hf: lnbuf[:, kk, :],
                           NK, NK,
                           lambda m, hf: kbuf[:, m, 0:SW],
                           BQKV_K, rhs_ready=lnr)
                if e == 0 and s == 0:
                    tap2d("d_q", qbuf)
                    tap2d("d_k", kbuf)
                emit_vproj(WqkvT, 2 * DM, lnbuf, lambda tc: tc, bcline,
                           rhs_ready=lnr)
                if e == 0 and s == 0:
                    tap_raw("d_v", vbuf[:, 0:4, :])
                emit_norms(qbuf[:, :, 0:SW], s, 0)
                emit_norms(kbuf[:, :, 0:SW], s, 1)
                emit_negc(None, 1, negcA, 0, 1)
                if e == 0 and s == 0:
                    tap_raw("d_negc", negcA[:, :])
                barrier()
                p.wait("tensor", S_S, quad_free[0][0])
                p.wait("tensor", S_V, quad_free[0][1])

                def emit_azmm(h2_, vzh_):
                    zh_, zl_ = (zhi, zlo) if h2_ % 2 == 0 else (zhiB, zloB)
                    p.wait("tensor", S_V, vzh_)
                    mm(pall[0:16, 5 * 512:6 * 512],
                       ehc[0:1, h2_ * 16:(h2_ + 1) * 16], zh_[0:1, :],
                       h2_ == 0, False)
                    S_T.inc()
                    mm(pall[0:16, 5 * 512:6 * 512],
                       ehc[0:1, h2_ * 16:(h2_ + 1) * 16], zl_[0:1, :],
                       False, h2_ == NH - 1, sem=(S_T, 1))
                    azg[h2_ % 2] = S_T.v

                qkm = {}
                expm_ = {}
                maskm = {}
                pvm = {}
                zq = []
                for h in range(NH + 1):
                    if h < NH:
                        sb = h % 2
                        pair, rb = h // 2, (h % 2) * 64
                        p.wait("tensor", S_S, expdA[sb])
                        for c in range(4):
                            sem = (S_T, 1) if c == 3 else None
                            if sem:
                                S_T.inc()
                            mm(bank(sb)[:, c * 128:(c + 1) * 128],
                               kbuf[rb:rb + 64, pair, c * 128:(c + 1) * 128],
                               qbuf[rb:rb + 64, pair, c * 128:(c + 1) * 128],
                               True, True, sem=sem, tp=(rb, 0))
                        qkm[h] = S_T.v
                    while zq and zq[0][0] <= h - 2:
                        emit_azmm(*zq.pop(0))
                    if h >= 1:
                        h2 = h - 1
                        pb = 2 + h2 % 2
                        rb2 = (h2 % 2) * 64
                        p.wait("tensor", S_G, maskm[h2])
                        p.wait("tensor", S_S, pvfA[h2 % 2][0])
                        p.wait("tensor", S_V, pvfA[h2 % 2][1])
                        for c in range(4):
                            sem = (S_T, 1) if c == 3 else None
                            if sem:
                                S_T.inc()
                            mm(bank(pb, 65)[:, c * 128:(c + 1) * 128],
                               vbuf[:, c, h2 * 65:(h2 + 1) * 65],
                               lnbuf[:, h2 % 3, c * 128:(c + 1) * 128],
                               True, True, sem=sem)
                        pvm[h2] = S_T.v
                        slotg3[h2 % 3] = S_T.v
                        p.wait("scalar", S_T, pvm[h2])
                        S_S.inc()
                        act(t1buf[rb2:rb2 + 64, h2 // 2, :],
                            bank(pb, 65)[0:64, :], AF.Copy, sem=(S_S, 1))
                        sevh = S_S.v
                        p.wait("vector", S_T, max(pvm[h2], azg[h2 % 2]))
                        zh, zl = (zhi, zlo) if h2 % 2 == 0 else (zhiB, zloB)
                        S_V.inc()
                        vop("tensor_copy", zh[0:1, :], bank(pb, 65)[64:65, :],
                            sem=(S_V, 1))
                        p.wait("vector", S_V, S_V.v)
                        S_V.inc()
                        vop("scalar_tensor_tensor", zl[0:1, :],
                            bank(pb, 65)[64:65, :], 1.0, zh[0:1, :],
                            ALU.mult, ALU.subtract, sem=(S_V, 1))
                        vzh = S_V.v
                        pvfA[h2 % 2] = (sevh, vzh)
                        zq.append((h2, vzh))
                    if h < NH:
                        p.wait("scalar", S_T, max(qkm[h], slotg3[h % 3]))
                        S_S.inc()
                        act(lnbuf[:, h % 3, :], bank(h % 2), AF.Exp,
                            bias=negcA[:, h:h + 1], scale=SCALE, sem=(S_S, 1))
                        expm_[h] = S_S.v
                        expdA[h % 2] = S_S.v
                        p.wait("gpsimd", S_S, expm_[h])
                        S_G.inc()
                        gvop("tensor_mul", lnbuf[:, h % 3, :], lnbuf[:, h % 3, :],
                             amask[:, :], sem=(S_G, 1))
                        maskm[h] = S_G.v
                while zq:
                    emit_azmm(*zq.pop(0))
                zdone = azg[(NH - 1) % 2]
                barrier()
                if e == 0 and s == 0:
                    tap2d("d_tmpu", t1buf)
                emit_normalize(zdone, lnbuf)
                if e == 0 and s == 0:
                    tap2d("d_out", lnbuf)
                emit_bproj(WoutT, 0,
                           lambda kk, hf: lnbuf[:, kk, :],
                           NK, NK, None, None, accfn=emit_resadd(s, BOUT),
                           rhs_ready=(0, 0))
                if _DBG2 and e == 0 and s == NSTR - 1:
                    tap_raw("d_negc", negcA[:, :])
                    tap2d("d_tmpu", t1buf)
                    tap2d("d_out", lnbuf)
                    tap_raw("d_v", vbuf[:, 12:16, :])
                if e == 0 and s == 0:
                    barrier()
                    p.wait("sync", S_V, S_V.v)
                    if _DBG:
                        for k in range(NK):
                            S_LDW.inc(16)
                            dma("sync", dbg["d_x"][k * 128:(k + 1) * 128, :],
                                x[:, k, 0:SW], (S_LDW, 16))
                        p.wait("sync", S_LDW, S_LDW.v)
            barrier()

        def ffn_tile(i):
            off = 512 + 512 * (i // 8)
            return kbuf[:, i % 8, off:off + 512]

        def emit_ffn_arena(guard):
            p.wait("gpsimd", S_T, guard)
            for a in range(4):
                for c in range(2):
                    S_LDF.inc(16)
                    dma("gpsimd", ffn_tile(16 + a)[:, c * 256:(c + 1) * 256],
                        U1[(2 * a + c) * 128:(2 * a + c + 1) * 128, :],
                        (S_LDF, 16))
            for i in range(16):
                kk2, c = i // 8, i % 8
                S_LDF.inc(16)
                dma("gpsimd", ffn_tile(i),
                    V1[kk2 * 128:(kk2 + 1) * 128, c * 512:(c + 1) * 512],
                    (S_LDF, 16))
            for kk in range(2):
                for h in range(2):
                    S_LDF.inc(16)
                    dma("gpsimd", ffn_tile(20 + kk * 2 + h),
                        V2[kk * 128:(kk + 1) * 128, h * 512:(h + 1) * 512],
                        (S_LDF, 16))
            mark_w = S_LDF.v
            for kk in range(32):
                S_LDF.inc(16)
                dma("gpsimd",
                    vbuf[:, 8 + kk // 4, (kk % 4) * 256:(kk % 4) * 256 + 256],
                    U2[kk * 128:(kk + 1) * 128, :], (S_LDF, 16))
            return mark_w, S_LDF.v

        def emit_ffn(arena_w, arena_u2):
            hsl = qbuf.ap().rearrange("p k (a t) -> p (k a) t", a=4)
            for s in range(NSTR):
                lnr = emit_ln(s)
                if s == 0:
                    p.wait("tensor", S_LDF, arena_w)
                tm = [0, 0]
                for kk in range(NK):
                    lt = ffn_tile(16 + kk // 2)
                    co = (kk % 2) * 256
                    for m in range(2):
                        if kk == 0:
                            p.wait("tensor", S_S,
                                   max(lnr[0], bankg[m][0], quad_free[1][0]))
                            p.wait("tensor", S_V,
                                   max(lnr[1], bankg[m][1], quad_free[1][1]))
                        last = kk == NK - 1
                        sem = (S_T, 1) if (last or m == 1) else None
                        if sem:
                            S_T.inc()
                        mm(bank(m), lt[:, co + m * 128:co + (m + 1) * 128],
                           lnbuf[:, kk, :], kk == 0, last, sem=sem)
                        if last:
                            tm[m] = S_T.v
                p.wait("scalar", S_T, tm[0])
                S_S.inc()
                act(kbuf[:, 0, 0:SW], bank(0), AF.Copy, sem=(S_S, 1))
                bankg[0] = (S_S.v, 0)
                p.wait("vector", S_T, tm[1])
                S_V.inc()
                vop("tensor_copy", kbuf[:, 1, 0:SW], bank(1), sem=(S_V, 1))
                bankg[1] = (0, S_V.v)
                pready = (S_S.v, S_V.v)
                gm = {}
                u2m = 0
                for m in range(33):
                    if m < 32:
                        bk = 4 + (m % 2)
                        if m < 2:
                            p.wait("tensor", S_S, max(pready[0], bankg[bk][0]))
                            p.wait("tensor", S_V, max(pready[1], bankg[bk][1]))
                        else:
                            p.wait("tensor", S_S, gm[m - 2])
                        for kk2 in range(2):
                            ti = kk2 * 8 + m // 4
                            sem = (S_T, 1) if kk2 == 1 else None
                            if sem:
                                S_T.inc()
                            mm(bank(bk),
                               ffn_tile(ti)[:, (m % 4) * 128:(m % 4 + 1) * 128],
                               kbuf[:, kk2, 0:SW], kk2 == 0, kk2 == 1, sem=sem)
                        vmm = S_T.v
                    if m >= 1:
                        m2 = m - 1
                        p.wait("tensor", S_S, gm[m2])
                        if m2 == 0:
                            p.wait("tensor", S_LDF, arena_u2)
                            p.wait("tensor", S_S, max(bankg[6][0], bankg[7][0]))
                            p.wait("tensor", S_V, max(bankg[6][1], bankg[7][1]))
                        u2 = vbuf[:, 8 + m2 // 4,
                                  (m2 % 4) * 256:(m2 % 4) * 256 + 256]
                        mm(bank(6), u2[:, 0:128], hsl[:, m2, :], m2 == 0, m2 == 31)
                        S_T.inc()
                        mm(bank(7), u2[:, 128:256], hsl[:, m2, :], m2 == 0,
                           m2 == 31, sem=(S_T, 1))
                        u2m = S_T.v
                    if m < 32:
                        p.wait("scalar", S_T, vmm)
                        S_S.inc()
                        act(hsl[:, m, :], bank(4 + (m % 2)), AF.Gelu_apprx_tanh,
                            bias=biasf[:, B1 + m:B1 + m + 1], sem=(S_S, 1))
                        gm[m] = S_S.v
                p.wait("scalar", S_T, u2m)
                S_S.inc()
                act(kbuf[:, 2, 0:SW], bank(6), AF.Copy, sem=(S_S, 1))
                bankg[6] = (S_S.v, 0)
                p.wait("vector", S_T, u2m)
                S_V.inc()
                vop("tensor_copy", kbuf[:, 3, 0:SW], bank(7), sem=(S_V, 1))
                bankg[7] = (0, S_V.v)
                qready = (S_S.v, S_V.v)
                bankg[4] = (gm[30], 0)
                bankg[5] = (gm[31], 0)
                tmark = [0] * 8
                ra = emit_resadd(s, B2)
                for kk in range(2):
                    for m in range(8):
                        if kk == 0:
                            p.wait("tensor", S_S, max(qready[0], bankg[m][0]))
                            p.wait("tensor", S_V, max(qready[1], bankg[m][1]))
                        last = kk == 1
                        sem = (S_T, 1) if (last or m == 7) else None
                        if sem:
                            S_T.inc()
                        mm(bank(m),
                           ffn_tile(20 + kk * 2 + m // 4)
                           [:, (m % 4) * 128:(m % 4 + 1) * 128],
                           kbuf[:, 2 + kk, 0:SW], kk == 0, last, sem=sem)
                        if last:
                            tmark[m] = S_T.v
                for m in range(8):
                    ra(m, 0, bank(m), tmark[m])
                    bankg[m] = (S_S.v, 0)
                quad_free[0] = (S_S.v, S_V.v)
            barrier()
            ffn_mm_done[0] = S_T.v

        if n_sub >= 2:
            am1, am2 = emit_ffn_arena(ffn_mm_done[0])
            emit_ffn(am1, am2)

        if n_sub >= 3:
            emit_bcline(blrow)
            p.wait("vector", S_V, S_V.v)
            S_V.inc()
            vop("memset", smax[:, 0:10], 0.0, sem=(S_V, 1))
            p.wait("vector", S_V, S_V.v)
            p.wait("vector", S_T, ffn_mm_done[0])
            S_V.inc()
            vop("memset",
                vbuf[:, 8:16, :].rearrange("p c (h n) -> p (c h) n", h=NH)
                [:, :, 64:65], 1.0, sem=(S_V, 1))
            p.wait("vector", S_V, S_V.v)
            for s in range(NSTR):
                lnr = emit_ln(s)
                sl = slice(s * SW, (s + 1) * SW)
                emit_bproj(Wq, 0,
                           lambda kk, hf: lnbuf[:, kk, :],
                           NK, NK,
                           lambda m, hf, s_=s: qbuf[:, m, s_ * SW:(s_ + 1) * SW],
                           BQ, rhs_ready=lnr)
                emit_norms(qbuf[:, :, sl], s, 2)
                p.wait("vector", S_V, S_V.v)
                S_V.inc()
                vop("tensor_max", smax[:, 0:1], smax[:, 0:1], smax[:, 2:3], sem=(S_V, 1))
                p.wait("vector", S_V, S_V.v)
                emit_bproj(Wk, 0,
                           lambda kk, hf: lnbuf[:, kk, :],
                           NK, NK,
                           lambda m, hf: t1buf[:, m, :],
                           BK, rhs_ready=lnr)
                emit_bproj(Wl, 0,
                           lambda kk, hf: t1buf[:, kk, :],
                           NK, NK,
                           lambda m, hf, s_=s: kbuf[:, m, s_ * SW:(s_ + 1) * SW],
                           BL)
                emit_norms(kbuf[:, :, sl], s, 3)
                p.wait("vector", S_V, S_V.v)
                S_V.inc()
                vop("tensor_max", smax[:, 1:2], smax[:, 1:2], smax[:, 3:4], sem=(S_V, 1))
                p.wait("vector", S_V, S_V.v)
                emit_bproj(Wv, 0,
                           lambda kk, hf: lnbuf[:, kk, :],
                           NK, NK,
                           lambda m, hf: t1buf[:, m, :],
                           BV, rhs_ready=lnr)
                emit_vproj(Wl, 0, t1buf, lambda tc, s_=s: s_ * 4 + tc, bcline)
                if _DBG4 and e == 0 and s == 0:
                    tap2d("d_q", qbuf)
                    tap2d("d_k", kbuf)
                    tap_raw("d_v", vbuf[:, 0:4, :])
            emit_negc(None, 1, negcB, 0, 1)
            barrier()
            zguard = [S_T.v]

            def emit_zmms(ha, hb, vzB_):
                p.wait("tensor", S_V, vzB_)
                mm(pall[0:16, 5 * 512:6 * 512], ehc[0:1, ha * 16:(ha + 1) * 16],
                   zhi[0:1, :], ha == 0, False)
                mm(pall[0:16, 5 * 512:6 * 512], ehc[0:1, ha * 16:(ha + 1) * 16],
                   zlo[0:1, :], False, False)
                mm(pall[0:16, 5 * 512:6 * 512], ehc[0:1, hb * 16:(hb + 1) * 16],
                   zhiB[0:1, :], False, False)
                S_T.inc()
                mm(pall[0:16, 5 * 512:6 * 512], ehc[0:1, hb * 16:(hb + 1) * 16],
                   zloB[0:1, :], False, hb == NH - 1, sem=(S_T, 1))
                zguard[0] = S_T.v

            if _KPIPE == 9:
                for s in range(NSTR):
                    qsl = slice(s * SW, (s + 1) * SW)
                    expm = [0, 0]
                    pvev = [(0, 0), (0, 0)]
                    for pair in range(NH // 2):
                        for hh in range(2):
                            h = 2 * pair + hh
                            rb = (h % 2) * 64
                            p.wait("tensor", S_S, pvev[hh][0])
                            p.wait("tensor", S_V, pvev[hh][1])
                            for kc in range(NC):
                                p.wait("tensor", S_S, expm[hh])
                                S_T.inc()
                                mm(bank(hh),
                                   kbuf[rb:rb + 64, pair, kc * 128:(kc + 1) * 128],
                                   qbuf[rb:rb + 64, pair, qsl], True, True,
                                   sem=(S_T, 1), tp=(rb, 0))
                                p.wait("scalar", S_T, S_T.v)
                                slot = hh * 3 + kc % 3
                                S_S.inc()
                                act(lnbuf[:, slot, :], bank(hh), AF.Exp,
                                    bias=negcB[:, h:h + 1], scale=SCALE, sem=(S_S, 1))
                                expm[hh] = S_S.v
                                p.wait("tensor", S_S, S_S.v)
                                S_T.inc()
                                mm(bank(2 + hh, 65), vbuf[:, kc, h * 65:(h + 1) * 65],
                                   lnbuf[:, slot, :], kc == 0, kc == NC - 1,
                                   sem=(S_T, 1))
                            tpv = S_T.v
                            kkt = h // 2
                            p.wait("scalar", S_T, tpv)
                            S_S.inc()
                            act(t1buf[rb:rb + 64, kkt, :], bank(2 + hh, 65)[0:64, :],
                                AF.Copy, sem=(S_S, 1))
                            sev = S_S.v
                            p.wait("vector", S_T, tpv)
                            p.wait("vector", S_T, zmm_guard[0])
                            S_V.inc()
                            vop("tensor_copy", zhi[0:1, :], bank(2 + hh, 65)[64:65, :],
                                sem=(S_V, 1))
                            p.wait("vector", S_V, S_V.v)
                            S_V.inc()
                            vop("scalar_tensor_tensor", zlo[0:1, :],
                                bank(2 + hh, 65)[64:65, :], 1.0, zhi[0:1, :],
                                ALU.mult, ALU.subtract, sem=(S_V, 1))
                            vz = S_V.v
                            pvev[hh] = (sev, vz)
                            p.wait("tensor", S_V, vz)
                            mm(pall[0:16, 5 * 512:6 * 512], ehc[0:1, h * 16:(h + 1) * 16],
                               zhi[0:1, :], h == 0, False)
                            S_T.inc()
                            mm(pall[0:16, 5 * 512:6 * 512], ehc[0:1, h * 16:(h + 1) * 16],
                               zlo[0:1, :], False, h == NH - 1, sem=(S_T, 1))
                            zmm_guard[0] = S_T.v
                    zdone = S_T.v
                    barrier()
                    emit_normalize(zdone, lnbuf)
                    emit_bproj(Wo, 0,
                               lambda kk, hf: lnbuf[:, kk, :],
                               NK, NK, None, None, accfn=emit_resadd(s, BO))
                    barrier()

            slotgA = [0, 0, 0]
            slotgB = [0, 0, 0]
            expdone = [0, 0, 0, 0]
            pvfreeA = (0, 0)
            pvfreeB = (0, 0)
            for s in ([] if _KPIPE == 9 else range(NSTR)):
                qsl = slice(s * SW, (s + 1) * SW)
                p.wait("tensor", S_S, quad_free[0][0])
                p.wait("tensor", S_V, quad_free[0][1])
                zpend = None
                for pr in range(NH // 2):
                    ha, hb = 2 * pr, 2 * pr + 1
                    expA = {}
                    expB = {}
                    pvA = {}
                    pvB = {}
                    qkA = {}
                    qkB = {}
                    for kc in range(NC + 1):
                        if _KPIPE == 0 and kc > 0:
                            barrier()
                        if kc < NC:
                            sbA, sbB = kc % 2, 2 + kc % 2
                            ks = slice(kc * 128, (kc + 1) * 128)
                            p.wait("tensor", S_S, expdone[sbA])
                            S_T.inc()
                            mm(bank(sbA), kbuf[0:64, pr, ks],
                               qbuf[0:64, pr, qsl], True, True,
                               sem=(S_T, 1), tp=(0, 0))
                            qkA[kc] = S_T.v
                            p.wait("tensor", S_S, expdone[sbB])
                            S_T.inc()
                            mm(bank(sbB), kbuf[64:128, pr, ks],
                               qbuf[64:128, pr, qsl], True, True,
                               sem=(S_T, 1), tp=(64, 0))
                            qkB[kc] = S_T.v
                        if kc == 2 and zpend is not None:
                            emit_zmms(*zpend)
                            zpend = None
                        if kc >= 1:
                            k2 = kc - 1
                            p.wait("tensor", S_S, expA[k2])
                            if k2 == 0:
                                p.wait("tensor", S_S, pvfreeA[0])
                                p.wait("tensor", S_V, pvfreeA[1])
                            S_T.inc()
                            mm(bank(4, 65), vbuf[:, k2, ha * 65:(ha + 1) * 65],
                               lnbuf[:, k2 % 3, :], k2 == 0, k2 == NC - 1,
                               sem=(S_T, 1))
                            pvA[k2] = S_T.v
                            slotgA[k2 % 3] = S_T.v
                            p.wait("tensor", S_S, expB[k2])
                            if k2 == 0:
                                p.wait("tensor", S_S, pvfreeB[0])
                                p.wait("tensor", S_V, pvfreeB[1])
                            S_T.inc()
                            mm(bank(6, 65), vbuf[:, k2, hb * 65:(hb + 1) * 65],
                               lnbuf[:, 3 + k2 % 3, :], k2 == 0, k2 == NC - 1,
                               sem=(S_T, 1))
                            pvB[k2] = S_T.v
                            slotgB[k2 % 3] = S_T.v
                        if kc < NC:
                            p.wait("scalar", S_T, max(qkA[kc], slotgA[kc % 3]))
                            S_S.inc()
                            act(lnbuf[:, kc % 3, :], bank(kc % 2), AF.Exp,
                                bias=negcB[:, ha:ha + 1], scale=SCALE,
                                sem=(S_S, 1))
                            expA[kc] = S_S.v
                            expdone[kc % 2] = S_S.v
                            p.wait("scalar", S_T, max(qkB[kc], slotgB[kc % 3]))
                            S_S.inc()
                            act(lnbuf[:, 3 + kc % 3, :], bank(2 + kc % 2),
                                AF.Exp, bias=negcB[:, hb:hb + 1], scale=SCALE,
                                sem=(S_S, 1))
                            expB[kc] = S_S.v
                            expdone[2 + kc % 2] = S_S.v
                    p.wait("scalar", S_T, pvA[NC - 1])
                    S_S.inc()
                    act(t1buf[0:64, pr, :], bank(4, 65)[0:64, :], AF.Copy,
                        sem=(S_S, 1))
                    sevA = S_S.v
                    p.wait("scalar", S_T, pvB[NC - 1])
                    S_S.inc()
                    act(t1buf[64:128, pr, :], bank(6, 65)[0:64, :], AF.Copy,
                        sem=(S_S, 1))
                    sevB = S_S.v
                    p.wait("vector", S_T, max(pvA[NC - 1], zguard[0]))
                    S_V.inc()
                    vop("tensor_copy", zhi[0:1, :], bank(4, 65)[64:65, :],
                        sem=(S_V, 1))
                    p.wait("vector", S_V, S_V.v)
                    S_V.inc()
                    vop("scalar_tensor_tensor", zlo[0:1, :],
                        bank(4, 65)[64:65, :], 1.0, zhi[0:1, :],
                        ALU.mult, ALU.subtract, sem=(S_V, 1))
                    vzA = S_V.v
                    p.wait("vector", S_T, pvB[NC - 1])
                    S_V.inc()
                    vop("tensor_copy", zhiB[0:1, :], bank(6, 65)[64:65, :],
                        sem=(S_V, 1))
                    p.wait("vector", S_V, S_V.v)
                    S_V.inc()
                    vop("scalar_tensor_tensor", zloB[0:1, :],
                        bank(6, 65)[64:65, :], 1.0, zhiB[0:1, :],
                        ALU.mult, ALU.subtract, sem=(S_V, 1))
                    vzB = S_V.v
                    pvfreeA = (sevA, vzA)
                    pvfreeB = (sevB, vzB)
                    zpend = (ha, hb, vzB)
                emit_zmms(*zpend)
                zdone = zguard[0]
                barrier()
                if _DBG4 and e == 0 and s == 0:
                    tap2d("d_tmpu", t1buf)
                emit_normalize(zdone, lnbuf)
                if _DBG4 and e == 0 and s == 0:
                    tap_raw("d_zg", rz[:, :])
                    tap2d("d_out", lnbuf)
                emit_bproj(Wo, 0,
                           lambda kk, hf: lnbuf[:, kk, :],
                           NK, NK, None, None, accfn=emit_resadd(s, BO),
                           rhs_ready=(0, 0))
                barrier()

        if n_sub >= 4:
            am1, am2 = emit_ffn_arena(max(zguard[0], zmm_guard[0])
                                      if n_sub >= 3 else ffn_mm_done[0])
            emit_ffn(am1, am2)

        barrier()
        p.wait("gpsimd", S_V, S_V.v)
        p.wait("gpsimd", S_S, S_S.v)
        for k in range(NK):
            for q4 in range(4):
                S_ST.inc(16)
                dma("gpsimd", outT[e, k * 128:(k + 1) * 128, q4 * 512:(q4 + 1) * 512],
                    x[:, k, q4 * 512:(q4 + 1) * 512], (S_ST, 16))
        x_store_done = S_ST.v

    if _DBG3:
        p.wait("gpsimd", S_T, S_T.v)
        p.wait("gpsimd", S_V, S_V.v)
        p.wait("gpsimd", S_S, S_S.v)
        p.wait("gpsimd", S_ST, S_ST.v)
        p.wait("gpsimd", S_LDF, S_LDF.v)
        for k in range(NK):
            S_LDW.inc(16)
            dma("gpsimd", dbg["d_x"][k * 128:(k + 1) * 128, :],
                x[:, k, 1536:2048], (S_LDW, 16))
            S_LDW.inc(16)
            dma("gpsimd", dbg["d_ln"][k * 128:(k + 1) * 128, :],
                lnbuf[:, k, :], (S_LDW, 16))
            S_LDW.inc(16)
            dma("gpsimd", dbg["d_tmpu"][k * 128:(k + 1) * 128, :],
                t1buf[:, k, :], (S_LDW, 16))
            S_LDW.inc(16)
            dma("gpsimd", dbg["d_q"][k * 128:(k + 1) * 128, :],
                qbuf[:, k, 0:SW], (S_LDW, 16))
            S_LDW.inc(16)
            dma("gpsimd", dbg["d_k"][k * 128:(k + 1) * 128, :],
                kbuf[:, k, 0:SW], (S_LDW, 16))
        S_LDW.inc(16)
        dma("gpsimd", dbg["d_negc"], negcA[:, :], (S_LDW, 16))
        S_LDW.inc(16)
        dma("gpsimd", dbg["d_v"], vbuf[:, 0:4, :], (S_LDW, 16))
        S_LDW.inc(16)
        dma("gpsimd", dbg["d_out"][0:16, :], rz[:, :], (S_LDW, 16))
        p.wait("gpsimd", S_LDW, S_LDW.v)
    p.wait("sync", S_ST, S_ST.v)
    import os
    if os.environ.get("SEMTOTALS"):
        print("SEMTOTALS", dict(ldx=S_LDX.v, ldw=S_LDW.v, ldf=S_LDF.v,
                                t=S_T.v, v=S_V.v, s=S_S.v, st=S_ST.v))
    with nc.Block() as block:
        @block.tensor
        def _(eng):
            for f in p.st["tensor"]:
                f(eng)

        @block.vector
        def _(eng):
            for f in p.st["vector"]:
                f(eng)

        @block.scalar
        def _(eng):
            for f in p.st["scalar"]:
                f(eng)

        @block.sync
        def _(eng):
            for f in p.st["sync"]:
                f(eng)

        @block.gpsimd
        def _(eng):
            for f in p.st["gpsimd"]:
                f(eng)

    stack.close()
    return nc



def _consts():
    onesc = np.ones((1, 128), np.float32)
    monesc = -np.ones((1, 128), np.float32)
    onescol = np.ones((128, 1), np.float32)
    id16 = np.eye(16, dtype=np.float32)
    indn = np.zeros((128, 8 * 16), np.float32)
    for k in range(8):
        for h2 in range(2):
            indn[h2 * 64:(h2 + 1) * 64, k * 16 + 2 * k + h2] = 1.0
    inds48 = np.zeros((48, NH * 64), np.float32)
    for h in range(NH):
        inds48[h, h * 64:(h + 1) * 64] = 1.0
        inds48[32 + h, h * 64:(h + 1) * 64] = 1.0
    sel33 = np.zeros((33, 128), np.float32)
    sel33[0, :] = 1.0
    sel33[32, :] = 1.0
    amask = np.zeros((128, 512), np.float32)
    for c in range(4):
        for b in range(2):
            amask[b * 64:(b + 1) * 64, c * 128 + b * 64: c * 128 + (b + 1) * 64] = 1.0
    ehc = np.zeros((1, 256), np.float32)
    for h in range(NH):
        ehc[0, h * 16 + h] = 1.0
    return dict(onesc=onesc, monesc=monesc, onescol=onescol, id16=id16,
                indn=indn, inds48=inds48, sel33=sel33, amask=amask, ehc=ehc)


def _pack_bias(b, ncol):
    return np.ascontiguousarray(np.asarray(b, np.float32).reshape(ncol, 128).T)


def kernel(L0_attn_a_x, L0_attn_a_W_qkv, L0_attn_a_b_qkv, L0_attn_a_W_out,
           L0_attn_a_b_out, L0_attn_b_Wq, L0_attn_b_bq, L0_attn_b_Wk,
           L0_attn_b_bk, L0_attn_b_Wv, L0_attn_b_bv, L0_attn_b_Wl,
           L0_attn_b_bl, L0_attn_b_Wo, L0_attn_b_bo, L0_ffn_U1, L0_ffn_V1,
           L0_ffn_b1, L0_ffn_U2, L0_ffn_V2, L0_ffn_b2):
    f32 = lambda a: np.ascontiguousarray(np.asarray(a), dtype=np.float32)
    xfull = f32(L0_attn_a_x)
    bqkv = f32(L0_attn_a_b_qkv)
    biasf = np.zeros((128, 112), np.float32)
    biasf[:, BQKV_Q:BQKV_Q + 8] = _pack_bias(bqkv[0:1024], 8)
    biasf[:, BQKV_K:BQKV_K + 8] = _pack_bias(bqkv[1024:2048], 8)
    biasf[:, BQ:BQ + 8] = _pack_bias(L0_attn_b_bq, 8)
    biasf[:, BK:BK + 8] = _pack_bias(L0_attn_b_bk, 8)
    biasf[:, BL:BL + 8] = _pack_bias(L0_attn_b_bl, 8)
    biasf[:, BO:BO + 8] = _pack_bias(L0_attn_b_bo, 8)
    biasf[:, BOUT:BOUT + 8] = _pack_bias(L0_attn_a_b_out, 8)
    biasf[:, B2:B2 + 8] = _pack_bias(L0_ffn_b2, 8)
    biasf[:, B1:B1 + 32] = _pack_bias(L0_ffn_b1, 32)
    biasf[:, BV:BV + 8] = _pack_bias(L0_attn_b_bv, 8)

    common = dict(
        WqkvT=np.ascontiguousarray(f32(L0_attn_a_W_qkv).T),
        WoutT=np.ascontiguousarray(f32(L0_attn_a_W_out).T),
        Wq=f32(L0_attn_b_Wq), Wk=f32(L0_attn_b_Wk), Wv=f32(L0_attn_b_Wv),
        Wl=f32(L0_attn_b_Wl), Wo=f32(L0_attn_b_Wo),
        U1=f32(L0_ffn_U1), V1=f32(L0_ffn_V1), U2=f32(L0_ffn_U2),
        V2=f32(L0_ffn_V2), biasf=biasf,
        bvrow=np.ascontiguousarray(bqkv[2048:3072].reshape(1, DM)),
        blrow=np.ascontiguousarray(f32(L0_attn_b_bl).reshape(1, DM)),
        **_consts(),
    )
    nc = build(_N_SUB)
    in_maps = []
    for c in range(NCORE):
        xs = xfull[c * NE:(c + 1) * NE]
        in_maps.append(dict(common, xT=np.ascontiguousarray(xs.transpose(0, 2, 1))))
    res = run_bass_kernel_spmd(nc, in_maps, core_ids=list(range(NCORE)))
    outs = [np.asarray(res.results[c]["outT"]).transpose(0, 2, 1)
            for c in range(NCORE)]
    return np.ascontiguousarray(np.concatenate(outs, axis=0), dtype=np.float32)
